# version 06 (final)
# baseline (speedup 1.0000x reference)
import numpy as np
import ml_dtypes
from contextlib import ExitStack
import concourse.bass as bass
import concourse.mybir as mybir
from concourse.ap import AP
from concourse.bass_utils import run_bass_kernel_spmd

F32 = mybir.dt.float32
BF16 = mybir.dt.bfloat16
I32 = mybir.dt.int32
AF = mybir.ActivationFunctionType
ALU = mybir.AluOpType
AX = mybir.AxisListType

D = 1024
NDC = 8
TOK = 2048
S = 8192
NCORES = 8

ENGS = ("sync", "vector", "scalar", "tensor", "gpsimd")


class P:

    def __init__(self, name="p"):
        self.nc = bass.Bass("TRN2", target_bir_lowering=False)
        self.q = {e: [] for e in ENGS}
        self.es = ExitStack()
        self.sems = {}
        self.cnt = {}
        self.name = name

    def sem(self, name):
        h = self.es.enter_context(self.nc.semaphore(name))
        self.sems[name] = h
        self.cnt[name] = 0
        return name

    def sb(self, name, shape, dt=F32):
        return self.es.enter_context(self.nc.sbuf_tensor(name, shape, dt))

    def ps(self, name, shape, dt=F32):
        return self.es.enter_context(self.nc.psum_tensor(name, shape, dt))

    def din(self, name, shape, dt=F32):
        return self.nc.dram_tensor(name, shape, dt, kind="ExternalInput").ap()

    def dout(self, name, shape, dt=F32):
        return self.nc.dram_tensor(name, shape, dt, kind="ExternalOutput").ap()

    def add(self, eng, fn):
        self.q[eng].append(fn)

    def wait(self, eng, sem, val=None):
        v = self.cnt[sem] if val is None else val
        self.add(eng, lambda E, s=self.sems[sem], vv=v: E.wait_ge(s, vv))

    def op(self, eng, fn, sem=None, by=1):
        if sem is None:
            self.add(eng, fn)
            return None
        self.cnt[sem] += by
        self.add(eng, lambda E, f=fn, s=self.sems[sem], b=by: f(E).then_inc(s, b))
        return self.cnt[sem]

    def opss(self, eng, fn):
        key = "_ss_" + eng
        if key not in self.sems:
            self.sem(key)
        c = self.op(eng, fn, key, 1)
        self.wait(eng, key, c)
        return c

    def dma(self, eng, dst, src, sem, slow=False):
        if slow:
            return self.op(eng, lambda E, d=dst, s=src: E.dma_start(
                out=d, in_=s, allow_slow_non_contiguous=True), sem, 16)
        return self.op(eng, lambda E, d=dst, s=src: E.dma_start(out=d, in_=s), sem, 16)

    def build(self):
        with self.es, self.nc.Block() as blk:
            for eng in ENGS:
                if self.q[eng]:
                    lst = self.q[eng]
                    getattr(blk, eng)(lambda E, L=lst: [f(E) for f in L])
        return self.nc


def _ceil(a, b):
    return (a + b - 1) // b


def emit_identity(p, iden):
    iot_c = p.sb("id_iot_c" + iden.name, [128, 128])
    iot_r = p.sb("id_iot_r" + iden.name, [128, 1])
    idf = p.sb("id_f" + iden.name, [128, 128])
    p.op("gpsimd", lambda E: E.iota(iot_c[:], pattern=[[1, 128]], base=0,
                                    channel_multiplier=0,
                                    allow_small_or_imprecise_dtypes=True), "gp")
    p.op("gpsimd", lambda E: E.iota(iot_r[:], pattern=[[1, 1]], base=0,
                                    channel_multiplier=1,
                                    allow_small_or_imprecise_dtypes=True), "gp")
    p.wait("vector", "gp")
    p.op("vector", lambda E: E.tensor_scalar(idf[:], iot_c[:], iot_r[:], None,
                                             op0=ALU.is_equal))
    p.op("vector", lambda E: E.tensor_copy(iden[:], idf[:]), "vp")
    p._id_parts = (iot_c, iot_r, idf)


def build_prep():
    p = P("prep")
    w_fc = p.din("w_fc", [4096, 1024])
    w_proj = p.din("w_proj", [1024, 4096])
    u = p.din("u", [4096, 16])
    v = p.din("v", [16, 1024])
    fast_b = p.din("fast_b", [16, 16])
    gate = p.din("gate", [1, 1])
    W_KQV = p.din("W_KQV", [3072, 1024])
    W_O = p.din("W_O", [1024, 1024])
    rel = p.din("rel", [129, 128])

    tfcT = p.dout("tfcT", [1024, 4096], BF16)
    scfc = p.dout("scfc", [4096])
    tprT = p.dout("tprT", [4096, 1024], BF16)
    scpr = p.dout("scpr", [1024])
    wkqvT = p.dout("wkqvT", [1024, 3072], BF16)
    woT = p.dout("woT", [1024, 1024], BF16)
    uT = p.dout("uT", [16, 4096], BF16)
    vT = p.dout("vT", [1024, 16], BF16)
    fbTg = p.dout("fbTg", [16, 16], BF16)
    relT = p.dout("relT", [128, 129], BF16)

    for s in ("ld", "vp", "pp", "st", "gp"):
        p.sem(s)

    iden = p.sb("iden", [128, 128], BF16)
    emit_identity(p, iden)

    wrow = p.sb("wrow", [128, 4096])
    stat = p.sb("stat", [128, 1])
    stat2 = p.sb("stat2", [128, 1])
    rinv = p.sb("rinv", [128, 1])
    terni = p.sb("terni", [128, 4096], I32)
    tsgn = p.sb("tsgn", [128, 4096])
    tabs = p.sb("tabs", [128, 4096])
    ternf = p.sb("ternf", [128, 4096])
    ternb = p.sb("ternb", [128, 4096], BF16)
    evb = p.sb("evb", [128, 128], BF16)
    ps_t = p.ps("ps_t", [128, 128], BF16)

    def quant_and_transpose(src_ap, nrows, ncols, sc_out, t_out):
        for rc in range(nrows // 128):
            c = p.dma("sync", wrow[:, 0:ncols], src_ap[rc * 128:(rc + 1) * 128, :], "ld")
            p.wait("vector", "ld", c)
            p.wait("vector", "st")
            p.opss("vector", lambda E: E.reduce_sum(stat[:], wrow[:, 0:ncols], axis=AX.X,
                                                    apply_absolute_value=True))
            p.opss("vector", lambda E, n=ncols: E.tensor_scalar(
                stat2[:], stat[:], 1.0 / n, 1e-5, op0=ALU.mult, op1=ALU.max))
            p.opss("vector", lambda E: E.reciprocal(rinv[:], stat2[:]))
            p.op("vector", lambda E, n=ncols: E.tensor_scalar(
                ternf[:, 0:n], wrow[:, 0:n], rinv[:], None, op0=ALU.mult))
            p.op("vector", lambda E, n=ncols: E.tensor_scalar(
                tsgn[:, 0:n], ternf[:, 0:n], 0.0, 2.0, op0=ALU.is_ge, op1=ALU.mult))
            p.op("vector", lambda E, n=ncols: E.tensor_scalar(
                tsgn[:, 0:n], tsgn[:, 0:n], -1.0, None, op0=ALU.add))
            p.op("vector", lambda E, n=ncols: E.tensor_mul(
                tabs[:, 0:n], ternf[:, 0:n], tsgn[:, 0:n]))
            p.op("vector", lambda E, n=ncols: E.tensor_scalar(
                tabs[:, 0:n], tabs[:, 0:n], 0.5, None, op0=ALU.add))
            p.op("vector", lambda E, n=ncols: E.tensor_copy(terni[:, 0:n], tabs[:, 0:n]))
            p.op("vector", lambda E, n=ncols: E.tensor_copy(ternf[:, 0:n], terni[:, 0:n]))
            p.op("vector", lambda E, n=ncols: E.tensor_sub(tabs[:, 0:n], ternf[:, 0:n],
                                                           tabs[:, 0:n]))
            p.op("vector", lambda E, n=ncols: E.tensor_scalar(
                tabs[:, 0:n], tabs[:, 0:n], 0.5, None, op0=ALU.add))
            p.op("vector", lambda E, n=ncols: E.tensor_scalar(
                tabs[:, 0:n], tabs[:, 0:n], 0.5, None, op0=ALU.is_gt))
            p.op("vector", lambda E, n=ncols: E.tensor_sub(ternf[:, 0:n], ternf[:, 0:n],
                                                           tabs[:, 0:n]))
            p.op("vector", lambda E, n=ncols: E.tensor_scalar(
                ternf[:, 0:n], ternf[:, 0:n], 1.0, None, op0=ALU.min))
            p.op("vector", lambda E, n=ncols: E.tensor_tensor(
                ternb[:, 0:n], ternf[:, 0:n], tsgn[:, 0:n], ALU.mult), "vp")
            p.wait("sync", "vp")
            p.dma("sync", AP(sc_out.tensor, rc * 128, [[1, 128]]), stat2[:], "st")
            p.wait("tensor", "vp")
            for cc in range(ncols // 128):
                pc = p.op("tensor", lambda E, c0=cc * 128: E.transpose(
                    ps_t[:], ternb[:, c0:c0 + 128], iden[:]), "pp")
                p.wait("vector", "pp", pc)
                vc = p.op("vector", lambda E: E.tensor_copy(evb[:], ps_t[:]), "vp")
                p.wait("sync", "vp", vc)
                p.dma("sync", t_out[cc * 128:(cc + 1) * 128, rc * 128:(rc + 1) * 128],
                      evb[:], "st")
                p.wait("tensor", "vp", vc)
                p.wait("vector", "st")

    quant_and_transpose(w_fc, 4096, 1024, scfc, tfcT)
    quant_and_transpose(w_proj, 1024, 4096, scpr, tprT)

    def plain_transpose(src_ap, nrows, ncols, t_out):
        rch = _ceil(nrows, 128)
        cch = _ceil(ncols, 128)
        for rc in range(rch):
            r0, r1 = rc * 128, min((rc + 1) * 128, nrows)
            n = r1 - r0
            c = p.dma("sync", wrow[0:n, 0:ncols], src_ap[r0:r1, :], "ld")
            p.wait("vector", "ld", c)
            p.wait("vector", "st")
            p.op("vector", lambda E, n_=n, nc_=ncols: E.tensor_copy(
                ternb[0:n_, 0:nc_], wrow[0:n_, 0:nc_]), "vp")
            p.wait("tensor", "vp")
            for cc in range(cch):
                c0, c1 = cc * 128, min((cc + 1) * 128, ncols)
                m = c1 - c0
                pc = p.op("tensor", lambda E, a=c0, m_=m, n_=n: E.transpose(
                    ps_t[0:m_, 0:n_], ternb[0:n_, a:a + m_], iden[0:n_, 0:n_]), "pp")
                p.wait("vector", "pp", pc)
                vc = p.op("vector", lambda E, m_=m, n_=n: E.tensor_copy(
                    evb[0:m_, 0:n_], ps_t[0:m_, 0:n_]), "vp")
                p.wait("sync", "vp", vc)
                p.dma("sync", t_out[c0:c1, r0:r1], evb[0:m, 0:n], "st", slow=(n < 8))
                p.wait("tensor", "vp", vc)
                p.wait("vector", "st")

    plain_transpose(W_KQV, 3072, 1024, wkqvT)
    plain_transpose(W_O, 1024, 1024, woT)
    plain_transpose(u, 4096, 16, uT)
    plain_transpose(v, 16, 1024, vT)
    plain_transpose(rel, 129, 128, relT)

    fb_s = p.sb("fb_s", [16, 16])
    g_s = p.sb("g_s", [1, 1])
    onec = p.sb("onec", [1, 16])
    gb_ps = p.ps("gb_ps", [16, 1])
    fbt_s = p.sb("fbt_s", [16, 16])
    gbc = p.sb("gbc", [16, 1])
    fbg = p.sb("fbg", [16, 16], BF16)
    p.dma("sync", fb_s[:], fast_b, "ld")
    c = p.dma("sync", g_s[:], gate, "ld")
    p.wait("vector", "ld", c)
    p.wait("vector", "st")
    p.op("vector", lambda E: E.memset(onec[:], 1.0))
    p.op("vector", lambda E: E.tensor_copy(ternb[0:16, 0:16], fb_s[:]), "vp")
    p.wait("tensor", "vp")
    p.op("tensor", lambda E: E.transpose(ps_t[0:16, 0:16], ternb[0:16, 0:16],
                                         iden[0:16, 0:16]), "pp")
    pc = p.op("tensor", lambda E: E.matmul(gb_ps[:], onec[:], g_s[:],
                                           start=True, stop=True), "pp")
    p.wait("vector", "pp", pc)
    p.opss("vector", lambda E: E.tensor_copy(fbt_s[:], ps_t[0:16, 0:16]))
    p.opss("vector", lambda E: E.tensor_copy(gbc[:], gb_ps[:]))
    p.op("vector", lambda E: E.tensor_scalar(fbg[:], fbt_s[:], gbc[:], None,
                                             op0=ALU.mult), "vp")
    p.wait("sync", "vp")
    p.dma("sync", fbTg, fbg[:], "st")
    p.wait("sync", "st")
    return p.build()


def mk_ln_consts(p, pref="", pool=None):
    c = {}
    c["ones_col"] = p.sb(pref + "ones_col", [128, 1], BF16)
    c["ones_row1"] = p.sb(pref + "ones_row1", [1, 128])
    if pool is None:
        c["ps_s1"] = p.ps(pref + "ps_s1", [1, 512])
        c["ps_s2"] = p.ps(pref + "ps_s2", [1, 512])
        c["ps_mb"] = p.ps(pref + "ps_mb", [128, 512])
        c["ps_rb"] = p.ps(pref + "ps_rb", [128, 512])
    else:
        c["ps_s1"] = pool[0][0:1, :]
        c["ps_s2"] = pool[1][0:1, :]
        c["ps_mb"] = pool[2]
        c["ps_rb"] = pool[3]
    c["m_"] = p.sb(pref + "m_", [1, 512])
    c["var_"] = p.sb(pref + "var_", [1, 512])
    c["sq_"] = p.sb(pref + "sq_", [1, 512])
    c["rstd_"] = p.sb(pref + "rstd_", [1, 512])
    c["x2b"] = p.sb(pref + "x2b", [128, 512], BF16)
    c["xb"] = p.sb(pref + "xb", [128, 512], BF16)
    c["tf"] = p.sb(pref + "tf", [128, 512])
    c["c2_"] = p.sb(pref + "c2_", [1, 512])
    c["eps_"] = p.sb(pref + "eps_", [1, 1])
    p.op("vector", lambda E: E.memset(c["eps_"][:], 1e-5))
    p.op("vector", lambda E: E.memset(c["ones_col"][:], 1.0))
    p.op("vector", lambda E: E.memset(c["ones_row1"][:], 1.0), "vp")
    return c


def emit_ln(p, xs, xnb, T, c, affine=None):
    ones_col, onesb = c["ones_col"], c["ones_row1"]
    ps_s1, ps_s2, ps_mb, ps_rb = c["ps_s1"], c["ps_s2"], c["ps_mb"], c["ps_rb"]
    m_, var_, sq_, rstd_, c2_ = c["m_"], c["var_"], c["sq_"], c["rstd_"], c["c2_"]
    x2b, xb, tf = c["x2b"], c["xb"], c["tf"]
    for ci in range(T // 512):
        sl = slice(ci * 512, ci * 512 + 512)
        for d in range(NDC):
            pp_before = p.cnt["pp"]
            if d > 0:
                p.wait("scalar", "pp", pp_before)
                p.wait("vector", "pp", pp_before)
            sc = p.op("scalar", lambda E, dd=d, s_=sl: E.activation(
                x2b[:], xs[dd][:, s_], AF.Square), "sp")
            vc = p.op("vector", lambda E, dd=d, s_=sl: E.tensor_copy(
                xb[:], xs[dd][:, s_]), "vp")
            p.wait("tensor", "sp", sc)
            p.wait("tensor", "vp", vc)
            p.op("tensor", lambda E, dd=d: E.matmul(ps_s1[:], ones_col[:], xb[:],
                                                    start=(dd == 0), stop=(dd == 7)))
            pc = p.op("tensor", lambda E, dd=d: E.matmul(ps_s2[:], ones_col[:], x2b[:],
                                                         start=(dd == 0), stop=(dd == 7)), "pp")
        p.wait("vector", "pp", pc)
        p.opss("vector", lambda E: E.tensor_scalar(m_[:], ps_s1[:], 1.0 / D, None, op0=ALU.mult))
        p.opss("vector", lambda E: E.tensor_tensor(sq_[:], m_[:], m_[:], ALU.mult))
        p.op("vector", lambda E: E.scalar_tensor_tensor(var_[:], ps_s2[:], 1.0 / D, sq_[:],
                                                        op0=ALU.mult, op1=ALU.subtract), "vp")
        p.wait("scalar", "vp")
        sc = p.op("scalar", lambda E: E.activation(rstd_[:], var_[:], AF.Sqrt,
                                                   bias=c["eps_"][:]), "sp")
        p.wait("vector", "sp", sc)
        vc = p.opss("vector", lambda E: E.reciprocal(c2_[:], rstd_[:]))
        vc = p.op("vector", lambda E: E.tensor_copy(rstd_[:], c2_[:]), "vp")
        p.wait("tensor", "vp", vc)
        p.op("tensor", lambda E: E.matmul(ps_mb[:], onesb[:], m_[:], start=True, stop=True))
        pc = p.op("tensor", lambda E: E.matmul(ps_rb[:], onesb[:], rstd_[:],
                                               start=True, stop=True), "pp")
        p.wait("vector", "pp", pc)
        for d in range(NDC):
            p.op("vector", lambda E, dd=d, s_=sl: E.tensor_sub(tf[:], xs[dd][:, s_], ps_mb[:]))
            if affine is None:
                vc = p.op("vector", lambda E, dd=d, s_=sl: E.tensor_mul(
                    xnb[dd][:, s_], tf[:], ps_rb[:]), "vp")
            else:
                w_t, b_t = affine
                p.op("vector", lambda E: E.tensor_mul(tf[:], tf[:], ps_rb[:]))
                vc = p.op("vector", lambda E, dd=d, s_=sl: E.tensor_scalar(
                    xnb[dd][:, s_], tf[:], w_t[:, dd:dd + 1], b_t[:, dd:dd + 1],
                    op0=ALU.mult, op1=ALU.add), "vp")
        p.wait("tensor", "vp", vc)


def build_ffn0():
    p = P("ffn0")
    x_in = p.din("x", [D, TOK])
    tfcT = p.din("tfcT", [1024, 4096], BF16)
    scfc = p.din("scfc", [4096])
    tprT = p.din("tprT", [4096, 1024], BF16)
    scpr = p.din("scpr", [1024])
    uTd = p.din("uT", [16, 4096], BF16)
    vTd = p.din("vT", [1024, 16], BF16)
    fbTgd = p.din("fbTg", [16, 16], BF16)
    xout_d = p.dout("xout", [D, TOK])

    for s in ("ld", "vp", "sp", "pp", "st", "gp", "wfc", "wpr"):
        p.sem(s)

    xs = [p.sb(f"x{d}", [128, TOK]) for d in range(NDC)]
    xnb = [p.sb(f"xn{d}", [128, TOK], BF16) for d in range(NDC)]
    consts = mk_ln_consts(p)
    p.sem("pp2")
    for d in range(NDC):
        p.dma("sync", xs[d][:], x_in[d * 128:(d + 1) * 128, :], "ld")
    p.wait("vector", "ld")
    p.wait("scalar", "ld")
    emit_ln(p, xs, xnb, TOK, consts)

    vT_s = p.sb("vT_s", [128, 16 * NDC], BF16)
    fb_s = p.sb("fb_s", [16, 16], BF16)
    uT_s = p.sb("uT_s", [16, 4096], BF16)
    t1 = p.sb("t1", [16, TOK], BF16)
    t2 = p.sb("t2", [16, TOK], BF16)
    ps_r = p.ps("ps_r", [16, 512])
    for d in range(NDC):
        p.dma("sync", vT_s[:, d * 16:(d + 1) * 16], vTd[d * 128:(d + 1) * 128, :], "ld")
    p.dma("sync", fb_s[:], fbTgd, "ld")
    c = p.dma("sync", uT_s[:], uTd, "ld")
    p.wait("tensor", "ld", c)
    for ci in range(TOK // 512):
        sl = slice(ci * 512, ci * 512 + 512)
        for d in range(NDC):
            pc = p.op("tensor", lambda E, dd=d, s_=sl: E.matmul(
                ps_r[:], vT_s[:, dd * 16:(dd + 1) * 16], xnb[dd][:, s_],
                start=(dd == 0), stop=(dd == 7)), "pp")
        p.wait("vector", "pp", pc)
        vc = p.op("vector", lambda E, s_=sl: E.tensor_copy(t1[:, s_], ps_r[:]), "vp")
        p.wait("tensor", "vp", vc)
        pc = p.op("tensor", lambda E, s_=sl: E.matmul(ps_r[:], fb_s[:], t1[:, s_],
                                                      start=True, stop=True), "pp")
        p.wait("vector", "pp", pc)
        vc = p.op("vector", lambda E, s_=sl: E.tensor_copy(t2[:, s_], ps_r[:]), "vp")
        p.wait("tensor", "vp", vc)

    fcbuf = [p.sb(f"fcb{i}", [128, 1024], BF16) for i in range(2)]
    scf_s = p.sb("scf_s", [128, 2])
    prbuf = [p.sb(f"prb{i}", [128, 4096], BF16) for i in range(2)]
    rtmp = p.sb("rtmp", [128, 512], BF16)
    r2 = p.sb("r2", [128, 512 * 32], BF16)
    xo = p.sb("xo", [128, 512])
    sch = p.sb("sch", [128, NDC])
    ps_h = [p.ps("ps_h0", [128, 512])]
    ps_o = [p.ps("ps_o0", [128, 512])]

    for d in range(NDC):
        c = p.dma("sync", sch[:, d:d + 1], AP(scpr.tensor, d * 128, [[1, 128]]), "ld")
    p.wait("vector", "ld", c)
    p.op("vector", lambda E: E.tensor_scalar(sch[:], sch[:], 0.5, None, op0=ALU.mult), "vp")

    for ci in range(TOK // 512):
        sl = slice(ci * 512, ci * 512 + 512)
        for hc in range(32):
            h0 = hc * 128
            buf = fcbuf[hc % 2]
            pb = ps_h[0]
            if ci > 0 or hc >= 2:
                p.wait("sync", "wfc", p.cnt["wfc"])
            for d in range(NDC):
                p.dma("sync", buf[:, d * 128:(d + 1) * 128],
                      tfcT[d * 128:(d + 1) * 128, h0:h0 + 128], "ld")
            c = p.dma("sync", scf_s[:, hc % 2:hc % 2 + 1],
                      AP(scfc.tensor, h0, [[1, 128]]), "ld")
            p.wait("tensor", "ld", c)
            for d in range(NDC):
                p.op("tensor", lambda E, dd=d, b=buf, s_=sl, pb_=pb: E.matmul(
                    pb_[:], b[:, dd * 128:(dd + 1) * 128], xnb[dd][:, s_],
                    start=(dd == 0), stop=False))
            pc = p.op("tensor", lambda E, h_=h0, s_=sl, pb_=pb: E.matmul(
                pb_[:], uT_s[:, h_:h_ + 128], t2[:, s_], start=False, stop=True), "pp")
            p.op("tensor", lambda E: E.nop(), "wfc")
            p.wait("scalar", "pp", pc)
            sc = p.op("scalar", lambda E, k=hc % 2, pb_=pb: E.activation(
                rtmp[:], pb_[:], AF.Relu, scale=scf_s[:, k:k + 1]), "sp")
            p.wait("vector", "sp", sc)
            vc = p.op("vector", lambda E, k=hc: E.tensor_mul(
                r2[:, k * 512:(k + 1) * 512], rtmp[:], rtmp[:]), "vp")
            p.wait("tensor", "vp", vc)
            p.wait("tensor", "sp", sc)
        for d in range(NDC):
            bufp = prbuf[d % 2]
            if ci > 0 or d >= 2:
                p.wait("sync", "wpr", p.cnt["wpr"])
            for hc in range(32):
                c = p.dma("sync", bufp[:, hc * 128:(hc + 1) * 128],
                          tprT[hc * 128:(hc + 1) * 128, d * 128:(d + 1) * 128], "ld")
            p.wait("tensor", "ld", c)
            po = ps_o[0]
            for hc in range(32):
                pc = p.op("tensor", lambda E, k=hc, b=bufp, po_=po: E.matmul(
                    po_[:], b[:, k * 128:(k + 1) * 128], r2[:, k * 512:(k + 1) * 512],
                    start=(k == 0), stop=(k == 31)), "pp" if hc == 31 else None)
            p.op("tensor", lambda E: E.nop(), "wpr")
            p.wait("vector", "pp", pc)
            vc = p.op("vector", lambda E, dd=d, s_=sl, po_=po: E.scalar_tensor_tensor(
                xo[:], po_[:], sch[:, dd:dd + 1], xs[dd][:, s_],
                op0=ALU.mult, op1=ALU.add), "vp")
            p.wait("sync", "vp", vc)
            p.dma("sync", xout_d[d * 128:(d + 1) * 128, sl], xo[:], "st")
            p.wait("tensor", "vp", vc)
            p.wait("vector", "st")
    p.wait("sync", "st")
    return p.build()


def build_comb():
    p = P("comb")
    x_in = p.din("x", [D, TOK])
    parts = [p.din(f"part{i}", [D, TOK]) for i in range(4)]
    bias = p.din("bias", [D])
    xout_d = p.dout("xout", [D, TOK])
    for s in ("ld", "vp", "st"):
        p.sem(s)
    a = p.sb("a", [128, TOK])
    b = p.sb("b", [128, TOK])
    t0 = p.sb("t0", [128, TOK])
    t1 = p.sb("t1", [128, TOK])
    bcol = p.sb("bcol", [128, 1])
    for d in range(NDC):
        dsl = slice(d * 128, (d + 1) * 128)
        if d > 0:
            p.wait("sync", "vp")
        c0 = p.dma("sync", a[:], x_in[dsl, :], "ld")
        p.dma("sync", t0[:], parts[0][dsl, :], "ld")
        p.dma("sync", t1[:], parts[1][dsl, :], "ld")
        c = p.dma("sync", bcol[:], AP(bias.tensor, d * 128, [[1, 128]]), "ld")
        p.wait("vector", "ld", c)
        p.op("vector", lambda E: E.tensor_add(b[:], t0[:], t1[:]), "vp")
        p.wait("sync", "vp")
        c = p.dma("sync", t0[:], parts[2][dsl, :], "ld")
        p.dma("sync", t1[:], parts[3][dsl, :], "ld")
        p.wait("vector", "ld")
        p.op("vector", lambda E: E.tensor_add(t0[:], b[:], t0[:]))
        p.op("vector", lambda E: E.tensor_add(t1[:], t0[:], t1[:]))
        p.op("vector", lambda E: E.tensor_add(a[:], t1[:], a[:]))
        vc = p.op("vector", lambda E: E.tensor_scalar(b[:], a[:], bcol[:], None,
                                                      op0=ALU.add), "vp")
        p.wait("sync", "vp", vc)
        p.dma("sync", xout_d[dsl, :], b[:], "st")
        p.wait("vector", "st")
    p.wait("sync", "st")
    return p.build()


def emit_ln_chunk(p, xin_aps, xout_aps, c, affine=None, sname="sp", vname="vp", pname="pp"):
    ones_col, onesb = c["ones_col"], c["ones_row1"]
    ps_s1, ps_s2, ps_mb, ps_rb = c["ps_s1"], c["ps_s2"], c["ps_mb"], c["ps_rb"]
    m_, var_, sq_, rstd_, c2_ = c["m_"], c["var_"], c["sq_"], c["rstd_"], c["c2_"]
    x2b, xb, tf = c["x2b"], c["xb"], c["tf"]
    W = xin_aps[0].shape[1]
    p.wait("scalar", "vp", p.cnt["vp"])
    for d in range(NDC):
        pp_before = p.cnt[pname]
        if d > 0:
            p.wait("scalar", pname, pp_before)
            p.wait("vector", pname, pp_before)
        sc = p.op("scalar", lambda E, dd=d: E.activation(
            x2b[:, 0:W], xin_aps[dd], AF.Square), sname)
        vc = p.op("vector", lambda E, dd=d: E.tensor_copy(xb[:, 0:W], xin_aps[dd]), vname)
        p.wait("tensor", sname, sc)
        p.wait("tensor", vname, vc)
        p.op("tensor", lambda E, dd=d: E.matmul(ps_s1[0:1, 0:W], ones_col[:], xb[:, 0:W],
                                                start=(dd == 0), stop=(dd == 7)))
        pc = p.op("tensor", lambda E, dd=d: E.matmul(ps_s2[0:1, 0:W], ones_col[:],
                                                     x2b[:, 0:W],
                                                     start=(dd == 0), stop=(dd == 7)), pname)
    p.wait("vector", pname, pc)
    p.opss("vector", lambda E: E.tensor_scalar(m_[:, 0:W], ps_s1[0:1, 0:W], 1.0 / D,
                                               None, op0=ALU.mult))
    p.opss("vector", lambda E: E.tensor_tensor(sq_[:, 0:W], m_[:, 0:W], m_[:, 0:W],
                                               ALU.mult))
    p.op("vector", lambda E: E.scalar_tensor_tensor(var_[:, 0:W], ps_s2[0:1, 0:W],
                                                    1.0 / D, sq_[:, 0:W],
                                                    op0=ALU.mult, op1=ALU.subtract), vname)
    p.wait("scalar", vname)
    sc = p.op("scalar", lambda E: E.activation(rstd_[:, 0:W], var_[:, 0:W], AF.Sqrt,
                                               bias=c["eps_"][:]), sname)
    p.wait("vector", sname, sc)
    p.opss("vector", lambda E: E.reciprocal(c2_[:, 0:W], rstd_[:, 0:W]))
    vc = p.op("vector", lambda E: E.tensor_copy(rstd_[:, 0:W], c2_[:, 0:W]), vname)
    p.wait("tensor", vname, vc)
    p.op("tensor", lambda E: E.matmul(ps_mb[:, 0:W], onesb[:], m_[:, 0:W],
                                      start=True, stop=True))
    pc = p.op("tensor", lambda E: E.matmul(ps_rb[:, 0:W], onesb[:], rstd_[:, 0:W],
                                           start=True, stop=True), pname)
    p.wait("vector", pname, pc)
    for d in range(NDC):
        p.op("vector", lambda E, dd=d: E.tensor_sub(tf[:, 0:W], xin_aps[dd],
                                                    ps_mb[:, 0:W]))
        if affine is None:
            vc = p.op("vector", lambda E, dd=d: E.tensor_mul(
                xout_aps[dd], tf[:, 0:W], ps_rb[:, 0:W]), vname)
        else:
            w_t, b_t = affine
            p.op("vector", lambda E: E.tensor_mul(tf[:, 0:W], tf[:, 0:W], ps_rb[:, 0:W]))
            vc = p.op("vector", lambda E, dd=d: E.tensor_scalar(
                xout_aps[dd], tf[:, 0:W], w_t[:, dd:dd + 1], b_t[:, dd:dd + 1],
                op0=ALU.mult, op1=ALU.add), vname)
    p.wait("tensor", vname, vc)
    return


DEBUG_L1 = False
L1_PHASES = 2
L1_NOMASK = False
L1_HEADSET = (0, 1, 2, 3)
L1_MAXQC = 16

def build_l1attn():
    p = P("l1attn")
    x_in = p.din("x", [D, S])
    if DEBUG_L1:
        dbg_p = p.dout("dbg_p", [128, 512 * 4])
        dbg_s = p.dout("dbg_s", [128, 512 * 4])
        dbg_y = p.dout("dbg_y", [65, 512])
    wq_d = p.din("wq", [1024, 256], BF16)
    wk_d = p.din("wk", [1024, 256], BF16)
    wv_d = p.din("wv", [1024, 256], BF16)
    wo_d = p.din("wo", [256, 1024], BF16)
    tc_d = p.din("tc", [1, 8])
    part_d = p.dout("part", [D, S])

    for s in ("ld", "vp", "sp", "pp", "gp", "st"):
        p.sem(s)

    consts = mk_ln_consts(p)
    ps4 = p.ps("ps4", [128, 512])
    ps5 = p.ps("ps5", [128, 512])
    ps6 = p.ps("ps6", [128, 512])
    ps7 = p.ps("ps7", [128, 512])

    xin = [p.sb("xin0", [128, 4096])]
    xnb = p.sb("xnb", [128, 4096], BF16)
    qfin = [p.sb(f"qfin{i}", [128, S], BF16) for i in range(2)]
    kfin = [p.sb(f"kfin{i}", [128, S], BF16) for i in range(2)]
    vT = p.sb("vT", [128, 65 * 256], BF16)
    wq_s = p.sb("wq_s", [128, 256 * 8], BF16)
    wk_s = p.sb("wk_s", [128, 256 * 8], BF16)
    wv_s = p.sb("wv_s", [128, 256 * 8], BF16)
    wo_s = [p.sb(f"wo_s{i}", [128, 1024], BF16) for i in range(2)]
    iden = p.sb("iden", [128, 128], BF16)
    emit_identity(p, iden)
    idenf = p.sb("idenf", [128, 128])
    p.op("vector", lambda E: E.tensor_copy(idenf[:], iden[:]), "vp")

    cosT = p.sb("cosT", [128, S], BF16)
    sinT = p.sb("sinT", [128, S], BF16)
    ic = p.sb("ic", [128, 1])
    l32 = p.sb("l32", [128, 1])
    invf = p.sb("invf", [128, 1])
    tc_s = p.sb("tc_s", [1, 8])
    tccol = p.sb("tccol", [128, 8])
    onesb1 = p.sb("onesb1", [1, 128])
    t1r = p.sb("t1r", [128, 512])
    t2r = p.sb("t2r", [128, 512])
    vfm = p.sb("vfm", [128, 512])
    pout = p.sb("pout", [128, 512])
    tmpa = t1r
    tmpb = t2r
    tmpi = p.sb("tmpi", [128, 512], I32)
    tmpn = vfm
    frq = pout
    t10k = p.sb("t10k", [1, 1])

    c = p.dma("sync", tc_s[:], tc_d, "ld")
    for d in range(NDC):
        p.dma("sync", wq_s[:, d * 256:(d + 1) * 256], wq_d[d * 128:(d + 1) * 128, :], "ld")
        p.dma("sync", wk_s[:, d * 256:(d + 1) * 256], wk_d[d * 128:(d + 1) * 128, :], "ld")
        p.dma("sync", wv_s[:, d * 256:(d + 1) * 256], wv_d[d * 128:(d + 1) * 128, :], "ld")
    for i in range(2):
        c = p.dma("sync", wo_s[i][:], wo_d[i * 128:(i + 1) * 128, :], "ld")
    p.wait("vector", "ld", c)
    p.wait("scalar", "ld", c)
    p.wait("gpsimd", "ld", c)
    for d in range(NDC):
        p.op("vector", lambda E, dd=d: E.tensor_scalar(
            wq_s[:, dd * 256:(dd + 1) * 256], wq_s[:, dd * 256:(dd + 1) * 256],
            0.125, None, op0=ALU.mult), "vp")
    p.op("gpsimd", lambda E: E.iota(ic[0:32, :], pattern=[[1, 1]], base=0,
                                    channel_multiplier=1,
                                    allow_small_or_imprecise_dtypes=True), "gp")
    p.wait("vector", "gp")
    for r0 in (32, 64, 96):
        p.op("vector", lambda E, r=r0: E.tensor_copy(ic[r:r + 32, :], ic[0:32, :]), "vp")
    p.op("vector", lambda E: E.memset(t10k[:], 10000.0), "vp")
    p.op("vector", lambda E: E.memset(onesb1[:], 1.0), "vp")
    p.wait("scalar", "vp")
    sc = p.op("scalar", lambda E: E.activation(t10k[:], t10k[:], AF.Ln), "sp")
    p.wait("tensor", "sp", sc)
    pc = p.op("tensor", lambda E: E.matmul(ps4[:, 0:1], onesb1[:], t10k[:],
                                           start=True, stop=True))
    pc = p.op("tensor", lambda E: E.matmul(ps5[:, 0:8], onesb1[:], tc_s[:],
                                           start=True, stop=True), "pp")
    p.wait("vector", "pp", pc)
    p.opss("vector", lambda E: E.tensor_scalar(l32[:], ps4[:, 0:1], -1.0 / 32.0, None,
                                               op0=ALU.mult))
    p.opss("vector", lambda E: E.tensor_copy(tccol[:], ps5[:, 0:8]))
    vc = p.op("vector", lambda E: E.tensor_scalar(invf[:], ic[:], l32[:], None,
                                                  op0=ALU.mult), "vp")
    p.wait("scalar", "vp", vc)
    sc = p.op("scalar", lambda E: E.activation(invf[:], invf[:], AF.Exp), "sp")
    p.wait("vector", "sp", sc)

    def emit_trig(dst, phase_col):
        for ci2 in range(S // 512):
            if True:
                p.wait("gpsimd", "vp", p.cnt["vp"])
            gc = p.op("gpsimd", lambda E, b=ci2 * 512: E.iota(
                tmpa[:], pattern=[[1, 512]], base=b, channel_multiplier=0,
                allow_small_or_imprecise_dtypes=True), "gp")
            p.wait("vector", "gp", gc)
            p.op("vector", lambda E: E.tensor_scalar(frq[:], tmpa[:], invf[:], None,
                                                     op0=ALU.mult), "vp")
            p.wait("vector", "_ss_vector", p.cnt["_ss_vector"]) if False else None
            if phase_col is not None:
                p.opss("vector", lambda E: E.tensor_scalar(
                    frq[:], frq[:], tccol[:, 4:5], None, op0=ALU.add))
            p.opss("vector", lambda E: E.tensor_scalar(tmpb[:], frq[:], tccol[:, 3:4],
                                                       None, op0=ALU.mult))
            p.opss("vector", lambda E: E.tensor_copy(tmpi[:], tmpb[:]))
            p.opss("vector", lambda E: E.tensor_copy(tmpn[:], tmpi[:]))
            for k in range(3):
                p.opss("vector", lambda E, kk=k: E.tensor_scalar(
                    tmpb[:], tmpn[:], tccol[:, kk:kk + 1], None, op0=ALU.mult))
                p.opss("vector", lambda E: E.tensor_sub(frq[:], frq[:], tmpb[:]))
            p.opss("vector", lambda E: E.tensor_scalar(tmpb[:], frq[:], tccol[:, 6:7],
                                                       tccol[:, 5:6], op0=ALU.is_gt,
                                                       op1=ALU.mult))
            p.opss("vector", lambda E: E.tensor_sub(frq[:], frq[:], tmpb[:]))
            p.opss("vector", lambda E: E.tensor_scalar(tmpn[:], frq[:], -1.0, None,
                                                       op0=ALU.mult))
            p.opss("vector", lambda E: E.tensor_scalar(tmpb[:], tmpn[:], tccol[:, 6:7],
                                                       tccol[:, 5:6], op0=ALU.is_gt,
                                                       op1=ALU.mult))
            p.opss("vector", lambda E: E.tensor_add(frq[:], frq[:], tmpb[:]))
            vc2 = p.op("vector", lambda E: E.nop(), "vp")
            p.wait("scalar", "vp", vc2)
            sc2 = p.op("scalar", lambda E, cc=ci2: E.activation(
                dst[:, cc * 512:(cc + 1) * 512], frq[:], AF.Sin), "sp")
            p.wait("vector", "sp", sc2)

    emit_trig(sinT, None)
    emit_trig(cosT, 4)

    p.op("vector", lambda E: E.memset(vT[:], 1.0), "vp")


    def rope_emit(ps_src, dstT, oc, sl512, xc):
        csl = slice(xc * 512, xc * 512 + 512)
        p.op("vector", lambda E: E.tensor_mul(t1r[0:64, :], ps_src[0:64, 0:512],
                                              cosT[0:64, csl]))
        p.op("vector", lambda E: E.tensor_mul(t1r[64:128, :], ps_src[0:64, 0:512],
                                              sinT[0:64, csl]))
        p.op("vector", lambda E: E.tensor_mul(t2r[0:64, :], ps_src[64:128, 0:512],
                                              sinT[64:128, csl]))
        vc = p.op("vector", lambda E: E.tensor_mul(t2r[64:128, :], ps_src[64:128, 0:512],
                                                   cosT[64:128, csl]), "vp")
        p.wait("tensor", "vp", vc)
        p.op("vector", lambda E: E.tensor_sub(dstT[0:32, csl], t1r[0:32, :], t2r[0:32, :]))
        p.op("vector", lambda E: E.tensor_sub(dstT[64:96, csl], t1r[32:64, :], t2r[32:64, :]))
        p.op("vector", lambda E: E.tensor_add(dstT[32:64, csl], t1r[64:96, :], t2r[64:96, :]))
        vc = p.op("vector", lambda E: E.tensor_add(dstT[96:128, csl], t1r[96:128, :],
                                                   t2r[96:128, :]), "vp")
        return vc

    for xc in range(S // 512):
        sl = slice(xc * 512, xc * 512 + 512)
        buf = xin[0]
        if xc >= 1:
            p.wait("sync", "vp", p.cnt["vp"])
            p.wait("sync", "sp", p.cnt["sp"])
        for d in range(NDC):
            c = p.dma("sync", buf[:, d * 512:(d + 1) * 512], x_in[d * 128:(d + 1) * 128, sl],
                      "ld")
        p.wait("vector", "ld", c)
        p.wait("scalar", "ld", c)
        emit_ln_chunk(p, [buf[:, d * 512:(d + 1) * 512] for d in range(NDC)],
                      [xnb[:, d * 512:(d + 1) * 512] for d in range(NDC)], consts)
        for oc in range(2):
            for d in range(NDC):
                pc = p.op("tensor", lambda E, dd=d, o=oc: E.matmul(
                    ps4[:, 0:512], wq_s[:, dd * 256 + o * 128: dd * 256 + o * 128 + 128],
                    xnb[:, dd * 512:(dd + 1) * 512],
                    start=(dd == 0), stop=(dd == 7)), "pp")
            p.wait("vector", "pp", pc)
            rope_emit(ps4, qfin[oc], oc, 512, xc)
            for d in range(NDC):
                pc = p.op("tensor", lambda E, dd=d, o=oc: E.matmul(
                    ps5[:, 0:512], wk_s[:, dd * 256 + o * 128: dd * 256 + o * 128 + 128],
                    xnb[:, dd * 512:(dd + 1) * 512],
                    start=(dd == 0), stop=(dd == 7)), "pp")
            p.wait("vector", "pp", pc)
            rope_emit(ps5, kfin[oc], oc, 512, xc)
            for d in range(NDC):
                pc = p.op("tensor", lambda E, dd=d, o=oc: E.matmul(
                    ps4[:, 0:512], wv_s[:, dd * 256 + o * 128: dd * 256 + o * 128 + 128],
                    xnb[:, dd * 512:(dd + 1) * 512],
                    start=(dd == 0), stop=(dd == 7)), "pp")
            p.wait("vector", "pp", pc)
            vc = p.op("vector", lambda E: E.tensor_copy(vfm[:], ps4[:, 0:512]), "vp")
            p.wait("tensor", "vp", vc)
            for kt4 in range(4):
                for hh in range(2):
                    kt = xc * 4 + kt4
                    h = oc * 2 + hh
                    slot = (kt * 4 + h) * 65
                    pc = p.op("tensor", lambda E, a=hh * 64, b=kt4 * 128: E.transpose(
                        ps7[0:128, 0:64], vfm[a:a + 64, b:b + 128],
                        idenf[a:a + 64, a:a + 64]), "pp")
                    p.wait("vector", "pp", pc)
                    vc = p.op("vector", lambda E, s_=slot: E.tensor_copy(
                        vT[:, s_:s_ + 64], ps7[0:128, 0:64]), "vp")
                    p.wait("tensor", "vp", vc)

    if L1_PHASES == 1:
        p.wait("gpsimd", "vp", p.cnt["vp"])
        p.dma("gpsimd", part_d[0:128, :], qfin[0][:, :], "st")
        p.dma("gpsimd", part_d[128:256, :], kfin[0][:, :], "st")
        p.wait("gpsimd", "st")
        return p.build()
    p.wait("tensor", "vp", p.cnt["vp"])
    p.wait("scalar", "vp", p.cnt["vp"])
    p.wait("gpsimd", "vp", p.cnt["vp"])
    csh = p.sb("csh", [128, 1])
    p.op("vector", lambda E: E.memset(csh[:], -12.0), "vp")
    pbuf = [p.sb(f"pbuf{i}", [128, 512], BF16) for i in range(2)]
    pbuf2 = [p.sb(f"pbuf2{i}", [128, 512], BF16) for i in range(2)]
    rz_s = p.sb("rz_s", [1, 512])
    rz2 = p.sb("rz2", [1, 512])
    rzb = p.sb("rzb", [64, 512])
    ych = [p.sb(f"ych{i}", [128, 512], BF16) for i in range(2)]
    onesb64 = p.sb("onesb64", [1, 64])
    p.op("vector", lambda E: E.memset(onesb64[:], 1.0), "vp")

    for qc in range(min(L1_MAXQC, S // 512)):
        for h in L1_HEADSET:
            oc, hh = h // 2, h % 2
            ktmax = 4 * qc + 4
            for kt in range(ktmax):
                pb = ps4 if kt % 2 == 0 else ps5
                rows_h = slice(hh * 64, hh * 64 + 64)
                pc = p.op("tensor", lambda E, o=oc, r=rows_h, k0=kt * 128, q0=qc * 512,
                          pb_=pb: E.matmul(pb_[:, 0:512], kfin[o][r, k0:k0 + 128],
                                           qfin[o][r, q0:q0 + 512],
                                           start=True, stop=True), "pp")
                if DEBUG_L1 and qc == 0 and h == 0:
                    p.wait("vector", "pp", pc)
                    vcd = p.op("vector", lambda E, pb_=pb: E.tensor_copy(
                        t2r[:], pb_[:, 0:512]), "vp")
                    p.wait("sync", "vp", vcd)
                    p.dma("sync", AP(dbg_s.tensor, kt * 512, [[2048, 128], [1, 512]]),
                          t2r[:], "st")
                p.wait("scalar", "pp", pc)
                sc = p.op("scalar", lambda E, k=kt % 2, pb_=pb: E.activation(
                    pbuf[k][:], pb_[:, 0:512], AF.Exp, bias=csh[:]), "sp")
                p.wait("tensor", "sp", sc)
                ptile = pbuf[kt % 2]
                if (not L1_NOMASK) and kt >= 4 * qc:
                    p.wait("gpsimd", "sp", sc)
                    p.wait("gpsimd", "pp", p.cnt["pp"])
                    gc = p.op("gpsimd", lambda E, k=kt % 2, b=qc * 512 - kt * 128:
                              E.affine_select(pbuf2[k][:], pbuf[k][:],
                                              pattern=[[1, 512]], compare_op=ALU.is_ge,
                                              fill=0.0, base=b, channel_multiplier=-1),
                              "gp")
                    p.wait("tensor", "gp", gc)
                    ptile = pbuf2[kt % 2]
                if DEBUG_L1 and qc == 0 and h == 0:
                    p.wait("vector", "sp", p.cnt["sp"])
                    p.wait("vector", "gp", p.cnt["gp"])
                    vcd = p.op("vector", lambda E, pt=ptile: E.tensor_copy(
                        t1r[:], pt[:]), "vp")
                    p.wait("sync", "vp", vcd)
                    p.dma("sync", AP(dbg_p.tensor, kt * 512, [[2048, 128], [1, 512]]),
                          t1r[:], "st")
                    p.wait("tensor", "vp", vcd)
                pc = p.op("tensor", lambda E, kt_=kt, h_=h, pt=ptile, km=ktmax: E.matmul(
                    ps6[0:65, 0:512], vT[:, (kt_ * 4 + h_) * 65:(kt_ * 4 + h_) * 65 + 65],
                    pt[:], start=(kt_ == 0), stop=(kt_ == km - 1)), "pp")
            p.wait("vector", "pp", pc)
            if DEBUG_L1 and qc == 0 and h == 0:
                vcd = p.op("vector", lambda E: E.tensor_copy(t2r[0:65, :], ps6[0:65, 0:512]),
                           "vp")
                p.wait("sync", "vp", vcd)
                p.dma("sync", dbg_y, t2r[0:65, :], "st")
            p.opss("vector", lambda E: E.reciprocal(rz_s[:], ps6[64:65, 0:512]))
            vc = p.op("vector", lambda E: E.tensor_copy(rz2[:], rz_s[:]), "vp")
            p.wait("tensor", "vp", vc)
            pc = p.op("tensor", lambda E: E.matmul(ps7[0:64, 0:512], onesb64[:], rz2[:],
                                                   start=True, stop=True), "pp")
            p.wait("vector", "pp", pc)
            p.opss("vector", lambda E: E.tensor_copy(rzb[:], ps7[0:64, 0:512]))
            vc = p.op("vector", lambda E, o=oc, a=hh * 64: E.tensor_mul(
                ych[o][a:a + 64, :], ps6[0:64, 0:512], rzb[:]), "vp")
            p.wait("tensor", "vp", vc)
        for d in range(NDC):
            for p2 in range(2):
                pc = p.op("tensor", lambda E, dd=d, pp2=p2: E.matmul(
                    ps7[:, 0:512], wo_s[pp2][:, dd * 128:(dd + 1) * 128], ych[pp2][:],
                    start=(pp2 == 0), stop=(pp2 == 1)), "pp")
            p.wait("vector", "pp", pc)
            vc = p.op("vector", lambda E: E.tensor_copy(pout[:], ps7[:, 0:512]), "vp")
            p.wait("sync", "vp", vc)
            p.dma("sync", part_d[d * 128:(d + 1) * 128, qc * 512:(qc + 1) * 512],
                  pout[:], "st")
            p.wait("tensor", "vp", vc)
            p.wait("vector", "st")
    p.wait("sync", "st")
    return p.build()


def trig_consts():
    import struct
    two_pi = 2 * np.pi
    c1 = np.float32(np.ldexp(np.floor(np.ldexp(two_pi, 11)), -11))
    c2 = np.float32(np.ldexp(np.floor(np.ldexp(two_pi - float(c1), 11 + 22)), -(11 + 22)))
    c3 = np.float32(two_pi - float(c1) - float(c2))
    inv2pi = np.float32(1.0 / two_pi)
    half_pi = np.float32(np.pi / 2)
    return np.array([[c1, c2, c3, inv2pi, half_pi, np.float32(two_pi),
                      np.float32(np.pi), 0]], dtype=np.float32)


def perm_cols_qk(base):
    return [base + 2 * i for i in range(32)] + [base + 2 * i + 1 for i in range(32)]


def l1_q_col_order(g):
    cols = []
    for pair in range(2):
        hA, hB = 4 * g + 2 * pair, 4 * g + 2 * pair + 1
        pa = perm_cols_qk(hA * 64)
        pb = perm_cols_qk(hB * 64)
        cols += pa[0:32] + pb[0:32] + pa[32:64] + pb[32:64]
    return cols


def emit_gating_cols(p, xnb, gw_sb, E, gcols, ts0, nts, tmp):
    gl, m8, g1, msk1, msk2, t_a, g2 = (tmp["gl"], tmp["m8"], tmp["g1"], tmp["msk1"],
                                       tmp["msk2"], tmp["t_a"], tmp["g2"])
    ps_g = tmp["ps_g"]
    xhi, xlo = xnb
    ghi, glo = gw_sb
    for ts in range(nts):
        nmm = 0
        for d in range(NDC):
            for a_, b_ in ((xhi, ghi), (xhi, glo), (xlo, ghi)):
                nmm += 1
                pc = p.op("tensor", lambda E_, dd=d, t=ts, aa=a_, bb=b_, n=nmm: E_.matmul(
                    ps_g[0:128, 0:E], aa[:, dd * 512 + t * 128: dd * 512 + t * 128 + 128],
                    bb[:, dd * E:(dd + 1) * E], start=(n == 1), stop=(n == 24)), "pp")
        p.wait("vector", "pp", pc)
        vc0 = p.op("vector", lambda E_: E_.tensor_copy(gl[:, 0:E], ps_g[0:128, 0:E]), "vp")
        p.wait("tensor", "vp", vc0)
        p.wait("vector", "vp", vc0)
        p.opss("vector", lambda E_: E_.max(out=m8[:], in_=gl[:, 0:E]))
        vc = p.op("vector", lambda E_: E_.tensor_sub(t_a[:, 0:1], m8[:, 0:1], m8[:, 1:2]),
                  "vp")
        p.wait("scalar", "vp", vc)
        sc = p.op("scalar", lambda E_: E_.activation(g1[:], t_a[:, 0:1], AF.Sigmoid), "sp")
        p.wait("vector", "sp", sc)
        p.opss("vector", lambda E_: E_.tensor_scalar(g2[:], g1[:], -1.0, 1.0,
                                                     op0=ALU.mult, op1=ALU.add))
        p.opss("vector", lambda E_: E_.tensor_scalar(msk1[:, 0:E], gl[:, 0:E],
                                                     m8[:, 0:1], None, op0=ALU.is_equal))
        p.opss("vector", lambda E_: E_.tensor_scalar(msk2[:, 0:E], gl[:, 0:E],
                                                     m8[:, 1:2], None, op0=ALU.is_equal))
        p.opss("vector", lambda E_: E_.tensor_scalar(msk1[:, 0:E], msk1[:, 0:E],
                                                     g1[:], None, op0=ALU.mult))
        p.opss("vector", lambda E_: E_.tensor_scalar(msk2[:, 0:E], msk2[:, 0:E],
                                                     g2[:], None, op0=ALU.mult))
        p.op("vector", lambda E_, t=ts: E_.tensor_add(
            gcols[:, (ts0 + t) * E:(ts0 + t + 1) * E], msk1[:, 0:E], msk2[:, 0:E]), "vp")


MOA_DBG = False

def build_moa():
    p = P("moa")
    x_in = p.din("x", [D, S])
    if MOA_DBG:
        dbg_gl = p.dout("dbg_gl", [128, 8])
    ln1w_d = p.din("ln1w", [D])
    ln1b_d = p.din("ln1b", [D])
    gw_d = p.din("gw", [D, 8])
    qw_d = [p.din(f"qw{e}", [D, 128]) for e in range(2)]
    qb_d = p.din("qb", [2, 128])
    kw_d = p.din("kw", [D, 128])
    kb_d = p.din("kb", [128])
    vw_d = p.din("vw", [D, 128])
    vb_d = p.din("vb", [128])
    ow_d = [p.din(f"ow{e}", [128, D]) for e in range(2)]
    relT_d = p.din("relT", [128, 129], BF16)
    part_d = p.dout("part", [S, D])
    pext = [p.nc.dram_tensor(f"pext{e}", [S, 1536], F32, kind="Internal").ap()
            for e in range(2)]

    for s in ("ld", "vp", "sp", "pp", "gp", "st", "tn"):
        p.sem(s)
    pool = [p.ps(f"pl{i}", [128, 512]) for i in range(8)]
    consts = mk_ln_consts(p, pool=pool)

    xin = p.sb("xin", [128, 4096])
    srcf = p.sb("srcf", [128, 4096])
    xnb = p.sb("xnb", [128, 4096], BF16)
    xn1f = p.sb("xn1f", [128, 4096])
    xn1lo = p.sb("xn1lo", [128, 4096], BF16)
    gwhi = p.sb("gwhi", [128, 64], BF16)
    gwlo = p.sb("gwlo", [128, 64], BF16)
    gwf = p.sb("gwf", [128, 64])
    qfm = [p.sb(f"qfm{e}", [128, S], BF16) for e in range(2)]
    kfm = p.sb("kfm", [128, S], BF16)
    vT = p.sb("vT", [128, S], BF16)
    gw_sb = p.sb("gw_sb", [128, 64], BF16)
    qw_sb = [p.sb(f"qw_sb{e}", [128, 1024], BF16) for e in range(2)]
    kw_sb = p.sb("kw_sb", [128, 1024], BF16)
    vw_sb = p.sb("vw_sb", [128, 1024], BF16)
    ow_sb = [p.sb(f"ow_sb{e}", [128, 1024], BF16) for e in range(2)]
    relT_sb = p.sb("relT_sb", [128, 129], BF16)
    wcol = p.sb("wcol", [128, NDC])
    bcol = p.sb("bcol", [128, NDC])
    qb_c = p.sb("qb_c", [128, 2])
    kb_c = p.sb("kb_c", [128, 1])
    vb_c = p.sb("vb_c", [128, 1])
    gcols = [p.sb(f"gcols{e}", [128, 64]) for e in range(2)]
    tmpg = dict(gl=p.sb("gl", [128, 16]), m8=p.sb("m8", [128, 8]),
                g1=p.sb("g1", [128, 1]), g2=p.sb("g2", [128, 1]),
                msk1=p.sb("msk1", [128, 16]), msk2=p.sb("msk2", [128, 16]),
                t_a=p.sb("t_a", [128, 2]), ps_g=pool[4])
    gall = p.sb("gall", [128, 8 * 64])
    if MOA_DBG:
        tmpg["dbg"] = dbg_gl
        tmpg["dbg2"] = p.sb("dbg2", [128, 8])
    P0t = [p.sb(f"P0t{e}", [128, 64]) for e in range(2)]
    P128t = [p.sb(f"P128t{e}", [128, 64]) for e in range(2)]
    Psb = p.sb("Psb", [128, 129])
    onesw = p.sb("onesw", [128, 831])
    zext = p.sb("zext", [128, 831])
    fexp = p.sb("fexp", [128, 2])
    negc = p.sb("negc", [128, 1])
    bias_sb = p.sb("bias_sb", [128, 512])
    s2 = p.sb("s2", [128, 512])
    pch = [p.sb(f"pch{i}", [128, 512], BF16) for i in range(2)]
    zacc = p.sb("zacc", [128, 16])
    zrow = p.sb("zrow", [128, 1])
    rz = p.sb("rz", [128, 1])
    gz = p.sb("gz", [128, 1])
    ydg = p.sb("ydg", [128, 128], BF16)
    pTe = p.sb("pTe", [128, 128], BF16)
    ydgT = p.sb("ydgT", [128, 128], BF16)
    pout = p.sb("pout2", [128, 128])
    iden = p.sb("iden", [128, 128], BF16)
    emit_identity(p, iden)

    def ldcast(dst_b, src_ap, ncolw, f32buf):
        for d2 in range(NDC):
            c = p.dma("sync", f32buf[:, 0:ncolw], src_ap[d2 * 128:(d2 + 1) * 128, :], "ld")
            p.wait("vector", "ld", c)
            p.op("vector", lambda E_, dd=d2, n=ncolw: E_.tensor_copy(
                dst_b[:, dd * n:(dd + 1) * n], f32buf[:, 0:n]), "vp")
            p.wait("sync", "vp")
    f32buf = p.sb("f32buf", [128, 1024])
    ldcast(gw_sb, gw_d, 8, f32buf)
    for d2 in range(NDC):
        c = p.dma("sync", gwf[:, d2 * 8:(d2 + 1) * 8], gw_d[d2 * 128:(d2 + 1) * 128, :],
                  "ld")
    p.wait("vector", "ld", c)
    p.op("vector", lambda E_: E_.tensor_copy(gwhi[:], gwf[:]))
    p.op("vector", lambda E_: E_.tensor_sub(gwf[:], gwf[:], gwhi[:]))
    p.op("vector", lambda E_: E_.tensor_copy(gwlo[:], gwf[:]), "vp")
    for e in range(2):
        ldcast(qw_sb[e], qw_d[e], 128, f32buf)
    ldcast(kw_sb, kw_d, 128, f32buf)
    ldcast(vw_sb, vw_d, 128, f32buf)
    for e in range(2):
        c = p.dma("sync", f32buf[:, 0:1024], ow_d[e], "ld")
        p.wait("vector", "ld", c)
        p.op("vector", lambda E_, ee=e: E_.tensor_copy(ow_sb[ee][:], f32buf[:, 0:1024]),
             "vp")
        p.wait("sync", "vp")
    c = p.dma("sync", relT_sb[:], relT_d, "ld")
    for d in range(NDC):
        p.dma("sync", wcol[:, d:d + 1], AP(ln1w_d.tensor, d * 128, [[1, 128]]), "ld")
        p.dma("sync", bcol[:, d:d + 1], AP(ln1b_d.tensor, d * 128, [[1, 128]]), "ld")
    for e in range(2):
        p.dma("sync", qb_c[:, e:e + 1], AP(qb_d.tensor, e * 128, [[1, 128]]), "ld")
    p.dma("sync", kb_c[:], AP(kb_d.tensor, 0, [[1, 128]]), "ld")
    c = p.dma("sync", vb_c[:], AP(vb_d.tensor, 0, [[1, 128]]), "ld")
    p.wait("vector", "ld", c)
    p.op("vector", lambda E_: E_.tensor_scalar(qb_c[:], qb_c[:], 0.08838834764831845,
                                               None, op0=ALU.mult))
    p.op("vector", lambda E_: E_.memset(onesw[:], 1.0))
    p.op("vector", lambda E_: E_.memset(negc[:], -16.0), "vp")

    for xc in range(S // 512):
        sl = slice(xc * 512, xc * 512 + 512)
        if xc >= 1:
            p.wait("sync", "vp", p.cnt["vp"])
            p.wait("sync", "sp", p.cnt["sp"])
        for d in range(NDC):
            c = p.dma("sync", xin[:, d * 512:(d + 1) * 512],
                      x_in[d * 128:(d + 1) * 128, sl], "ld")
        p.wait("vector", "ld", c)
        p.wait("scalar", "ld", c)
        xaps = [xin[:, d * 512:(d + 1) * 512] for d in range(NDC)]
        saps = [srcf[:, d * 512:(d + 1) * 512] for d in range(NDC)]
        naps = [xn1f[:, d * 512:(d + 1) * 512] for d in range(NDC)]
        emit_ln_chunk(p, xaps, saps, consts)
        emit_ln_chunk(p, saps, naps, consts, affine=(wcol, bcol))
        for d in range(NDC):
            p.op("vector", lambda E_, dd=d: E_.tensor_copy(
                xnb[:, dd * 512:(dd + 1) * 512], xn1f[:, dd * 512:(dd + 1) * 512]))
            p.op("vector", lambda E_, dd=d: E_.tensor_sub(
                srcf[:, dd * 512:(dd + 1) * 512], xn1f[:, dd * 512:(dd + 1) * 512],
                xnb[:, dd * 512:(dd + 1) * 512]))
            p.op("vector", lambda E_, dd=d: E_.tensor_copy(
                xn1lo[:, dd * 512:(dd + 1) * 512], srcf[:, dd * 512:(dd + 1) * 512]),
                "vp" if d == 7 else None)
        p.wait("tensor", "vp", p.cnt["vp"])
        for which, wsb, bc, dstt in (("q0", qw_sb[0], qb_c[:, 0:1], qfm[0]),
                                     ("q1", qw_sb[1], qb_c[:, 1:2], qfm[1]),
                                     ("k", kw_sb, kb_c[:], kfm),
                                     ("v", vw_sb, vb_c[:], None)):
            pb = pool[5]
            for d in range(NDC):
                pc = p.op("tensor", lambda E_, dd=d, w=wsb, pb_=pb: E_.matmul(
                    pb_[:, 0:512], w[:, dd * 128:(dd + 1) * 128],
                    xnb[:, dd * 512:(dd + 1) * 512],
                    start=(dd == 0), stop=(dd == 7)), "pp")
            p.wait("scalar", "pp", pc)
            if which in ("q0", "q1"):
                sc = p.op("scalar", lambda E_, w2=dstt, b=bc, s_=sl, pb_=pb: E_.activation(
                    w2[:, s_], pb_[:, 0:512], AF.Identity, bias=b,
                    scale=0.08838834764831845), "sp")
                p.wait("tensor", "sp", sc)
            elif which == "k":
                sc = p.op("scalar", lambda E_, w2=dstt, b=bc, s_=sl, pb_=pb: E_.activation(
                    w2[:, s_], pb_[:, 0:512], AF.Identity, bias=b), "sp")
                p.wait("tensor", "sp", sc)
            else:
                sc = p.op("scalar", lambda E_, b=bc, pb_=pb: E_.activation(
                    s2[:], pb_[:, 0:512], AF.Identity, bias=b), "sp")
                p.wait("vector", "sp", sc)
                vc = p.op("vector", lambda E_: E_.tensor_copy(pch[0][:], s2[:]), "vp")
                p.wait("tensor", "vp", vc)
                for kt4 in range(4):
                    kt = xc * 4 + kt4
                    pc = p.op("tensor", lambda E_, a=kt4 * 128: E_.transpose(
                        pool[6][:, 0:128].bitcast(BF16)[:, 0:128], pch[0][:, a:a + 128],
                        iden[:]), "pp")
                    p.wait("vector", "pp", pc)
                    vc = p.op("vector", lambda E_, k=kt: E_.tensor_copy(
                        vT[:, k * 128:(k + 1) * 128],
                        pool[6][:, 0:128].bitcast(BF16)[:, 0:128]), "vp")
                    p.wait("tensor", "vp", vc)
        emit_gating_cols(p, (xnb, xn1lo), (gwhi, gwlo), 8, gall, xc * 4, 4, tmpg)
        p.op("vector", lambda E_: E_.nop(), "vp")
    for e in range(2):
        ge = MOA_LOCAL_EXPERTS[e]
        for t in range(64):
            p.op("vector", lambda E_, ee=e, tt=t, g=ge: E_.tensor_copy(
                gcols[ee][:, tt:tt + 1], gall[:, tt * 8 + g:tt * 8 + g + 1]),
                "vp" if t == 63 else None)

    p.wait("tensor", "sp", p.cnt["sp"])
    p.wait("tensor", "vp", p.cnt["vp"])
    for e in range(2):
        for t in range(64):
            pc = p.op("tensor", lambda E_, ee=e, tt=t: E_.matmul(
                pool[5][:, 0:129], qfm[ee][:, tt * 128:(tt + 1) * 128], relT_sb[:],
                start=True, stop=True), "pp")
            p.wait("vector", "pp", pc)
            p.opss("vector", lambda E_: E_.tensor_copy(Psb[:], pool[5][:, 0:129]))
            p.op("vector", lambda E_, ee=e, tt=t: E_.tensor_copy(
                P0t[ee][:, tt:tt + 1], Psb[:, 0:1]))
            p.op("vector", lambda E_, ee=e, tt=t: E_.tensor_copy(
                P128t[ee][:, tt:tt + 1], Psb[:, 128:129]))
            p.opss("vector", lambda E_: E_.tensor_scalar(
                zext[:, 0:576], onesw[:, 0:576], Psb[:, 0:1], None, op0=ALU.mult))
            vc = p.op("vector", lambda E_: E_.nop(), "vp")
            p.wait("gpsimd", "vp", vc)
            p.dma("gpsimd", pext[e][t * 128:(t + 1) * 128, 0:576], zext[:, 0:576], "st")
            p.dma("gpsimd", pext[e][t * 128:(t + 1) * 128, 576:705], Psb[:], "st")
            p.opss("vector", lambda E_: E_.tensor_scalar(
                zext[:, 0:831], onesw[:, 0:831], Psb[:, 128:129], None, op0=ALU.mult))
            vc = p.op("vector", lambda E_: E_.nop(), "vp")
            p.wait("gpsimd", "vp", vc)
            p.dma("gpsimd", pext[e][t * 128:(t + 1) * 128, 705:1536], zext[:, 0:831],
                  "st")
            p.wait("vector", "st")
            p.wait("tensor", "vp", vc)

    p.wait("sync", "st", p.cnt["st"])
    p.wait("tensor", "st", p.cnt["st"])
    p.wait("scalar", "vp", p.cnt["vp"])
    for t in range(64):
        q0 = t * 128
        for e in range(2):
            p.wait("vector", "sp", p.cnt["sp"])
            p.op("vector", lambda E_, ee=e, tt=t: E_.tensor_tensor(
                fexp[:, 0:1], P0t[ee][:, tt:tt + 1], negc[:], ALU.add))
            vc = p.op("vector", lambda E_, ee=e, tt=t: E_.tensor_tensor(
                fexp[:, 1:2], P128t[ee][:, tt:tt + 1], negc[:], ALU.add), "vp")
            p.wait("scalar", "vp", vc)
            for c5 in range(16):
                k0 = c5 * 512
                lo_band, hi_band = q0 - 64, q0 + 192
                banded = not (k0 + 512 <= lo_band or k0 >= hi_band)
                pb = pool[c5 % 2]
                pc = p.op("tensor", lambda E_, ee=e, tt=t, kk=k0, pb_=pb: E_.matmul(
                    pb_[:, 0:512], qfm[ee][:, tt * 128:(tt + 1) * 128],
                    kfm[:, kk:kk + 512], start=True, stop=True), "pp")
                if banded:
                    off = t * 128 * 1536 + k0 - t * 128 + 768
                    assert 768 + k0 - (q0 + 127) >= 0 and 768 + k0 + 511 - q0 < 1536
                    p.wait("sync", "vp", p.cnt["vp"])
                    c = p.dma("sync", bias_sb[:],
                              AP(pext[e].tensor, off, [[1535, 128], [1, 512]]), "ld")
                    p.wait("vector", "ld", c)
                    p.wait("vector", "pp", pc)
                    vc = p.op("vector", lambda E_, pb_=pb: E_.tensor_add(
                        s2[:], pb_[:, 0:512], bias_sb[:]), "vp")
                    p.wait("scalar", "vp", vc)
                    sc = p.op("scalar", lambda E_, i=c5 % 2, cc=c5: E_.activation(
                        pch[i][:], s2[:], AF.Exp, bias=negc[:],
                        accum_out=zacc[:, cc:cc + 1]), "sp")
                    p.wait("tensor", "sp", sc)
                    p.wait("vector", "sp", sc)
                else:
                    right = k0 >= hi_band
                    p.wait("scalar", "pp", pc)
                    sc = p.op("scalar", lambda E_, i=c5 % 2, cc=c5, r=right, pb_=pb:
                              E_.activation(pch[i][:], pb_[:, 0:512], AF.Exp,
                                            bias=fexp[:, 1:2] if r else fexp[:, 0:1],
                                            accum_out=zacc[:, cc:cc + 1]), "sp")
                    p.wait("tensor", "sp", sc)
                for kt4 in range(4):
                    kt = c5 * 4 + kt4
                    pc = p.op("tensor", lambda E_, i=c5 % 2, a=kt4 * 128: E_.transpose(
                        pool[2][:, 0:64].bitcast(BF16)[:, 0:128], pch[i][:, a:a + 128],
                        iden[:]), "pp")
                    p.wait("vector", "pp", pc)
                    vc = p.op("vector", lambda E_: E_.tensor_copy(
                        pTe[:], pool[2][:, 0:64].bitcast(BF16)[:, 0:128]), "vp")
                    p.wait("tensor", "vp", vc)
                    pc = p.op("tensor", lambda E_, k=kt: E_.matmul(
                        pool[3][:, 0:128], pTe[:], vT[:, k * 128:(k + 1) * 128],
                        start=(k == 0), stop=(k == 63), skip_group_check=True), "pp")
                    p.op("tensor", lambda E_: E_.nop(), "tn")
                    p.wait("vector", "tn", p.cnt["tn"])
            p.wait("vector", "pp", pc)
            p.opss("vector", lambda E_: E_.reduce_sum(zrow[:], zacc[:], axis=AX.X))
            p.opss("vector", lambda E_: E_.reciprocal(rz[:], zrow[:]))
            p.opss("vector", lambda E_, ee=e, tt=t: E_.tensor_tensor(
                gz[:], rz[:], gcols[ee][:, tt:tt + 1], ALU.mult))
            vc = p.op("vector", lambda E_: E_.tensor_scalar(
                ydg[:], pool[3][:, 0:128], gz[:], None, op0=ALU.mult), "vp")
            p.wait("tensor", "vp", vc)
            pc = p.op("tensor", lambda E_: E_.transpose(
                pool[2][:, 0:64].bitcast(BF16)[:, 0:128], ydg[:], iden[:]), "pp")
            p.wait("vector", "pp", pc)
            vc = p.op("vector", lambda E_: E_.tensor_copy(
                ydgT[:], pool[2][:, 0:64].bitcast(BF16)[:, 0:128]), "vp")
            p.wait("tensor", "vp", vc)
            for dc2 in range(2):
                pc = p.op("tensor", lambda E_, ee=e, d2=dc2: E_.matmul(
                    pool[6 + d2][:, 0:512], ydgT[:],
                    ow_sb[ee][:, d2 * 512:(d2 + 1) * 512],
                    start=(ee == 0), stop=(ee == 1), skip_group_check=True), "pp")
                if e == 1:
                    p.wait("vector", "pp", pc)
                    vc = p.op("vector", lambda E_, d2=dc2: E_.tensor_copy(
                        s2[:], pool[6 + d2][:, 0:512]), "vp")
                    p.wait("gpsimd", "vp", vc)
                    p.dma("gpsimd", part_d[t * 128:(t + 1) * 128,
                                           dc2 * 512:(dc2 + 1) * 512], s2[:], "st")
                    p.wait("vector", "st")
                    p.wait("tensor", "vp", vc)
    if MOA_DBG:
        dbgo = p.dout("dbg_pers", [128, 64 * 4 + 128 + 512])
        p.wait("gpsimd", "vp", p.cnt["vp"])
        p.dma("gpsimd", AP(dbgo.tensor, 0, [[896, 128], [1, 64]]), P0t[0][:], "st")
        p.dma("gpsimd", AP(dbgo.tensor, 64, [[896, 128], [1, 64]]), P128t[0][:], "st")
        p.dma("gpsimd", AP(dbgo.tensor, 128, [[896, 128], [1, 64]]), gcols[0][:], "st")
        p.dma("gpsimd", AP(dbgo.tensor, 192, [[896, 128], [1, 64]]), gcols[1][:], "st")
        p.dma("gpsimd", AP(dbgo.tensor, 256, [[896, 128], [1, 128]]),
              qfm[0][:, 256:384], "st")
        p.dma("gpsimd", AP(dbgo.tensor, 384, [[896, 128], [1, 512]]),
              kfm[:, 1024:1536], "st")
    p.wait("sync", "st")
    p.wait("gpsimd", "st")
    return p.build()


MOA_LOCAL_EXPERTS = (0, 1)


G3_DBG = False

def build_g3():
    p = P("g3")
    x_in = p.din("x", [D, TOK])
    parts = [p.din(f"part{i}", [D, TOK]) for i in range(4)]
    ln1w_d = p.din("ln1w", [D]); ln1b_d = p.din("ln1b", [D])
    ln2w_d = p.din("ln2w", [D]); ln2b_d = p.din("ln2b", [D])
    gw8_d = p.din("gw8", [D, 8])
    ob_d = p.din("ob", [8, D])
    gw16_d = p.din("gw16", [D, 16])
    w1_d = p.din("w1", [16, 1024, 512])
    b1_d = p.din("b1", [16, 512])
    w2_d = p.din("w2", [16, 512, 1024])
    b2_d = p.din("b2", [16, 1024])
    xout_d = p.dout("xout", [D, TOK])

    for s in ("ld", "vp", "sp", "pp", "gp", "st", "tn"):
        p.sem(s)
    pool = [p.ps(f"pl{i}", [128, 512]) for i in range(8)]
    consts = mk_ln_consts(p, pool=pool)

    xin = p.sb("xin", [128, 4096])
    srcf = p.sb("srcf", [128, 4096])
    xn1f = p.sb("xn1f", [128, 4096])
    out1 = p.sb("out1", [128, 4096])
    xn2f = p.sb("xn2f", [128, 4096])
    xn2b = p.sb("xn2b", [128, 4096], BF16)
    outacc = p.sb("outacc", [128, 4096])
    rf = p.sb("rf", [128, 2048])
    rgb = p.sb("rgb", [128, 2048], BF16)
    w1sb = p.sb("w1sb", [128, 4096], BF16)
    w2sb = p.sb("w2sb", [128, 4096], BF16)
    f32buf = p.sb("f32buf", [128, 1024])
    gwf8 = p.sb("gwf8", [128, 64])
    gwf16 = p.sb("gwf16", [128, 128])
    gw8hi = p.sb("gw8hi", [128, 64], BF16); gw8lo = p.sb("gw8lo", [128, 64], BF16)
    gw16hi = p.sb("gw16hi", [128, 128], BF16); gw16lo = p.sb("gw16lo", [128, 128], BF16)
    xg_hi = p.sb("xg_hi", [128, 4096], BF16); xg_lo = p.sb("xg_lo", [128, 4096], BF16)
    obb = p.sb("obb", [8, 1024], BF16)
    b2b = p.sb("b2b", [16, 1024], BF16)
    b1c = p.sb("b1c", [128, 4])
    gA = p.sb("gA", [8, 512], BF16)
    gFblk = p.sb("gFblk", [16, 512], BF16)
    gtb = p.sb("gtb", [128, 16], BF16)
    selb = p.sb("selb", [16, 2048], BF16)
    sel_i1 = p.sb("sel_i1", [16, 1])
    sel_i2 = p.sb("sel_i2", [16, 2048])
    w1col = p.sb("w1col", [128, NDC]); b1col = p.sb("b1col", [128, NDC])
    w2col = p.sb("w2col", [128, NDC]); b2col = p.sb("b2col", [128, NDC])
    tmpg = dict(gl=p.sb("gl", [128, 16]), m8=p.sb("m8", [128, 8]),
                g1=p.sb("g1", [128, 1]), g2=p.sb("g2", [128, 1]),
                msk1=p.sb("msk1", [128, 16]), msk2=p.sb("msk2", [128, 16]),
                t_a=p.sb("t_a", [128, 2]), ps_g=pool[4])
    gall8 = p.sb("gall8", [128, 8 * 4])
    gall16 = p.sb("gall16", [128, 16 * 4])
    gcl = p.sb("gcl", [128, 1])
    tchunk = p.sb("tchunk", [128, 512])
    iden = p.sb("iden", [128, 128], BF16)
    emit_identity(p, iden)
    idenf = p.sb("idenf", [128, 128])
    p.op("vector", lambda E_: E_.tensor_copy(idenf[:], iden[:]), "vp")
    consts["idenf_g3"] = idenf
    if G3_DBG:
        iden1 = p.sb("iden1", [128, 128], BF16)
        iden2 = p.sb("iden2", [128, 128], BF16)
        iden3 = p.sb("iden3", [128, 128], BF16)
        _ic, _ir, _if = p._id_parts
        p.op("vector", lambda E_: E_.tensor_copy(iden1[:], _if[:]))
        p.op("vector", lambda E_: E_.tensor_copy(iden2[:, 0:64], _if[:, 0:64]))
        p.op("vector", lambda E_: E_.tensor_copy(iden2[:, 64:128], _if[:, 64:128]))
        p.op("vector", lambda E_: E_.tensor_scalar(iden3[:], _ic[:], _ir[:], None,
                                                   op0=ALU.is_equal), "vp")
        idsnap = p.sb("idsnap", [128, 512])
        _snap = [0]
        def snap_iden():
            k = _snap[0]; _snap[0] += 1
            if k < 4:
                p.op("vector", lambda E_, kk=k: E_.tensor_copy(
                    idsnap[:, kk * 128:(kk + 1) * 128], iden[:]), "vp")
        snap_iden()
    else:
        def snap_iden():
            pass
    p.op("gpsimd", lambda E_: E_.iota(sel_i1[:], pattern=[[1, 1]], base=0,
                                      channel_multiplier=1,
                                      allow_small_or_imprecise_dtypes=True), "gp")
    p.op("gpsimd", lambda E_: E_.iota(sel_i2[:], pattern=[[1, 16], [0, 128]], base=0,
                                      channel_multiplier=0,
                                      allow_small_or_imprecise_dtypes=True), "gp")
    p.wait("vector", "gp")
    p.op("vector", lambda E_: E_.tensor_scalar(selb[:], sel_i2[:], sel_i1[:], None,
                                               op0=ALU.is_equal), "vp")

    for d in range(NDC):
        p.dma("sync", gwf8[:, d * 8:(d + 1) * 8], gw8_d[d * 128:(d + 1) * 128, :], "ld")
        p.dma("sync", gwf16[:, d * 16:(d + 1) * 16], gw16_d[d * 128:(d + 1) * 128, :], "ld")
        p.dma("sync", w1col[:, d:d + 1], AP(ln1w_d.tensor, d * 128, [[1, 128]]), "ld")
        p.dma("sync", b1col[:, d:d + 1], AP(ln1b_d.tensor, d * 128, [[1, 128]]), "ld")
        p.dma("sync", w2col[:, d:d + 1], AP(ln2w_d.tensor, d * 128, [[1, 128]]), "ld")
        p.dma("sync", b2col[:, d:d + 1], AP(ln2b_d.tensor, d * 128, [[1, 128]]), "ld")
    p.wait("vector", "ld")
    p.op("vector", lambda E_: E_.tensor_copy(gw8hi[:], gwf8[:]))
    p.op("vector", lambda E_: E_.tensor_sub(f32buf[:, 0:64], gwf8[:], gw8hi[:]))
    p.op("vector", lambda E_: E_.tensor_copy(gw8lo[:], f32buf[:, 0:64]))
    p.op("vector", lambda E_: E_.tensor_copy(gw16hi[:], gwf16[:]))
    p.op("vector", lambda E_: E_.tensor_sub(f32buf[:, 0:128], gwf16[:], gw16hi[:]))
    p.op("vector", lambda E_: E_.tensor_copy(gw16lo[:], f32buf[:, 0:128]), "vp")
    p.wait("sync", "vp")
    c = p.dma("sync", f32buf[0:8, 0:1024], ob_d, "ld")
    p.wait("vector", "ld", c)
    p.op("vector", lambda E_: E_.tensor_copy(obb[:], f32buf[0:8, 0:1024]), "vp")
    p.wait("sync", "vp")
    c = p.dma("sync", f32buf[0:16, 0:1024], b2_d, "ld")
    p.wait("vector", "ld", c)
    p.op("vector", lambda E_: E_.tensor_copy(b2b[:], f32buf[0:16, 0:1024]), "vp")
    p.wait("sync", "vp")

    p.wait("vector", "ld", p.cnt["ld"])
    p.op("vector", lambda E_: E_.tensor_copy(iden[:], p._id_parts[2][:]), "vp")
    snap_iden()
    for ci in range(TOK // 512):
        sl = slice(ci * 512, ci * 512 + 512)
        if ci >= 1:
            p.wait("sync", "vp", p.cnt["vp"])
            p.wait("sync", "sp", p.cnt["sp"])
        for d in range(NDC):
            c = p.dma("sync", xin[:, d * 512:(d + 1) * 512],
                      x_in[d * 128:(d + 1) * 128, sl], "ld")
        p.wait("vector", "ld", c)
        p.wait("scalar", "ld", c)
        xaps = [xin[:, d * 512:(d + 1) * 512] for d in range(NDC)]
        saps = [srcf[:, d * 512:(d + 1) * 512] for d in range(NDC)]
        n1aps = [xn1f[:, d * 512:(d + 1) * 512] for d in range(NDC)]
        emit_ln_chunk(p, xaps, saps, consts)
        emit_ln_chunk(p, saps, n1aps, consts, affine=(w1col, b1col))
        for d in range(NDC):
            p.op("vector", lambda E_, dd=d: E_.tensor_copy(
                xg_hi[:, dd * 512:(dd + 1) * 512], xn1f[:, dd * 512:(dd + 1) * 512]))
            p.op("vector", lambda E_, dd=d: E_.tensor_sub(
                tchunk[:], xn1f[:, dd * 512:(dd + 1) * 512],
                xg_hi[:, dd * 512:(dd + 1) * 512]))
            p.op("vector", lambda E_, dd=d: E_.tensor_copy(
                xg_lo[:, dd * 512:(dd + 1) * 512], tchunk[:]),
                "vp" if d == 7 else None)
        p.wait("tensor", "vp", p.cnt["vp"])
        emit_gating_cols(p, (xg_hi, xg_lo), (gw8hi, gw8lo), 8, gall8, 0, 4, tmpg)
        p.wait("tensor", "vp", p.cnt["vp"])
        for ts in range(4):
            vc = p.op("vector", lambda E_, t=ts: E_.tensor_copy(
                gtb[:, 0:8], gall8[:, t * 8:(t + 1) * 8]), "vp")
            p.wait("tensor", "vp", vc)
            pc = p.op("tensor", lambda E_: E_.matmul(
                pool[5][0:8, 0:128], gtb[:, 0:8], iden[:],
                start=True, stop=True), "pp")
            p.wait("vector", "pp", pc)
            vc = p.op("vector", lambda E_, t=ts: E_.tensor_copy(
                gA[0:8, t * 128:(t + 1) * 128], pool[5][0:8, 0:128]), "vp")
            p.wait("tensor", "vp", vc)
        snap_iden()
        if True:
            pass
        for d in range(NDC):
            dsl = slice(d * 128, (d + 1) * 128)
            pc = p.op("tensor", lambda E_, dd=d: E_.matmul(
                pool[6][:, 0:512], obb[:, dd * 128:(dd + 1) * 128], gA[:],
                start=True, stop=True), "pp")
            c = p.dma("sync", tchunk[:], parts[0][dsl, sl], "ld")
            p.wait("vector", "ld", c)
            p.op("vector", lambda E_, dd=d: E_.tensor_add(
                out1[:, dd * 512:(dd + 1) * 512], srcf[:, dd * 512:(dd + 1) * 512],
                tchunk[:]), "vp")
            p.wait("sync", "vp")
            for pi in range(1, 4):
                c = p.dma("sync", tchunk[:], parts[pi][dsl, sl], "ld")
                p.wait("vector", "ld", c)
                p.op("vector", lambda E_, dd=d: E_.tensor_add(
                    out1[:, dd * 512:(dd + 1) * 512], out1[:, dd * 512:(dd + 1) * 512],
                    tchunk[:]), "vp")
                p.wait("sync", "vp")
            p.wait("vector", "pp", p.cnt["pp"])
            p.op("vector", lambda E_, dd=d: E_.tensor_add(
                out1[:, dd * 512:(dd + 1) * 512], out1[:, dd * 512:(dd + 1) * 512],
                pool[6][:, 0:512]), "vp")
            p.wait("tensor", "vp", p.cnt["vp"])
        o1aps = [out1[:, d * 512:(d + 1) * 512] for d in range(NDC)]
        n2aps = [xn2f[:, d * 512:(d + 1) * 512] for d in range(NDC)]
        emit_ln_chunk(p, o1aps, n2aps, consts, affine=(w2col, b2col))
        for d in range(NDC):
            p.op("vector", lambda E_, dd=d: E_.tensor_copy(
                xn2b[:, dd * 512:(dd + 1) * 512], xn2f[:, dd * 512:(dd + 1) * 512]),
                "vp" if d == 7 else None)
        p.wait("tensor", "vp", p.cnt["vp"])
        for d in range(NDC):
            p.op("vector", lambda E_, dd=d: E_.tensor_copy(
                xg_hi[:, dd * 512:(dd + 1) * 512], xn2f[:, dd * 512:(dd + 1) * 512]))
            p.op("vector", lambda E_, dd=d: E_.tensor_sub(
                tchunk[:], xn2f[:, dd * 512:(dd + 1) * 512],
                xg_hi[:, dd * 512:(dd + 1) * 512]))
            p.op("vector", lambda E_, dd=d: E_.tensor_copy(
                xg_lo[:, dd * 512:(dd + 1) * 512], tchunk[:]),
                "vp" if d == 7 else None)
        p.wait("tensor", "vp", p.cnt["vp"])
        emit_gating_cols(p, (xg_hi, xg_lo), (gw16hi, gw16lo), 16, gall16, 0, 4, tmpg)
        p.wait("tensor", "vp", p.cnt["vp"])
        for ts in range(4):
            vc = p.op("vector", lambda E_, t=ts: E_.tensor_copy(
                gtb[:, 0:16], gall16[:, t * 16:(t + 1) * 16]), "vp")
            p.wait("tensor", "vp", vc)
            pc = p.op("tensor", lambda E_: E_.matmul(
                pool[5][0:16, 0:128], gtb[:, 0:16], iden[:],
                start=True, stop=True), "pp")
            p.wait("vector", "pp", pc)
            vc = p.op("vector", lambda E_, t=ts: E_.tensor_copy(
                gFblk[0:16, t * 128:(t + 1) * 128], pool[5][0:16, 0:128]), "vp")
            p.wait("tensor", "vp", vc)
        snap_iden()
        for d in range(NDC):
            p.op("vector", lambda E_, dd=d: E_.memset(
                outacc[:, dd * 512:(dd + 1) * 512], 0.0), "vp" if d == 7 else None)
        for e in range(16):
            for d in range(NDC):
                c = p.dma("sync", f32buf[:, 0:512],
                          AP(w1_d.tensor, e * 1024 * 512 + d * 128 * 512,
                             [[512, 128], [1, 512]]), "ld")
                p.wait("vector", "ld", c)
                p.op("vector", lambda E_, dd=d: E_.tensor_copy(
                    w1sb[:, dd * 512:(dd + 1) * 512], f32buf[:, 0:512]), "vp")
                p.wait("sync", "vp")
            for hc in range(4):
                c = p.dma("sync", f32buf[:, 0:1024],
                          AP(w2_d.tensor, e * 512 * 1024 + hc * 128 * 1024,
                             [[1024, 128], [1, 1024]]), "ld")
                p.wait("vector", "ld", c)
                p.op("vector", lambda E_, h=hc: E_.tensor_copy(
                    w2sb[:, h * 1024:(h + 1) * 1024], f32buf[:, 0:1024]), "vp")
                p.wait("sync", "vp")
            c = p.dma("sync", b1c[:, 0:4], AP(b1_d.tensor, e * 512, [[1, 128], [128, 4]]),
                      "ld", slow=True)
            p.wait("scalar", "ld", c)
            p.wait("tensor", "vp", p.cnt["vp"])
            for hc in range(4):
                for d in range(NDC):
                    pc = p.op("tensor", lambda E_, dd=d, h=hc: E_.matmul(
                        pool[5][:, 0:512],
                        w1sb[:, dd * 512 + h * 128: dd * 512 + h * 128 + 128],
                        xn2b[:, dd * 512:(dd + 1) * 512],
                        start=(dd == 0), stop=(dd == 7)), "pp")
                p.wait("scalar", "pp", pc)
                sc = p.op("scalar", lambda E_, h=hc: E_.activation(
                    rf[:, h * 512:(h + 1) * 512], pool[5][:, 0:512], AF.Relu,
                    bias=b1c[:, h:h + 1]), "sp")
                p.wait("tensor", "sp", sc)
            pc = p.op("tensor", lambda E_, ee=e: E_.matmul(
                pool[6][:, 0:512], selb[:, ee * 128:(ee + 1) * 128], gFblk[:],
                start=True, stop=True), "pp")
            p.wait("vector", "pp", pc)
            p.wait("vector", "sp", p.cnt["sp"])
            for hc in range(4):
                vc = p.op("vector", lambda E_, h=hc: E_.tensor_mul(
                    rgb[:, h * 512:(h + 1) * 512], rf[:, h * 512:(h + 1) * 512],
                    pool[6][:, 0:512]), "vp")
            p.wait("tensor", "vp", vc)
            for d in range(NDC):
                for hc in range(4):
                    pc = p.op("tensor", lambda E_, dd=d, h=hc, ee=e: E_.matmul(
                        pool[7][:, 0:512],
                        w2sb[:, h * 1024 + dd * 128: h * 1024 + dd * 128 + 128],
                        rgb[:, h * 512:(h + 1) * 512],
                        start=(h == 0), stop=(h == 3 and ee != 15)), "pp")
                if e == 15:
                    pc = p.op("tensor", lambda E_, dd=d: E_.matmul(
                        pool[7][:, 0:512], b2b[:, dd * 128:(dd + 1) * 128], gFblk[:],
                        start=False, stop=True), "pp")
                p.wait("vector", "pp", pc)
                vc = p.op("vector", lambda E_, dd=d: E_.tensor_add(
                    outacc[:, dd * 512:(dd + 1) * 512], outacc[:, dd * 512:(dd + 1) * 512],
                    pool[7][:, 0:512]), "vp")
                p.wait("tensor", "vp", vc)
        for d in range(NDC):
            p.op("vector", lambda E_, dd=d: E_.tensor_add(
                tchunk[:], out1[:, dd * 512:(dd + 1) * 512],
                outacc[:, dd * 512:(dd + 1) * 512]))
            vc = p.op("vector", lambda E_, dd=d: E_.scalar_tensor_tensor(
                tchunk[:], tchunk[:], 0.5, xin[:, dd * 512:(dd + 1) * 512],
                op0=ALU.mult, op1=ALU.add), "vp")
            p.wait("gpsimd", "vp", vc)
            p.dma("gpsimd", xout_d[d * 128:(d + 1) * 128, sl], tchunk[:], "st")
            p.wait("vector", "st")
    if G3_DBG:
        W_ = 2816
        dbgo = p.dout("dbg_pers", [128, W_])
        p.wait("gpsimd", "vp", p.cnt["vp"])
        p.dma("gpsimd", AP(dbgo.tensor, 0, [[W_, 8], [1, 512]]), gA[0:8, :], "st")
        p.dma("gpsimd", AP(dbgo.tensor, 512, [[W_, 16], [1, 512]]), gFblk[0:16, :], "st")
        p.dma("gpsimd", AP(dbgo.tensor, 1024, [[W_, 128], [1, 512]]),
              out1[:, 0:512], "st")
        p.dma("gpsimd", AP(dbgo.tensor, 1536, [[W_, 128], [1, 512]]),
              outacc[:, 0:512], "st")
        p.dma("gpsimd", AP(dbgo.tensor, 2048, [[W_, 128], [1, 512]]),
              xn2f[:, 0:512], "st")
        p.dma("gpsimd", AP(dbgo.tensor, 2560, [[W_, 128], [1, 32]]), gall8[:], "st")
        p.dma("gpsimd", AP(dbgo.tensor, 2592, [[W_, 128], [1, 64]]), gall16[:], "st")
        p.dma("gpsimd", AP(dbgo.tensor, 2656, [[W_, 128], [1, 16]]),
              tmpg["gl"][:], "st")
        p.dma("gpsimd", AP(dbgo.tensor, 2672, [[W_, 128], [1, 8]]),
              tmpg["m8"][:], "st")
        p.dma("gpsimd", AP(dbgo.tensor, 2680, [[W_, 128], [1, 16]]),
              tmpg["msk1"][:], "st")
        p.dma("gpsimd", AP(dbgo.tensor, 2696, [[W_, 128], [1, 16]]),
              tmpg["msk2"][:], "st")
        p.dma("gpsimd", AP(dbgo.tensor, 2712, [[W_, 128], [1, 16]]),
              gtb[:], "st")
        dbgo2 = p.dout("dbg_iden", [128, 128])
        p.dma("gpsimd", dbgo2, iden[:], "st")
        dbgo3 = p.dout("dbg_idsnap", [128, 512])
        p.dma("gpsimd", dbgo3, idsnap[:], "st")
        dbgo4 = p.dout("dbg_idparts", [128, 512])
        _ic, _ir, _if = p._id_parts
        p.dma("gpsimd", AP(dbgo4.tensor, 0, [[512, 128], [1, 128]]), _if[:], "st")
        p.dma("gpsimd", AP(dbgo4.tensor, 128, [[512, 128], [1, 128]]), iden1[:], "st")
        p.dma("gpsimd", AP(dbgo4.tensor, 256, [[512, 128], [1, 128]]), iden2[:], "st")
        p.dma("gpsimd", AP(dbgo4.tensor, 384, [[512, 128], [1, 128]]), iden3[:], "st")
    p.wait("sync", "st")
    p.wait("gpsimd", "st")
    return p.build()


L0_NSUB = 17
L0_DBG = False

def build_l0attn():
    p = P("l0attn")
    x_in = p.din("x", [D, S])
    if L0_DBG:
        dbg_stl = p.dout("dbg_stl", [128, 2176])
        dbg_cand = p.dout("dbg_cand", [128, 64])
        dbg_p = p.dout("dbg_pt", [128, 2176])
    wq_d = p.din("wq", [1024, 256])
    bq_d = p.din("bq", [256])
    wk_d = p.din("wk", [1024, 64])
    bk_d = p.din("bk", [64])
    wv_d = p.din("wv", [1024, 64])
    bv_d = p.din("bv", [64])
    qnw_d = p.din("qnw", [64])
    knw_d = p.din("knw", [64])
    slopes_d = p.din("slopes", [1, 4])
    wo_d = p.din("wo", [256, 1024])
    part_d = p.dout("part", [D, S])

    for s in ("ld", "vp", "sp", "pp", "gp", "st", "tn"):
        p.sem(s)
    pool = [p.ps(f"pl{i}", [128, 512]) for i in range(8)]
    consts = mk_ln_consts(p, pool=pool)

    xin = p.sb("xin", [128, 4096])
    xnb = p.sb("xnb", [128, 4096], BF16)
    qfm = [p.sb(f"qfm{i}", [128, S], BF16) for i in range(2)]
    kfm = p.sb("kfm", [128, S], BF16)
    vT = p.sb("vT", [128, 64 * 64], BF16)
    wq_sb = p.sb("wq_sb", [128, 256 * 8], BF16)
    wk_sb = p.sb("wk_sb", [128, 64 * 8], BF16)
    wv_sb = p.sb("wv_sb", [128, 64 * 8], BF16)
    woh = p.sb("woh", [64, 4096], BF16)
    f32buf = p.sb("f32buf", [128, 1024])
    bq_c = p.sb("bq_c", [128, 2])
    bk_c = p.sb("bk_c", [128, 1])
    bv_c = p.sb("bv_c", [128, 1])
    qn_c = p.sb("qn_c", [128, 1])
    kn_c = p.sb("kn_c", [128, 1])
    epsr = p.sb("epsr", [1, 1])
    slop = p.sb("slop", [1, 4])
    onesr64 = p.sb("onesr64", [1, 64])
    ali_row = p.sb("ali_row", [1, 2176])
    iotaw = p.sb("iotaw", [1, 2176])
    alib = p.sb("alib", [128, 2176])
    qraw = p.sb("qraw", [128, 512])
    qsq = p.sb("qsq", [128, 512], BF16)
    rstdr = p.sb("rstdr", [1, 512])
    rstr2 = p.sb("rstr2", [1, 512])
    rb64 = p.sb("rb64", [64, 512])
    stl = p.sb("stl", [128, 2176])
    w1e = p.sb("w1e", [128, 2176])
    w2e = p.sb("w2e", [128, 2176])
    cand = p.sb("cand", [128, 64])
    negm = p.sb("negm", [128, 1])
    thr = p.sb("thr", [128, 1])
    zc = p.sb("zc", [128, 1])
    rz = p.sb("rz", [128, 1])
    etile = p.sb("etile", [128, 2176])
    ptile = p.sb("ptile", [128, 2176], BF16)
    pTs = p.sb("pTs", [128, 128], BF16)
    ynorm = p.sb("ynorm", [128, 64], BF16)
    yT = p.sb("yT", [64, 128 * 4], BF16)
    pout = p.sb("pout3", [128, 128])
    iden = p.sb("iden", [128, 128], BF16)
    emit_identity(p, iden)
    cm_c = p.sb("cm_c", [128, 128])
    cm_r = p.sb("cm_r", [128, 1])
    cmask = p.sb("cmask", [128, 128])
    p.op("gpsimd", lambda E_: E_.iota(cm_c[:], pattern=[[1, 128]], base=0,
                                      channel_multiplier=0,
                                      allow_small_or_imprecise_dtypes=True), "gp")
    p.op("gpsimd", lambda E_: E_.iota(cm_r[:], pattern=[[1, 1]], base=0,
                                      channel_multiplier=1,
                                      allow_small_or_imprecise_dtypes=True), "gp")
    p.wait("vector", "gp")
    p.op("vector", lambda E_: E_.tensor_scalar(cmask[:], cm_c[:], cm_r[:], -1e30,
                                               op0=ALU.is_gt, op1=ALU.mult), "vp")

    def ldcastw(dst_b, src_ap, ncolw):
        for d2 in range(NDC):
            c = p.dma("sync", f32buf[:, 0:ncolw], src_ap[d2 * 128:(d2 + 1) * 128, :], "ld")
            p.wait("vector", "ld", c)
            p.op("vector", lambda E_, dd=d2, n=ncolw: E_.tensor_copy(
                dst_b[:, dd * n:(dd + 1) * n], f32buf[:, 0:n]), "vp")
            p.wait("sync", "vp")
    ldcastw(wq_sb, wq_d, 256)
    ldcastw(wk_sb, wk_d, 64)
    ldcastw(wv_sb, wv_d, 64)
    for i in range(4):
        c = p.dma("sync", f32buf[0:64, 0:1024], wo_d[i * 64:(i + 1) * 64, :], "ld")
        p.wait("vector", "ld", c)
        p.op("vector", lambda E_, ii=i: E_.tensor_copy(
            woh[0:64, ii * 1024:(ii + 1) * 1024], f32buf[0:64, 0:1024]), "vp")
        p.wait("sync", "vp")
    for e in range(2):
        p.dma("sync", bq_c[:, e:e + 1], AP(bq_d.tensor, e * 128, [[1, 128]]), "ld")
    p.dma("sync", bk_c[0:64, :], AP(bk_d.tensor, 0, [[1, 64]]), "ld")
    p.dma("sync", bv_c[0:64, :], AP(bv_d.tensor, 0, [[1, 64]]), "ld")
    p.dma("sync", qn_c[0:64, :], AP(qnw_d.tensor, 0, [[1, 64]]), "ld")
    p.dma("sync", kn_c[0:64, :], AP(knw_d.tensor, 0, [[1, 64]]), "ld")
    c = p.dma("sync", slop[:], slopes_d, "ld")
    p.wait("vector", "ld", c)
    p.op("vector", lambda E_: E_.tensor_copy(qn_c[64:128, :], qn_c[0:64, :]))
    p.op("vector", lambda E_: E_.tensor_copy(kn_c[64:128, :], kn_c[0:64, :]))
    p.op("vector", lambda E_: E_.tensor_copy(bq_c[64:128, 0:1], bq_c[0:64, 0:1]))
    p.op("vector", lambda E_: E_.tensor_scalar(qn_c[:], qn_c[:], 0.125, None,
                                               op0=ALU.mult))
    p.op("vector", lambda E_: E_.memset(epsr[:], 1e-8))
    p.op("vector", lambda E_: E_.memset(onesr64[:], 1.0), "vp")
    for e in range(2):
        c = p.dma("sync", bq_c[:, e:e + 1], AP(bq_d.tensor, e * 128, [[1, 128]]), "ld")
    p.wait("gpsimd", "ld", c)
    gc = p.op("gpsimd", lambda E_: E_.iota(iotaw[:], pattern=[[1, 2176]], base=-2048,
                                           channel_multiplier=0,
                                           allow_small_or_imprecise_dtypes=True), "gp")
    p.wait("vector", "gp", gc)

    def rmsnorm_emit(pb, dst, sl2, base, nwcol, hname):
        sc = p.op("scalar", lambda E_, b=base: E_.activation(
            qsq[b:b + 64, :], pb[b:b + 64, 0:512], AF.Square), "sp")
        p.wait("tensor", "sp", sc)
        pc = p.op("tensor", lambda E_, b=base: E_.matmul(
            pool[6][0:1, 0:512], consts["ones_col"][b:b + 64, :], qsq[b:b + 64, :],
            start=True, stop=True), "pp")
        p.wait("scalar", "pp", pc)
        sc = p.op("scalar", lambda E_: E_.activation(rstdr[:], pool[6][0:1, 0:512],
                                                     AF.Sqrt, bias=epsr[:],
                                                     scale=1.0 / 64.0), "sp")
        p.wait("vector", "sp", sc)
        p.opss("vector", lambda E_: E_.reciprocal(rstr2[:], rstdr[:]))
        vc = p.op("vector", lambda E_: E_.tensor_copy(rstdr[:], rstr2[:]), "vp")
        p.wait("tensor", "vp", vc)
        pc = p.op("tensor", lambda E_: E_.matmul(pool[7][0:64, 0:512], onesr64[:],
                                                 rstdr[:], start=True, stop=True), "pp")
        p.wait("vector", "pp", pc)
        p.opss("vector", lambda E_: E_.tensor_copy(rb64[0:64, :], pool[7][0:64, 0:512]))
        p.op("vector", lambda E_, b=base: E_.tensor_mul(
            qraw[b:b + 64, :], pb[b:b + 64, 0:512], rb64[0:64, :]))
        vc = p.op("vector", lambda E_, b=base, w=nwcol, d2=dst, s_=sl2: E_.tensor_scalar(
            d2[b:b + 64, s_] if d2 is not kfm else d2[0:64, s_],
            qraw[b:b + 64, :], w[b:b + 64, :], None, op0=ALU.mult), "vp")
        p.wait("tensor", "vp", vc)

    for xc in range(S // 512):
        sl = slice(xc * 512, xc * 512 + 512)
        if xc >= 1:
            p.wait("sync", "vp", p.cnt["vp"])
            p.wait("sync", "sp", p.cnt["sp"])
        for d in range(NDC):
            c = p.dma("sync", xin[:, d * 512:(d + 1) * 512],
                      x_in[d * 128:(d + 1) * 128, sl], "ld")
        p.wait("vector", "ld", c)
        p.wait("scalar", "ld", c)
        xaps = [xin[:, d * 512:(d + 1) * 512] for d in range(NDC)]
        naps = [xnb[:, d * 512:(d + 1) * 512] for d in range(NDC)]
        emit_ln_chunk(p, xaps, naps, consts)
        for oc in range(2):
            pb = pool[5]
            for d in range(NDC):
                pc = p.op("tensor", lambda E_, dd=d, o=oc, pb_=pb: E_.matmul(
                    pb_[:, 0:512], wq_sb[:, dd * 256 + o * 128: dd * 256 + o * 128 + 128],
                    xnb[:, dd * 512:(dd + 1) * 512], start=(dd == 0), stop=(dd == 7)),
                    "pp")
            p.wait("scalar", "pp", pc)
            sc = p.op("scalar", lambda E_, o=oc, pb_=pb: E_.activation(
                pb_[:, 0:512], pb_[:, 0:512], AF.Identity, bias=bq_c[:, o:o + 1]), "sp")
            p.wait("vector", "sp", sc)
            p.wait("tensor", "sp", sc)
            for hh in range(2):
                rmsnorm_emit(pb, qfm[oc], sl, hh * 64, qn_c, f"q{oc}{hh}")
        pb = pool[5]
        for d in range(NDC):
            pc = p.op("tensor", lambda E_, dd=d, pb_=pb: E_.matmul(
                pb_[0:64, 0:512], wk_sb[:, dd * 64:(dd + 1) * 64],
                xnb[:, dd * 512:(dd + 1) * 512], start=(dd == 0), stop=(dd == 7)), "pp")
        p.wait("scalar", "pp", pc)
        sc = p.op("scalar", lambda E_, pb_=pb: E_.activation(
            pb_[0:64, 0:512], pb_[0:64, 0:512], AF.Identity, bias=bk_c[0:64, :]), "sp")
        p.wait("vector", "sp", sc)
        p.wait("tensor", "sp", sc)
        rmsnorm_emit(pb, kfm, sl, 0, kn_c, "k")
        p.op("vector", lambda E_, s_=sl: E_.tensor_copy(kfm[64:128, s_], kfm[0:64, s_]),
             "vp")
        pb = pool[5]
        for d in range(NDC):
            pc = p.op("tensor", lambda E_, dd=d, pb_=pb: E_.matmul(
                pb_[0:64, 0:512], wv_sb[:, dd * 64:(dd + 1) * 64],
                xnb[:, dd * 512:(dd + 1) * 512], start=(dd == 0), stop=(dd == 7)), "pp")
        p.wait("scalar", "pp", pc)
        sc = p.op("scalar", lambda E_, pb_=pb: E_.activation(
            qraw[0:64, :], pb_[0:64, 0:512], AF.Identity, bias=bv_c[0:64, :]), "sp")
        p.wait("vector", "sp", sc)
        vc = p.op("vector", lambda E_: E_.tensor_copy(qsq[0:64, :], qraw[0:64, :]), "vp")
        p.wait("tensor", "vp", vc)
        for kt4 in range(4):
            kt = xc * 4 + kt4
            pc = p.op("tensor", lambda E_, a=kt4 * 128: E_.transpose(
                pool[6][:, 0:32].bitcast(BF16)[:, 0:64], qsq[0:64, a:a + 128],
                iden[0:64, 0:64]), "pp")
            p.wait("vector", "pp", pc)
            vc = p.op("vector", lambda E_, k=kt: E_.tensor_copy(
                vT[:, k * 64:(k + 1) * 64], pool[6][:, 0:32].bitcast(BF16)[:, 0:64]),
                "vp")
            p.wait("tensor", "vp", vc)

    p.wait("tensor", "vp", p.cnt["vp"])
    p.wait("tensor", "sp", p.cnt["sp"])
    p.wait("scalar", "vp", p.cnt["vp"])
    p.wait("gpsimd", "vp", p.cnt["vp"])
    for t in range(64):
        lo = max(0, t - 16)
        nsub = t - lo + 1
        L = nsub * 128
        u0 = 2176 - L
        for h in range(4):
            oc, hh = h // 2, h % 2
            if h == 0 or True:
                pass
            vc = p.op("vector", lambda E_, h_=h: E_.tensor_scalar(
                ali_row[:], iotaw[:], slop[:, h_:h_ + 1], None, op0=ALU.mult), "vp")
            p.wait("tensor", "vp", vc)
            nch = _ceil(L, 512)
            for cc in range(nch):
                c0 = cc * 512
                w_ = min(512, L - c0)
                pc = p.op("tensor", lambda E_, a=u0 + c0, w2=w_: E_.matmul(
                    pool[4][:, 0:w2], consts["ones_row1"][:], ali_row[:, a:a + w2],
                    start=True, stop=True), "pp")
                p.wait("vector", "pp", pc)
                vc = p.op("vector", lambda E_, a=c0, w2=w_: E_.tensor_copy(
                    alib[:, a:a + w2], pool[4][:, 0:w2]), "vp")
                p.wait("tensor", "vp", vc)
            for cc in range(nch):
                c0 = cc * 512
                w_ = min(512, L - c0)
                pb = pool[cc % 2]
                for j4 in range(_ceil(w_, 128)):
                    kt = lo + (c0 // 128) + j4
                    pc = p.op("tensor", lambda E_, o=oc, r0=hh * 64, k=kt, j=j4 * 128,
                              t_=t, pb_=pb: E_.matmul(
                        pb_[:, j:j + 128], qfm[o][r0:r0 + 64, t_ * 128:t_ * 128 + 128],
                        kfm[r0:r0 + 64, k * 128:(k + 1) * 128],
                        start=True, stop=True), "pp")
                p.wait("vector", "pp", pc)
                vc = p.op("vector", lambda E_, a=c0, w2=w_, pb_=pb: E_.tensor_add(
                    stl[:, a:a + w2], pb_[:, 0:w2], alib[:, a:a + w2]), "vp")
                p.wait("tensor", "vp", vc)
            vc = p.op("vector", lambda E_, a=L - 128: E_.tensor_add(
                stl[:, a:a + 128], stl[:, a:a + 128], cmask[:]), "vp")
            src_t = stl
            for r in range(8):
                p.opss("vector", lambda E_, rr=r, s_=src_t, w2=L: E_.max(
                    out=cand[:, rr * 8:(rr + 1) * 8], in_=s_[:, 0:w2]))
                dst_t = w1e if src_t is not w1e else w2e
                p.opss("vector", lambda E_, rr=r, s_=src_t, d2=dst_t, w2=L:
                       E_.match_replace(out=d2[:, 0:w2],
                                        in_to_replace=cand[:, rr * 8:(rr + 1) * 8],
                                        in_values=s_[:, 0:w2], imm_value=-1e30))
                src_t = dst_t
            if L0_DBG and t == 2 and h == 1:
                p.wait("sync", "_ss_vector", p.cnt["_ss_vector"])
                p.dma("sync", dbg_stl[:, 0:2176], stl[:], "st")
                p.dma("sync", dbg_cand, cand[:], "st")
            p.opss("vector", lambda E_: E_.tensor_scalar(negm[:], cand[:, 0:1], -1.0,
                                                         None, op0=ALU.mult))
            vc = p.op("vector", lambda E_: E_.tensor_copy(thr[:], cand[:, 63:64]), "vp")
            p.wait("scalar", "vp", vc)
            sc = p.op("scalar", lambda E_: E_.activation(qsq[:, 0:64], cand[:], AF.Exp,
                                                         bias=negm[:],
                                                         accum_out=zc[:]), "sp")
            p.wait("vector", "sp", sc)
            p.opss("vector", lambda E_: E_.reciprocal(rz[:], zc[:]))
            for cc in range(nch):
                c0 = cc * 512
                w_ = min(512, L - c0)
                sc = p.op("scalar", lambda E_, a=c0, w2=w_: E_.activation(
                    etile[:, a:a + w2], stl[:, a:a + w2], AF.Exp, bias=negm[:]), "sp")
                p.wait("vector", "sp", sc)
                vc = p.op("vector", lambda E_, a=c0, w2=w_: E_.scalar_tensor_tensor(
                    ptile[:, a:a + w2], stl[:, a:a + w2], thr[:], etile[:, a:a + w2],
                    op0=ALU.is_ge, op1=ALU.mult), "vp")
            p.wait("tensor", "vp", vc)
            if L0_DBG and t == 2 and h == 1:
                p.wait("sync", "vp", p.cnt["vp"])
                vcd = p.op("gpsimd", lambda E_: E_.dma_start(
                    out=dbg_p[:, 0:2176], in_=ptile[:]), "st", 16)
            for j in range(nsub):
                kt = lo + j
                pc = p.op("tensor", lambda E_, a=j * 128: E_.transpose(
                    pool[2][:, 0:64].bitcast(BF16)[:, 0:128], ptile[:, a:a + 128],
                    iden[:]), "pp")
                p.wait("vector", "pp", pc)
                vc = p.op("vector", lambda E_: E_.tensor_copy(
                    pTs[:], pool[2][:, 0:64].bitcast(BF16)[:, 0:128]), "vp")
                p.wait("tensor", "vp", vc)
                pc = p.op("tensor", lambda E_, k=kt, j_=j, n=nsub: E_.matmul(
                    pool[3][:, 0:64], pTs[:], vT[:, k * 64:(k + 1) * 64],
                    start=(j_ == 0), stop=(j_ == n - 1), skip_group_check=True), "pp")
                p.op("tensor", lambda E_: E_.nop(), "tn")
                p.wait("vector", "tn", p.cnt["tn"])
            p.wait("vector", "pp", pc)
            vc = p.op("vector", lambda E_: E_.tensor_scalar(
                ynorm[:], pool[3][:, 0:64], rz[:], None, op0=ALU.mult), "vp")
            p.wait("tensor", "vp", vc)
            pc = p.op("tensor", lambda E_: E_.transpose(
                pool[2][0:64, 0:64].bitcast(BF16)[:, 0:128], ynorm[:], iden[:]), "pp")
            p.wait("vector", "pp", pc)
            vc = p.op("vector", lambda E_, h_=h: E_.tensor_copy(
                yT[0:64, h_ * 128:(h_ + 1) * 128],
                pool[2][0:64, 0:64].bitcast(BF16)[:, 0:128]), "vp")
            p.wait("tensor", "vp", vc)
        for d in range(NDC):
            for h in range(4):
                pc = p.op("tensor", lambda E_, dd=d, h_=h: E_.matmul(
                    pool[6][:, 0:128],
                    woh[0:64, h_ * 1024 + dd * 128: h_ * 1024 + dd * 128 + 128],
                    yT[0:64, h_ * 128:(h_ + 1) * 128],
                    start=(h_ == 0), stop=(h_ == 3), skip_group_check=True), "pp")
            p.wait("vector", "pp", pc)
            vc = p.op("vector", lambda E_: E_.tensor_copy(pout[:], pool[6][:, 0:128]),
                      "vp")
            p.wait("gpsimd", "vp", vc)
            p.dma("gpsimd", part_d[d * 128:(d + 1) * 128, t * 128:(t + 1) * 128],
                  pout[:], "st")
            p.wait("vector", "st")
            p.wait("tensor", "vp", vc)
    p.wait("sync", "st")
    p.wait("gpsimd", "st")
    return p.build()


_NC_CACHE = {}

def _get_nc(name, builder):
    if name not in _NC_CACHE:
        _NC_CACHE[name] = builder()
    return _NC_CACHE[name]

def _ct(a):
    return np.ascontiguousarray(a)

def kernel(**z):
    ncores = list(range(NCORES))
    f32 = np.float32

    nc = _get_nc("prep", build_prep)
    im = dict(w_fc=_ct(z["L0_ffn_w_fc"]), w_proj=_ct(z["L0_ffn_w_proj"]),
              u=_ct(z["L0_ffn_u"]), v=_ct(z["L0_ffn_v"]), fast_b=_ct(z["L0_ffn_fast_b"]),
              gate=np.asarray(z["L0_ffn_gate"], f32).reshape(1, 1),
              W_KQV=_ct(z["L1_attn_W_KQV"]), W_O=_ct(z["L1_attn_W_O"]),
              rel=_ct(z["L1_ffn_rel_table"]))
    pr = {k: np.asarray(v) for k, v in
          run_bass_kernel_spmd(nc, [im] * NCORES, core_ids=ncores).results[0].items()}

    x = np.asarray(z["L0_attn_x"], f32)
    xf = [_ct(x[b].T) for b in range(2)]

    def slices(xfb):
        return [_ct(xfb[c // 4][:, (c % 4) * TOK:(c % 4 + 1) * TOK]) for c in range(NCORES)]

    def run(name, builder, ims):
        import sys, time as _t
        t0 = _t.time()
        print(f"[kernel] launch {name} ...", file=sys.stderr, flush=True)
        nc2 = _get_nc(name, builder)
        r = run_bass_kernel_spmd(nc2, ims, core_ids=ncores).results
        print(f"[kernel] launch {name} done {_t.time()-t0:.1f}s", file=sys.stderr,
              flush=True)
        return r

    def ffn0(xfb):
        ims = []
        xs = slices(xfb)
        for c in range(NCORES):
            ims.append(dict(x=xs[c], tfcT=pr["tfcT"], scfc=pr["scfc"], tprT=pr["tprT"],
                            scpr=pr["scpr"], uT=pr["uT"], vT=pr["vT"], fbTg=pr["fbTg"]))
        res = run("ffn0", build_ffn0, ims)
        return [np.concatenate([np.asarray(res[b * 4 + g2]["xout"]) for g2 in range(4)],
                               axis=1) for b in range(2)]

    def l0attn(xfb):
        slopes = (2.0 ** (-(np.arange(16, dtype=f32) / 15 * 8.0))).astype(f32)
        ims = []
        for c in range(NCORES):
            b, g = c // 4, c % 4
            ims.append(dict(
                x=xfb[b],
                wq=_ct(z["L0_attn_wq"][:, g * 256:(g + 1) * 256]),
                bq=_ct(z["L0_attn_bq"][g * 256:(g + 1) * 256]),
                wk=_ct(z["L0_attn_wkv"][:, g * 64:(g + 1) * 64]),
                bk=_ct(z["L0_attn_bkv"][g * 64:(g + 1) * 64]),
                wv=_ct(z["L0_attn_wkv"][:, 256 + g * 64:256 + (g + 1) * 64]),
                bv=_ct(z["L0_attn_bkv"][256 + g * 64:256 + (g + 1) * 64]),
                qnw=_ct(z["L0_attn_qn_w"]), knw=_ct(z["L0_attn_kn_w"]),
                slopes=_ct(slopes[4 * g:4 * g + 4]).reshape(1, 4),
                wo=_ct(z["L0_attn_wo"][g * 256:(g + 1) * 256, :])))
        res = run("l0attn", build_l0attn, ims)
        return [np.asarray(res[c]["part"]) for c in range(NCORES)]

    def comb(xfb, parts, bias):
        ims = []
        for c in range(NCORES):
            b, g = c // 4, c % 4
            tsl = slice(g * TOK, (g + 1) * TOK)
            ims.append(dict(
                x=_ct(xfb[b][:, tsl]),
                part0=_ct(parts[b * 4 + 0][:, tsl]), part1=_ct(parts[b * 4 + 1][:, tsl]),
                part2=_ct(parts[b * 4 + 2][:, tsl]), part3=_ct(parts[b * 4 + 3][:, tsl]),
                bias=_ct(bias)))
        res = run("comb", build_comb, ims)
        return [np.concatenate([np.asarray(res[b * 4 + g2]["xout"]) for g2 in range(4)],
                               axis=1) for b in range(2)]

    def l1attn(xfb):
        tc = trig_consts()
        wkqvT = pr["wkqvT"]
        ims = []
        for c in range(NCORES):
            b, g = c // 4, c % 4
            qcols = [1024 + i for i in l1_q_col_order(g)]
            kcols = [0 + i for i in l1_q_col_order(g)]
            vcols = [2048 + g * 256 + i for i in range(256)]
            ims.append(dict(
                x=xfb[b],
                wq=_ct(wkqvT[:, qcols]), wk=_ct(wkqvT[:, kcols]), wv=_ct(wkqvT[:, vcols]),
                wo=_ct(pr["woT"][g * 256:(g + 1) * 256, :]), tc=tc))
        res = run("l1attn", build_l1attn, ims)
        return [np.asarray(res[c]["part"]) for c in range(NCORES)]

    def moa(xfb):
        ims = []
        for c in range(NCORES):
            b, g = c // 4, c % 4
            perm = [2 * g, 2 * g + 1] + [e for e in range(8) if e not in (2 * g, 2 * g + 1)]
            ims.append(dict(
                x=xfb[b], ln1w=_ct(z["L1_ffn_ln1_w"]), ln1b=_ct(z["L1_ffn_ln1_b"]),
                gw=_ct(z["L1_ffn_attn_gate_w"][:, perm]),
                qw0=_ct(z["L1_ffn_q_w"][2 * g]), qw1=_ct(z["L1_ffn_q_w"][2 * g + 1]),
                qb=_ct(z["L1_ffn_q_b"][2 * g:2 * g + 2]),
                kw=_ct(z["L1_ffn_k_w"]), kb=_ct(z["L1_ffn_k_b"]),
                vw=_ct(z["L1_ffn_v_w"]), vb=_ct(z["L1_ffn_v_b"]),
                ow0=_ct(z["L1_ffn_o_w"][2 * g]), ow1=_ct(z["L1_ffn_o_w"][2 * g + 1]),
                relT=pr["relT"]))
        res = run("moa", build_moa, ims)
        return [np.asarray(res[c]["part"]) for c in range(NCORES)]

    def g3(xfb, mparts):
        ims = []
        for c in range(NCORES):
            b, g = c // 4, c % 4
            tsl = slice(g * TOK, (g + 1) * TOK)
            pslices = [_ct(mparts[b * 4 + j][tsl, :].T) for j in range(4)]
            ims.append(dict(
                x=_ct(xfb[b][:, tsl]),
                part0=pslices[0], part1=pslices[1], part2=pslices[2], part3=pslices[3],
                ln1w=_ct(z["L1_ffn_ln1_w"]), ln1b=_ct(z["L1_ffn_ln1_b"]),
                ln2w=_ct(z["L1_ffn_ln2_w"]), ln2b=_ct(z["L1_ffn_ln2_b"]),
                gw8=_ct(z["L1_ffn_attn_gate_w"]), ob=_ct(z["L1_ffn_o_b"]),
                gw16=_ct(z["L1_ffn_ff_gate_w"]),
                w1=_ct(z["L1_ffn_ff_w1"]), b1=_ct(z["L1_ffn_ff_b1"]),
                w2=_ct(z["L1_ffn_ff_w2"]), b2=_ct(z["L1_ffn_ff_b2"])))
        res = run("g3", build_g3, ims)
        return [np.concatenate([np.asarray(res[b * 4 + g2]["xout"]) for g2 in range(4)],
                               axis=1) for b in range(2)]

    x1 = ffn0(xf)
    parts = l0attn(x1)
    x2 = comb(x1, parts, z["L0_attn_bo"])
    x3 = ffn0(x2)
    mparts = moa(x3)
    x4 = g3(x3, mparts)
    parts = l1attn(x4)
    x5 = comb(x4, parts, np.zeros(D, f32))
    mparts = moa(x5)
    x6 = g3(x5, mparts)

    out = np.stack([x6[0].T, x6[1].T], axis=0).astype(f32)
    return np.ascontiguousarray(out)


# version 07
# speedup vs baseline: 1.0043x; 1.0043x over previous
import numpy as np
import ml_dtypes
from contextlib import ExitStack
import concourse.bass as bass
import concourse.mybir as mybir
from concourse.ap import AP
from concourse.bass_utils import run_bass_kernel_spmd

F32 = mybir.dt.float32
BF16 = mybir.dt.bfloat16
I32 = mybir.dt.int32
AF = mybir.ActivationFunctionType
ALU = mybir.AluOpType
AX = mybir.AxisListType

D = 1024
NDC = 8
TOK = 2048
S = 8192
NCORES = 8

ENGS = ("sync", "vector", "scalar", "tensor", "gpsimd")


class P:

    def __init__(self, name="p"):
        self.nc = bass.Bass("TRN2", target_bir_lowering=False)
        self.q = {e: [] for e in ENGS}
        self.es = ExitStack()
        self.sems = {}
        self.cnt = {}
        self.name = name

    def sem(self, name):
        h = self.es.enter_context(self.nc.semaphore(name))
        self.sems[name] = h
        self.cnt[name] = 0
        return name

    def sb(self, name, shape, dt=F32):
        return self.es.enter_context(self.nc.sbuf_tensor(name, shape, dt))

    def ps(self, name, shape, dt=F32):
        return self.es.enter_context(self.nc.psum_tensor(name, shape, dt))

    def din(self, name, shape, dt=F32):
        return self.nc.dram_tensor(name, shape, dt, kind="ExternalInput").ap()

    def dout(self, name, shape, dt=F32):
        return self.nc.dram_tensor(name, shape, dt, kind="ExternalOutput").ap()

    def add(self, eng, fn):
        self.q[eng].append(fn)

    def wait(self, eng, sem, val=None):
        v = self.cnt[sem] if val is None else val
        self.add(eng, lambda E, s=self.sems[sem], vv=v: E.wait_ge(s, vv))

    def op(self, eng, fn, sem=None, by=1):
        if sem is None:
            self.add(eng, fn)
            return None
        self.cnt[sem] += by
        self.add(eng, lambda E, f=fn, s=self.sems[sem], b=by: f(E).then_inc(s, b))
        return self.cnt[sem]

    def opss(self, eng, fn):
        key = "_ss_" + eng
        if key not in self.sems:
            self.sem(key)
        c = self.op(eng, fn, key, 1)
        self.wait(eng, key, c)
        return c

    def dma(self, eng, dst, src, sem, slow=False):
        if slow:
            return self.op(eng, lambda E, d=dst, s=src: E.dma_start(
                out=d, in_=s, allow_slow_non_contiguous=True), sem, 16)
        return self.op(eng, lambda E, d=dst, s=src: E.dma_start(out=d, in_=s), sem, 16)

    def build(self):
        with self.es, self.nc.Block() as blk:
            for eng in ENGS:
                if self.q[eng]:
                    lst = self.q[eng]
                    getattr(blk, eng)(lambda E, L=lst: [f(E) for f in L])
        return self.nc


def _ceil(a, b):
    return (a + b - 1) // b


def emit_identity(p, iden):
    iot_c = p.sb("id_iot_c" + iden.name, [128, 128])
    iot_r = p.sb("id_iot_r" + iden.name, [128, 1])
    idf = p.sb("id_f" + iden.name, [128, 128])
    p.op("gpsimd", lambda E: E.iota(iot_c[:], pattern=[[1, 128]], base=0,
                                    channel_multiplier=0,
                                    allow_small_or_imprecise_dtypes=True), "gp")
    p.op("gpsimd", lambda E: E.iota(iot_r[:], pattern=[[1, 1]], base=0,
                                    channel_multiplier=1,
                                    allow_small_or_imprecise_dtypes=True), "gp")
    p.wait("vector", "gp")
    p.op("vector", lambda E: E.tensor_scalar(idf[:], iot_c[:], iot_r[:], None,
                                             op0=ALU.is_equal))
    p.op("vector", lambda E: E.tensor_copy(iden[:], idf[:]), "vp")
    p._id_parts = (iot_c, iot_r, idf)


def build_prep():
    p = P("prep")
    w_fc = p.din("w_fc", [4096, 1024])
    w_proj = p.din("w_proj", [1024, 4096])
    u = p.din("u", [4096, 16])
    v = p.din("v", [16, 1024])
    fast_b = p.din("fast_b", [16, 16])
    gate = p.din("gate", [1, 1])
    W_KQV = p.din("W_KQV", [3072, 1024])
    W_O = p.din("W_O", [1024, 1024])
    rel = p.din("rel", [129, 128])

    tfcT = p.dout("tfcT", [1024, 4096], BF16)
    scfc = p.dout("scfc", [4096])
    tprT = p.dout("tprT", [4096, 1024], BF16)
    scpr = p.dout("scpr", [1024])
    wkqvT = p.dout("wkqvT", [1024, 3072], BF16)
    woT = p.dout("woT", [1024, 1024], BF16)
    uT = p.dout("uT", [16, 4096], BF16)
    vT = p.dout("vT", [1024, 16], BF16)
    fbTg = p.dout("fbTg", [16, 16], BF16)
    relT = p.dout("relT", [128, 129], BF16)

    for s in ("ld", "vp", "pp", "st", "gp"):
        p.sem(s)

    iden = p.sb("iden", [128, 128], BF16)
    emit_identity(p, iden)

    wrow = p.sb("wrow", [128, 4096])
    stat = p.sb("stat", [128, 1])
    stat2 = p.sb("stat2", [128, 1])
    rinv = p.sb("rinv", [128, 1])
    terni = p.sb("terni", [128, 4096], I32)
    tsgn = p.sb("tsgn", [128, 4096])
    tabs = p.sb("tabs", [128, 4096])
    ternf = p.sb("ternf", [128, 4096])
    ternb = p.sb("ternb", [128, 4096], BF16)
    evb = p.sb("evb", [128, 128], BF16)
    ps_t = p.ps("ps_t", [128, 128], BF16)

    def quant_and_transpose(src_ap, nrows, ncols, sc_out, t_out):
        for rc in range(nrows // 128):
            c = p.dma("sync", wrow[:, 0:ncols], src_ap[rc * 128:(rc + 1) * 128, :], "ld")
            p.wait("vector", "ld", c)
            p.wait("vector", "st")
            p.opss("vector", lambda E: E.reduce_sum(stat[:], wrow[:, 0:ncols], axis=AX.X,
                                                    apply_absolute_value=True))
            p.opss("vector", lambda E, n=ncols: E.tensor_scalar(
                stat2[:], stat[:], 1.0 / n, 1e-5, op0=ALU.mult, op1=ALU.max))
            p.opss("vector", lambda E: E.reciprocal(rinv[:], stat2[:]))
            p.op("vector", lambda E, n=ncols: E.tensor_scalar(
                ternf[:, 0:n], wrow[:, 0:n], rinv[:], None, op0=ALU.mult))
            p.op("vector", lambda E, n=ncols: E.tensor_scalar(
                tsgn[:, 0:n], ternf[:, 0:n], 0.0, 2.0, op0=ALU.is_ge, op1=ALU.mult))
            p.op("vector", lambda E, n=ncols: E.tensor_scalar(
                tsgn[:, 0:n], tsgn[:, 0:n], -1.0, None, op0=ALU.add))
            p.op("vector", lambda E, n=ncols: E.tensor_mul(
                tabs[:, 0:n], ternf[:, 0:n], tsgn[:, 0:n]))
            p.op("vector", lambda E, n=ncols: E.tensor_scalar(
                tabs[:, 0:n], tabs[:, 0:n], 0.5, None, op0=ALU.add))
            p.op("vector", lambda E, n=ncols: E.tensor_copy(terni[:, 0:n], tabs[:, 0:n]))
            p.op("vector", lambda E, n=ncols: E.tensor_copy(ternf[:, 0:n], terni[:, 0:n]))
            p.op("vector", lambda E, n=ncols: E.tensor_sub(tabs[:, 0:n], ternf[:, 0:n],
                                                           tabs[:, 0:n]))
            p.op("vector", lambda E, n=ncols: E.tensor_scalar(
                tabs[:, 0:n], tabs[:, 0:n], 0.5, None, op0=ALU.add))
            p.op("vector", lambda E, n=ncols: E.tensor_scalar(
                tabs[:, 0:n], tabs[:, 0:n], 0.5, None, op0=ALU.is_gt))
            p.op("vector", lambda E, n=ncols: E.tensor_sub(ternf[:, 0:n], ternf[:, 0:n],
                                                           tabs[:, 0:n]))
            p.op("vector", lambda E, n=ncols: E.tensor_scalar(
                ternf[:, 0:n], ternf[:, 0:n], 1.0, None, op0=ALU.min))
            p.op("vector", lambda E, n=ncols: E.tensor_tensor(
                ternb[:, 0:n], ternf[:, 0:n], tsgn[:, 0:n], ALU.mult), "vp")
            p.wait("sync", "vp")
            p.dma("sync", AP(sc_out.tensor, rc * 128, [[1, 128]]), stat2[:], "st")
            p.wait("tensor", "vp")
            for cc in range(ncols // 128):
                pc = p.op("tensor", lambda E, c0=cc * 128: E.transpose(
                    ps_t[:], ternb[:, c0:c0 + 128], iden[:]), "pp")
                p.wait("vector", "pp", pc)
                vc = p.op("vector", lambda E: E.tensor_copy(evb[:], ps_t[:]), "vp")
                p.wait("sync", "vp", vc)
                p.dma("sync", t_out[cc * 128:(cc + 1) * 128, rc * 128:(rc + 1) * 128],
                      evb[:], "st")
                p.wait("tensor", "vp", vc)
                p.wait("vector", "st")

    quant_and_transpose(w_fc, 4096, 1024, scfc, tfcT)
    quant_and_transpose(w_proj, 1024, 4096, scpr, tprT)

    def plain_transpose(src_ap, nrows, ncols, t_out):
        rch = _ceil(nrows, 128)
        cch = _ceil(ncols, 128)
        for rc in range(rch):
            r0, r1 = rc * 128, min((rc + 1) * 128, nrows)
            n = r1 - r0
            c = p.dma("sync", wrow[0:n, 0:ncols], src_ap[r0:r1, :], "ld")
            p.wait("vector", "ld", c)
            p.wait("vector", "st")
            p.op("vector", lambda E, n_=n, nc_=ncols: E.tensor_copy(
                ternb[0:n_, 0:nc_], wrow[0:n_, 0:nc_]), "vp")
            p.wait("tensor", "vp")
            for cc in range(cch):
                c0, c1 = cc * 128, min((cc + 1) * 128, ncols)
                m = c1 - c0
                pc = p.op("tensor", lambda E, a=c0, m_=m, n_=n: E.transpose(
                    ps_t[0:m_, 0:n_], ternb[0:n_, a:a + m_], iden[0:n_, 0:n_]), "pp")
                p.wait("vector", "pp", pc)
                vc = p.op("vector", lambda E, m_=m, n_=n: E.tensor_copy(
                    evb[0:m_, 0:n_], ps_t[0:m_, 0:n_]), "vp")
                p.wait("sync", "vp", vc)
                p.dma("sync", t_out[c0:c1, r0:r1], evb[0:m, 0:n], "st", slow=(n < 8))
                p.wait("tensor", "vp", vc)
                p.wait("vector", "st")

    plain_transpose(W_KQV, 3072, 1024, wkqvT)
    plain_transpose(W_O, 1024, 1024, woT)
    plain_transpose(u, 4096, 16, uT)
    plain_transpose(v, 16, 1024, vT)
    plain_transpose(rel, 129, 128, relT)

    fb_s = p.sb("fb_s", [16, 16])
    g_s = p.sb("g_s", [1, 1])
    onec = p.sb("onec", [1, 16])
    gb_ps = p.ps("gb_ps", [16, 1])
    fbt_s = p.sb("fbt_s", [16, 16])
    gbc = p.sb("gbc", [16, 1])
    fbg = p.sb("fbg", [16, 16], BF16)
    p.dma("sync", fb_s[:], fast_b, "ld")
    c = p.dma("sync", g_s[:], gate, "ld")
    p.wait("vector", "ld", c)
    p.wait("vector", "st")
    p.op("vector", lambda E: E.memset(onec[:], 1.0))
    p.op("vector", lambda E: E.tensor_copy(ternb[0:16, 0:16], fb_s[:]), "vp")
    p.wait("tensor", "vp")
    p.op("tensor", lambda E: E.transpose(ps_t[0:16, 0:16], ternb[0:16, 0:16],
                                         iden[0:16, 0:16]), "pp")
    pc = p.op("tensor", lambda E: E.matmul(gb_ps[:], onec[:], g_s[:],
                                           start=True, stop=True), "pp")
    p.wait("vector", "pp", pc)
    p.opss("vector", lambda E: E.tensor_copy(fbt_s[:], ps_t[0:16, 0:16]))
    p.opss("vector", lambda E: E.tensor_copy(gbc[:], gb_ps[:]))
    p.op("vector", lambda E: E.tensor_scalar(fbg[:], fbt_s[:], gbc[:], None,
                                             op0=ALU.mult), "vp")
    p.wait("sync", "vp")
    p.dma("sync", fbTg, fbg[:], "st")
    p.wait("sync", "st")
    return p.build()


def mk_ln_consts(p, pref="", pool=None):
    c = {}
    c["ones_col"] = p.sb(pref + "ones_col", [128, 1], BF16)
    c["ones_row1"] = p.sb(pref + "ones_row1", [1, 128])
    if pool is None:
        c["ps_s1"] = p.ps(pref + "ps_s1", [1, 512])
        c["ps_s2"] = p.ps(pref + "ps_s2", [1, 512])
        c["ps_mb"] = p.ps(pref + "ps_mb", [128, 512])
        c["ps_rb"] = p.ps(pref + "ps_rb", [128, 512])
    else:
        c["ps_s1"] = pool[0][0:1, :]
        c["ps_s2"] = pool[1][0:1, :]
        c["ps_mb"] = pool[2]
        c["ps_rb"] = pool[3]
    c["m_"] = p.sb(pref + "m_", [1, 512])
    c["var_"] = p.sb(pref + "var_", [1, 512])
    c["sq_"] = p.sb(pref + "sq_", [1, 512])
    c["rstd_"] = p.sb(pref + "rstd_", [1, 512])
    c["x2b"] = p.sb(pref + "x2b", [128, 512], BF16)
    c["xb"] = p.sb(pref + "xb", [128, 512], BF16)
    c["tf"] = p.sb(pref + "tf", [128, 512])
    c["c2_"] = p.sb(pref + "c2_", [1, 512])
    c["eps_"] = p.sb(pref + "eps_", [1, 1])
    p.op("vector", lambda E: E.memset(c["eps_"][:], 1e-5))
    p.op("vector", lambda E: E.memset(c["ones_col"][:], 1.0))
    p.op("vector", lambda E: E.memset(c["ones_row1"][:], 1.0), "vp")
    return c


def emit_ln(p, xs, xnb, T, c, affine=None):
    ones_col, onesb = c["ones_col"], c["ones_row1"]
    ps_s1, ps_s2, ps_mb, ps_rb = c["ps_s1"], c["ps_s2"], c["ps_mb"], c["ps_rb"]
    m_, var_, sq_, rstd_, c2_ = c["m_"], c["var_"], c["sq_"], c["rstd_"], c["c2_"]
    x2b, xb, tf = c["x2b"], c["xb"], c["tf"]
    for ci in range(T // 512):
        sl = slice(ci * 512, ci * 512 + 512)
        for d in range(NDC):
            pp_before = p.cnt["pp"]
            if d > 0:
                p.wait("scalar", "pp", pp_before)
                p.wait("vector", "pp", pp_before)
            sc = p.op("scalar", lambda E, dd=d, s_=sl: E.activation(
                x2b[:], xs[dd][:, s_], AF.Square), "sp")
            vc = p.op("vector", lambda E, dd=d, s_=sl: E.tensor_copy(
                xb[:], xs[dd][:, s_]), "vp")
            p.wait("tensor", "sp", sc)
            p.wait("tensor", "vp", vc)
            p.op("tensor", lambda E, dd=d: E.matmul(ps_s1[:], ones_col[:], xb[:],
                                                    start=(dd == 0), stop=(dd == 7)))
            pc = p.op("tensor", lambda E, dd=d: E.matmul(ps_s2[:], ones_col[:], x2b[:],
                                                         start=(dd == 0), stop=(dd == 7)), "pp")
        p.wait("vector", "pp", pc)
        p.opss("vector", lambda E: E.tensor_scalar(m_[:], ps_s1[:], 1.0 / D, None, op0=ALU.mult))
        p.opss("vector", lambda E: E.tensor_tensor(sq_[:], m_[:], m_[:], ALU.mult))
        p.op("vector", lambda E: E.scalar_tensor_tensor(var_[:], ps_s2[:], 1.0 / D, sq_[:],
                                                        op0=ALU.mult, op1=ALU.subtract), "vp")
        p.wait("scalar", "vp")
        sc = p.op("scalar", lambda E: E.activation(rstd_[:], var_[:], AF.Sqrt,
                                                   bias=c["eps_"][:]), "sp")
        p.wait("vector", "sp", sc)
        vc = p.opss("vector", lambda E: E.reciprocal(c2_[:], rstd_[:]))
        vc = p.op("vector", lambda E: E.tensor_copy(rstd_[:], c2_[:]), "vp")
        p.wait("tensor", "vp", vc)
        p.op("tensor", lambda E: E.matmul(ps_mb[:], onesb[:], m_[:], start=True, stop=True))
        pc = p.op("tensor", lambda E: E.matmul(ps_rb[:], onesb[:], rstd_[:],
                                               start=True, stop=True), "pp")
        p.wait("vector", "pp", pc)
        for d in range(NDC):
            p.op("vector", lambda E, dd=d, s_=sl: E.tensor_sub(tf[:], xs[dd][:, s_], ps_mb[:]))
            if affine is None:
                vc = p.op("vector", lambda E, dd=d, s_=sl: E.tensor_mul(
                    xnb[dd][:, s_], tf[:], ps_rb[:]), "vp")
            else:
                w_t, b_t = affine
                p.op("vector", lambda E: E.tensor_mul(tf[:], tf[:], ps_rb[:]))
                vc = p.op("vector", lambda E, dd=d, s_=sl: E.tensor_scalar(
                    xnb[dd][:, s_], tf[:], w_t[:, dd:dd + 1], b_t[:, dd:dd + 1],
                    op0=ALU.mult, op1=ALU.add), "vp")
        p.wait("tensor", "vp", vc)


def build_ffn0(with_comb=False):
    p = P("ffn0c" if with_comb else "ffn0")
    x_in = p.din("x", [D, TOK])
    if with_comb:
        parts_d = [p.din(f"part{i}", [D, TOK]) for i in range(4)]
        bias_d = p.din("bias", [D])
    tfcT = p.din("tfcT", [1024, 4096], BF16)
    scfc = p.din("scfc", [4096])
    tprT = p.din("tprT", [4096, 1024], BF16)
    scpr = p.din("scpr", [1024])
    uTd = p.din("uT", [16, 4096], BF16)
    vTd = p.din("vT", [1024, 16], BF16)
    fbTgd = p.din("fbTg", [16, 16], BF16)
    xout_d = p.dout("xout", [D, TOK])

    for s in ("ld", "vp", "sp", "pp", "st", "gp", "wfc", "wpr"):
        p.sem(s)

    xs = [p.sb(f"x{d}", [128, TOK]) for d in range(NDC)]
    xnb = [p.sb(f"xn{d}", [128, TOK], BF16) for d in range(NDC)]
    consts = mk_ln_consts(p)
    p.sem("pp2")
    for d in range(NDC):
        p.dma("sync", xs[d][:], x_in[d * 128:(d + 1) * 128, :], "ld")
    if with_comb:
        cta = p.sb("cmb_a", [128, TOK])
        ctb = p.sb("cmb_b", [128, TOK])
        cbc = p.sb("cmb_bc", [128, 1])
        for d in range(NDC):
            dsl = slice(d * 128, (d + 1) * 128)
            if d > 0:
                p.wait("sync", "vp")
            p.dma("sync", cta[:], parts_d[0][dsl, :], "ld")
            p.dma("sync", ctb[:], parts_d[1][dsl, :], "ld")
            c = p.dma("sync", cbc[:], AP(bias_d.tensor, d * 128, [[1, 128]]), "ld")
            p.wait("vector", "ld", c)
            p.op("vector", lambda E, dd=d: E.tensor_add(xs[dd][:], xs[dd][:], cta[:]))
            p.op("vector", lambda E, dd=d: E.tensor_add(xs[dd][:], xs[dd][:], ctb[:]),
                 "vp")
            p.wait("sync", "vp")
            c = p.dma("sync", cta[:], parts_d[2][dsl, :], "ld")
            p.dma("sync", ctb[:], parts_d[3][dsl, :], "ld")
            p.wait("vector", "ld")
            p.op("vector", lambda E, dd=d: E.tensor_add(xs[dd][:], xs[dd][:], cta[:]))
            p.op("vector", lambda E, dd=d: E.tensor_add(xs[dd][:], xs[dd][:], ctb[:]))
            p.op("vector", lambda E, dd=d: E.tensor_scalar(
                xs[dd][:], xs[dd][:], cbc[:], None, op0=ALU.add), "vp")
        p.wait("scalar", "vp", p.cnt["vp"])
    p.wait("vector", "ld")
    p.wait("scalar", "ld")
    emit_ln(p, xs, xnb, TOK, consts)

    vT_s = p.sb("vT_s", [128, 16 * NDC], BF16)
    fb_s = p.sb("fb_s", [16, 16], BF16)
    uT_s = p.sb("uT_s", [16, 4096], BF16)
    t1 = p.sb("t1", [16, TOK], BF16)
    t2 = p.sb("t2", [16, TOK], BF16)
    ps_r = p.ps("ps_r", [16, 512])
    for d in range(NDC):
        p.dma("sync", vT_s[:, d * 16:(d + 1) * 16], vTd[d * 128:(d + 1) * 128, :], "ld")
    p.dma("sync", fb_s[:], fbTgd, "ld")
    c = p.dma("sync", uT_s[:], uTd, "ld")
    p.wait("tensor", "ld", c)
    for ci in range(TOK // 512):
        sl = slice(ci * 512, ci * 512 + 512)
        for d in range(NDC):
            pc = p.op("tensor", lambda E, dd=d, s_=sl: E.matmul(
                ps_r[:], vT_s[:, dd * 16:(dd + 1) * 16], xnb[dd][:, s_],
                start=(dd == 0), stop=(dd == 7)), "pp")
        p.wait("vector", "pp", pc)
        vc = p.op("vector", lambda E, s_=sl: E.tensor_copy(t1[:, s_], ps_r[:]), "vp")
        p.wait("tensor", "vp", vc)
        pc = p.op("tensor", lambda E, s_=sl: E.matmul(ps_r[:], fb_s[:], t1[:, s_],
                                                      start=True, stop=True), "pp")
        p.wait("vector", "pp", pc)
        vc = p.op("vector", lambda E, s_=sl: E.tensor_copy(t2[:, s_], ps_r[:]), "vp")
        p.wait("tensor", "vp", vc)

    fcbuf = [p.sb(f"fcb{i}", [128, 1024], BF16) for i in range(2)]
    scf_s = p.sb("scf_s", [128, 2])
    prbuf = [p.sb(f"prb{i}", [128, 4096], BF16) for i in range(2)]
    rtmp = p.sb("rtmp", [128, 512], BF16)
    r2 = p.sb("r2", [128, 512 * 32], BF16)
    xo = p.sb("xo", [128, 512])
    sch = p.sb("sch", [128, NDC])
    ps_h = [p.ps("ps_h0", [128, 512])]
    ps_o = [p.ps("ps_o0", [128, 512])]

    for d in range(NDC):
        c = p.dma("sync", sch[:, d:d + 1], AP(scpr.tensor, d * 128, [[1, 128]]), "ld")
    p.wait("vector", "ld", c)
    p.op("vector", lambda E: E.tensor_scalar(sch[:], sch[:], 0.5, None, op0=ALU.mult), "vp")

    for ci in range(TOK // 512):
        sl = slice(ci * 512, ci * 512 + 512)
        for hc in range(32):
            h0 = hc * 128
            buf = fcbuf[hc % 2]
            pb = ps_h[0]
            if ci > 0 or hc >= 2:
                p.wait("sync", "wfc", p.cnt["wfc"])
            for d in range(NDC):
                p.dma("sync", buf[:, d * 128:(d + 1) * 128],
                      tfcT[d * 128:(d + 1) * 128, h0:h0 + 128], "ld")
            c = p.dma("sync", scf_s[:, hc % 2:hc % 2 + 1],
                      AP(scfc.tensor, h0, [[1, 128]]), "ld")
            p.wait("tensor", "ld", c)
            for d in range(NDC):
                p.op("tensor", lambda E, dd=d, b=buf, s_=sl, pb_=pb: E.matmul(
                    pb_[:], b[:, dd * 128:(dd + 1) * 128], xnb[dd][:, s_],
                    start=(dd == 0), stop=False))
            pc = p.op("tensor", lambda E, h_=h0, s_=sl, pb_=pb: E.matmul(
                pb_[:], uT_s[:, h_:h_ + 128], t2[:, s_], start=False, stop=True), "pp")
            p.op("tensor", lambda E: E.nop(), "wfc")
            p.wait("scalar", "pp", pc)
            sc = p.op("scalar", lambda E, k=hc % 2, pb_=pb: E.activation(
                rtmp[:], pb_[:], AF.Relu, scale=scf_s[:, k:k + 1]), "sp")
            p.wait("vector", "sp", sc)
            vc = p.op("vector", lambda E, k=hc: E.tensor_mul(
                r2[:, k * 512:(k + 1) * 512], rtmp[:], rtmp[:]), "vp")
            p.wait("tensor", "vp", vc)
            p.wait("tensor", "sp", sc)
        for d in range(NDC):
            bufp = prbuf[d % 2]
            if ci > 0 or d >= 2:
                p.wait("sync", "wpr", p.cnt["wpr"])
            for hc in range(32):
                c = p.dma("sync", bufp[:, hc * 128:(hc + 1) * 128],
                          tprT[hc * 128:(hc + 1) * 128, d * 128:(d + 1) * 128], "ld")
            p.wait("tensor", "ld", c)
            po = ps_o[0]
            for hc in range(32):
                pc = p.op("tensor", lambda E, k=hc, b=bufp, po_=po: E.matmul(
                    po_[:], b[:, k * 128:(k + 1) * 128], r2[:, k * 512:(k + 1) * 512],
                    start=(k == 0), stop=(k == 31)), "pp" if hc == 31 else None)
            p.op("tensor", lambda E: E.nop(), "wpr")
            p.wait("vector", "pp", pc)
            vc = p.op("vector", lambda E, dd=d, s_=sl, po_=po: E.scalar_tensor_tensor(
                xo[:], po_[:], sch[:, dd:dd + 1], xs[dd][:, s_],
                op0=ALU.mult, op1=ALU.add), "vp")
            p.wait("sync", "vp", vc)
            p.dma("sync", xout_d[d * 128:(d + 1) * 128, sl], xo[:], "st")
            p.wait("tensor", "vp", vc)
            p.wait("vector", "st")
    p.wait("sync", "st")
    return p.build()


def build_comb():
    p = P("comb")
    x_in = p.din("x", [D, TOK])
    parts = [p.din(f"part{i}", [D, TOK]) for i in range(4)]
    bias = p.din("bias", [D])
    xout_d = p.dout("xout", [D, TOK])
    for s in ("ld", "vp", "st"):
        p.sem(s)
    a = p.sb("a", [128, TOK])
    b = p.sb("b", [128, TOK])
    t0 = p.sb("t0", [128, TOK])
    t1 = p.sb("t1", [128, TOK])
    bcol = p.sb("bcol", [128, 1])
    for d in range(NDC):
        dsl = slice(d * 128, (d + 1) * 128)
        if d > 0:
            p.wait("sync", "vp")
        c0 = p.dma("sync", a[:], x_in[dsl, :], "ld")
        p.dma("sync", t0[:], parts[0][dsl, :], "ld")
        p.dma("sync", t1[:], parts[1][dsl, :], "ld")
        c = p.dma("sync", bcol[:], AP(bias.tensor, d * 128, [[1, 128]]), "ld")
        p.wait("vector", "ld", c)
        p.op("vector", lambda E: E.tensor_add(b[:], t0[:], t1[:]), "vp")
        p.wait("sync", "vp")
        c = p.dma("sync", t0[:], parts[2][dsl, :], "ld")
        p.dma("sync", t1[:], parts[3][dsl, :], "ld")
        p.wait("vector", "ld")
        p.op("vector", lambda E: E.tensor_add(t0[:], b[:], t0[:]))
        p.op("vector", lambda E: E.tensor_add(t1[:], t0[:], t1[:]))
        p.op("vector", lambda E: E.tensor_add(a[:], t1[:], a[:]))
        vc = p.op("vector", lambda E: E.tensor_scalar(b[:], a[:], bcol[:], None,
                                                      op0=ALU.add), "vp")
        p.wait("sync", "vp", vc)
        p.dma("sync", xout_d[dsl, :], b[:], "st")
        p.wait("vector", "st")
    p.wait("sync", "st")
    return p.build()


def emit_ln_chunk(p, xin_aps, xout_aps, c, affine=None, sname="sp", vname="vp", pname="pp"):
    ones_col, onesb = c["ones_col"], c["ones_row1"]
    ps_s1, ps_s2, ps_mb, ps_rb = c["ps_s1"], c["ps_s2"], c["ps_mb"], c["ps_rb"]
    m_, var_, sq_, rstd_, c2_ = c["m_"], c["var_"], c["sq_"], c["rstd_"], c["c2_"]
    x2b, xb, tf = c["x2b"], c["xb"], c["tf"]
    W = xin_aps[0].shape[1]
    p.wait("scalar", "vp", p.cnt["vp"])
    for d in range(NDC):
        pp_before = p.cnt[pname]
        if d > 0:
            p.wait("scalar", pname, pp_before)
            p.wait("vector", pname, pp_before)
        sc = p.op("scalar", lambda E, dd=d: E.activation(
            x2b[:, 0:W], xin_aps[dd], AF.Square), sname)
        vc = p.op("vector", lambda E, dd=d: E.tensor_copy(xb[:, 0:W], xin_aps[dd]), vname)
        p.wait("tensor", sname, sc)
        p.wait("tensor", vname, vc)
        p.op("tensor", lambda E, dd=d: E.matmul(ps_s1[0:1, 0:W], ones_col[:], xb[:, 0:W],
                                                start=(dd == 0), stop=(dd == 7)))
        pc = p.op("tensor", lambda E, dd=d: E.matmul(ps_s2[0:1, 0:W], ones_col[:],
                                                     x2b[:, 0:W],
                                                     start=(dd == 0), stop=(dd == 7)), pname)
    p.wait("vector", pname, pc)
    p.opss("vector", lambda E: E.tensor_scalar(m_[:, 0:W], ps_s1[0:1, 0:W], 1.0 / D,
                                               None, op0=ALU.mult))
    p.opss("vector", lambda E: E.tensor_tensor(sq_[:, 0:W], m_[:, 0:W], m_[:, 0:W],
                                               ALU.mult))
    p.op("vector", lambda E: E.scalar_tensor_tensor(var_[:, 0:W], ps_s2[0:1, 0:W],
                                                    1.0 / D, sq_[:, 0:W],
                                                    op0=ALU.mult, op1=ALU.subtract), vname)
    p.wait("scalar", vname)
    sc = p.op("scalar", lambda E: E.activation(rstd_[:, 0:W], var_[:, 0:W], AF.Sqrt,
                                               bias=c["eps_"][:]), sname)
    p.wait("vector", sname, sc)
    p.opss("vector", lambda E: E.reciprocal(c2_[:, 0:W], rstd_[:, 0:W]))
    vc = p.op("vector", lambda E: E.tensor_copy(rstd_[:, 0:W], c2_[:, 0:W]), vname)
    p.wait("tensor", vname, vc)
    p.op("tensor", lambda E: E.matmul(ps_mb[:, 0:W], onesb[:], m_[:, 0:W],
                                      start=True, stop=True))
    pc = p.op("tensor", lambda E: E.matmul(ps_rb[:, 0:W], onesb[:], rstd_[:, 0:W],
                                           start=True, stop=True), pname)
    p.wait("vector", pname, pc)
    for d in range(NDC):
        p.op("vector", lambda E, dd=d: E.tensor_sub(tf[:, 0:W], xin_aps[dd],
                                                    ps_mb[:, 0:W]))
        if affine is None:
            vc = p.op("vector", lambda E, dd=d: E.tensor_mul(
                xout_aps[dd], tf[:, 0:W], ps_rb[:, 0:W]), vname)
        else:
            w_t, b_t = affine
            p.op("vector", lambda E: E.tensor_mul(tf[:, 0:W], tf[:, 0:W], ps_rb[:, 0:W]))
            vc = p.op("vector", lambda E, dd=d: E.tensor_scalar(
                xout_aps[dd], tf[:, 0:W], w_t[:, dd:dd + 1], b_t[:, dd:dd + 1],
                op0=ALU.mult, op1=ALU.add), vname)
    p.wait("tensor", vname, vc)
    return


DEBUG_L1 = False
L1_PHASES = 2
L1_NOMASK = False
L1_HEADSET = (0, 1, 2, 3)
L1_MAXQC = 16

def build_l1attn():
    p = P("l1attn")
    x_in = p.din("x", [D, S])
    if DEBUG_L1:
        dbg_p = p.dout("dbg_p", [128, 512 * 4])
        dbg_s = p.dout("dbg_s", [128, 512 * 4])
        dbg_y = p.dout("dbg_y", [65, 512])
    wq_d = p.din("wq", [1024, 256], BF16)
    wk_d = p.din("wk", [1024, 256], BF16)
    wv_d = p.din("wv", [1024, 256], BF16)
    wo_d = p.din("wo", [256, 1024], BF16)
    tc_d = p.din("tc", [1, 8])
    part_d = p.dout("part", [D, S])

    for s in ("ld", "vp", "sp", "pp", "gp", "st"):
        p.sem(s)

    consts = mk_ln_consts(p)
    ps4 = p.ps("ps4", [128, 512])
    ps5 = p.ps("ps5", [128, 512])
    ps6 = p.ps("ps6", [128, 512])
    ps7 = p.ps("ps7", [128, 512])

    xin = [p.sb("xin0", [128, 4096])]
    xnb = p.sb("xnb", [128, 4096], BF16)
    qfin = [p.sb(f"qfin{i}", [128, S], BF16) for i in range(2)]
    kfin = [p.sb(f"kfin{i}", [128, S], BF16) for i in range(2)]
    vT = p.sb("vT", [128, 65 * 256], BF16)
    wq_s = p.sb("wq_s", [128, 256 * 8], BF16)
    wk_s = p.sb("wk_s", [128, 256 * 8], BF16)
    wv_s = p.sb("wv_s", [128, 256 * 8], BF16)
    wo_s = [p.sb(f"wo_s{i}", [128, 1024], BF16) for i in range(2)]
    iden = p.sb("iden", [128, 128], BF16)
    emit_identity(p, iden)
    idenf = p.sb("idenf", [128, 128])
    p.op("vector", lambda E: E.tensor_copy(idenf[:], iden[:]), "vp")

    cosT = p.sb("cosT", [128, S], BF16)
    sinT = p.sb("sinT", [128, S], BF16)
    ic = p.sb("ic", [128, 1])
    l32 = p.sb("l32", [128, 1])
    invf = p.sb("invf", [128, 1])
    tc_s = p.sb("tc_s", [1, 8])
    tccol = p.sb("tccol", [128, 8])
    onesb1 = p.sb("onesb1", [1, 128])
    t1r = p.sb("t1r", [128, 512])
    t2r = p.sb("t2r", [128, 512])
    vfm = p.sb("vfm", [128, 512])
    pout = p.sb("pout", [128, 512])
    tmpa = t1r
    tmpb = t2r
    tmpi = p.sb("tmpi", [128, 512], I32)
    tmpn = vfm
    frq = pout
    t10k = p.sb("t10k", [1, 1])

    c = p.dma("sync", tc_s[:], tc_d, "ld")
    for d in range(NDC):
        p.dma("sync", wq_s[:, d * 256:(d + 1) * 256], wq_d[d * 128:(d + 1) * 128, :], "ld")
        p.dma("sync", wk_s[:, d * 256:(d + 1) * 256], wk_d[d * 128:(d + 1) * 128, :], "ld")
        p.dma("sync", wv_s[:, d * 256:(d + 1) * 256], wv_d[d * 128:(d + 1) * 128, :], "ld")
    for i in range(2):
        c = p.dma("sync", wo_s[i][:], wo_d[i * 128:(i + 1) * 128, :], "ld")
    p.wait("vector", "ld", c)
    p.wait("scalar", "ld", c)
    p.wait("gpsimd", "ld", c)
    for d in range(NDC):
        p.op("vector", lambda E, dd=d: E.tensor_scalar(
            wq_s[:, dd * 256:(dd + 1) * 256], wq_s[:, dd * 256:(dd + 1) * 256],
            0.125, None, op0=ALU.mult), "vp")
    p.op("gpsimd", lambda E: E.iota(ic[0:32, :], pattern=[[1, 1]], base=0,
                                    channel_multiplier=1,
                                    allow_small_or_imprecise_dtypes=True), "gp")
    p.wait("vector", "gp")
    for r0 in (32, 64, 96):
        p.op("vector", lambda E, r=r0: E.tensor_copy(ic[r:r + 32, :], ic[0:32, :]), "vp")
    p.op("vector", lambda E: E.memset(t10k[:], 10000.0), "vp")
    p.op("vector", lambda E: E.memset(onesb1[:], 1.0), "vp")
    p.wait("scalar", "vp")
    sc = p.op("scalar", lambda E: E.activation(t10k[:], t10k[:], AF.Ln), "sp")
    p.wait("tensor", "sp", sc)
    pc = p.op("tensor", lambda E: E.matmul(ps4[:, 0:1], onesb1[:], t10k[:],
                                           start=True, stop=True))
    pc = p.op("tensor", lambda E: E.matmul(ps5[:, 0:8], onesb1[:], tc_s[:],
                                           start=True, stop=True), "pp")
    p.wait("vector", "pp", pc)
    p.opss("vector", lambda E: E.tensor_scalar(l32[:], ps4[:, 0:1], -1.0 / 32.0, None,
                                               op0=ALU.mult))
    p.opss("vector", lambda E: E.tensor_copy(tccol[:], ps5[:, 0:8]))
    vc = p.op("vector", lambda E: E.tensor_scalar(invf[:], ic[:], l32[:], None,
                                                  op0=ALU.mult), "vp")
    p.wait("scalar", "vp", vc)
    sc = p.op("scalar", lambda E: E.activation(invf[:], invf[:], AF.Exp), "sp")
    p.wait("vector", "sp", sc)

    def emit_trig(dst, phase_col):
        for ci2 in range(S // 512):
            if True:
                p.wait("gpsimd", "vp", p.cnt["vp"])
            gc = p.op("gpsimd", lambda E, b=ci2 * 512: E.iota(
                tmpa[:], pattern=[[1, 512]], base=b, channel_multiplier=0,
                allow_small_or_imprecise_dtypes=True), "gp")
            p.wait("vector", "gp", gc)
            p.op("vector", lambda E: E.tensor_scalar(frq[:], tmpa[:], invf[:], None,
                                                     op0=ALU.mult), "vp")
            p.wait("vector", "_ss_vector", p.cnt["_ss_vector"]) if False else None
            if phase_col is not None:
                p.opss("vector", lambda E: E.tensor_scalar(
                    frq[:], frq[:], tccol[:, 4:5], None, op0=ALU.add))
            p.opss("vector", lambda E: E.tensor_scalar(tmpb[:], frq[:], tccol[:, 3:4],
                                                       None, op0=ALU.mult))
            p.opss("vector", lambda E: E.tensor_copy(tmpi[:], tmpb[:]))
            p.opss("vector", lambda E: E.tensor_copy(tmpn[:], tmpi[:]))
            for k in range(3):
                p.opss("vector", lambda E, kk=k: E.tensor_scalar(
                    tmpb[:], tmpn[:], tccol[:, kk:kk + 1], None, op0=ALU.mult))
                p.opss("vector", lambda E: E.tensor_sub(frq[:], frq[:], tmpb[:]))
            p.opss("vector", lambda E: E.tensor_scalar(tmpb[:], frq[:], tccol[:, 6:7],
                                                       tccol[:, 5:6], op0=ALU.is_gt,
                                                       op1=ALU.mult))
            p.opss("vector", lambda E: E.tensor_sub(frq[:], frq[:], tmpb[:]))
            p.opss("vector", lambda E: E.tensor_scalar(tmpn[:], frq[:], -1.0, None,
                                                       op0=ALU.mult))
            p.opss("vector", lambda E: E.tensor_scalar(tmpb[:], tmpn[:], tccol[:, 6:7],
                                                       tccol[:, 5:6], op0=ALU.is_gt,
                                                       op1=ALU.mult))
            p.opss("vector", lambda E: E.tensor_add(frq[:], frq[:], tmpb[:]))
            vc2 = p.op("vector", lambda E: E.nop(), "vp")
            p.wait("scalar", "vp", vc2)
            sc2 = p.op("scalar", lambda E, cc=ci2: E.activation(
                dst[:, cc * 512:(cc + 1) * 512], frq[:], AF.Sin), "sp")
            p.wait("vector", "sp", sc2)

    emit_trig(sinT, None)
    emit_trig(cosT, 4)

    p.op("vector", lambda E: E.memset(vT[:], 1.0), "vp")


    def rope_emit(ps_src, dstT, oc, sl512, xc):
        csl = slice(xc * 512, xc * 512 + 512)
        p.op("vector", lambda E: E.tensor_mul(t1r[0:64, :], ps_src[0:64, 0:512],
                                              cosT[0:64, csl]))
        p.op("vector", lambda E: E.tensor_mul(t1r[64:128, :], ps_src[0:64, 0:512],
                                              sinT[0:64, csl]))
        p.op("vector", lambda E: E.tensor_mul(t2r[0:64, :], ps_src[64:128, 0:512],
                                              sinT[64:128, csl]))
        vc = p.op("vector", lambda E: E.tensor_mul(t2r[64:128, :], ps_src[64:128, 0:512],
                                                   cosT[64:128, csl]), "vp")
        p.wait("tensor", "vp", vc)
        p.op("vector", lambda E: E.tensor_sub(dstT[0:32, csl], t1r[0:32, :], t2r[0:32, :]))
        p.op("vector", lambda E: E.tensor_sub(dstT[64:96, csl], t1r[32:64, :], t2r[32:64, :]))
        p.op("vector", lambda E: E.tensor_add(dstT[32:64, csl], t1r[64:96, :], t2r[64:96, :]))
        vc = p.op("vector", lambda E: E.tensor_add(dstT[96:128, csl], t1r[96:128, :],
                                                   t2r[96:128, :]), "vp")
        return vc

    for xc in range(S // 512):
        sl = slice(xc * 512, xc * 512 + 512)
        buf = xin[0]
        if xc >= 1:
            p.wait("sync", "vp", p.cnt["vp"])
            p.wait("sync", "sp", p.cnt["sp"])
        for d in range(NDC):
            c = p.dma("sync", buf[:, d * 512:(d + 1) * 512], x_in[d * 128:(d + 1) * 128, sl],
                      "ld")
        p.wait("vector", "ld", c)
        p.wait("scalar", "ld", c)
        emit_ln_chunk(p, [buf[:, d * 512:(d + 1) * 512] for d in range(NDC)],
                      [xnb[:, d * 512:(d + 1) * 512] for d in range(NDC)], consts)
        for oc in range(2):
            for d in range(NDC):
                pc = p.op("tensor", lambda E, dd=d, o=oc: E.matmul(
                    ps4[:, 0:512], wq_s[:, dd * 256 + o * 128: dd * 256 + o * 128 + 128],
                    xnb[:, dd * 512:(dd + 1) * 512],
                    start=(dd == 0), stop=(dd == 7)), "pp")
            p.wait("vector", "pp", pc)
            rope_emit(ps4, qfin[oc], oc, 512, xc)
            for d in range(NDC):
                pc = p.op("tensor", lambda E, dd=d, o=oc: E.matmul(
                    ps5[:, 0:512], wk_s[:, dd * 256 + o * 128: dd * 256 + o * 128 + 128],
                    xnb[:, dd * 512:(dd + 1) * 512],
                    start=(dd == 0), stop=(dd == 7)), "pp")
            p.wait("vector", "pp", pc)
            rope_emit(ps5, kfin[oc], oc, 512, xc)
            for d in range(NDC):
                pc = p.op("tensor", lambda E, dd=d, o=oc: E.matmul(
                    ps4[:, 0:512], wv_s[:, dd * 256 + o * 128: dd * 256 + o * 128 + 128],
                    xnb[:, dd * 512:(dd + 1) * 512],
                    start=(dd == 0), stop=(dd == 7)), "pp")
            p.wait("vector", "pp", pc)
            vc = p.op("vector", lambda E: E.tensor_copy(vfm[:], ps4[:, 0:512]), "vp")
            p.wait("tensor", "vp", vc)
            for kt4 in range(4):
                for hh in range(2):
                    kt = xc * 4 + kt4
                    h = oc * 2 + hh
                    slot = (kt * 4 + h) * 65
                    pc = p.op("tensor", lambda E, a=hh * 64, b=kt4 * 128: E.transpose(
                        ps7[0:128, 0:64], vfm[a:a + 64, b:b + 128],
                        idenf[a:a + 64, a:a + 64]), "pp")
                    p.wait("vector", "pp", pc)
                    vc = p.op("vector", lambda E, s_=slot: E.tensor_copy(
                        vT[:, s_:s_ + 64], ps7[0:128, 0:64]), "vp")
                    p.wait("tensor", "vp", vc)

    if L1_PHASES == 1:
        p.wait("gpsimd", "vp", p.cnt["vp"])
        p.dma("gpsimd", part_d[0:128, :], qfin[0][:, :], "st")
        p.dma("gpsimd", part_d[128:256, :], kfin[0][:, :], "st")
        p.wait("gpsimd", "st")
        return p.build()
    p.wait("tensor", "vp", p.cnt["vp"])
    p.wait("scalar", "vp", p.cnt["vp"])
    p.wait("gpsimd", "vp", p.cnt["vp"])
    csh = p.sb("csh", [128, 1])
    p.op("vector", lambda E: E.memset(csh[:], -12.0), "vp")
    pbuf = [p.sb(f"pbuf{i}", [128, 512], BF16) for i in range(2)]
    pbuf2 = [p.sb(f"pbuf2{i}", [128, 512], BF16) for i in range(2)]
    rz_s = p.sb("rz_s", [1, 512])
    rz2 = p.sb("rz2", [1, 512])
    rzb = p.sb("rzb", [64, 512])
    ych = [p.sb(f"ych{i}", [128, 512], BF16) for i in range(2)]
    onesb64 = p.sb("onesb64", [1, 64])
    p.op("vector", lambda E: E.memset(onesb64[:], 1.0), "vp")

    for qc in range(min(L1_MAXQC, S // 512)):
        for h in L1_HEADSET:
            oc, hh = h // 2, h % 2
            ktmax = 4 * qc + 4
            for kt in range(ktmax):
                pb = ps4 if kt % 2 == 0 else ps5
                rows_h = slice(hh * 64, hh * 64 + 64)
                pc = p.op("tensor", lambda E, o=oc, r=rows_h, k0=kt * 128, q0=qc * 512,
                          pb_=pb: E.matmul(pb_[:, 0:512], kfin[o][r, k0:k0 + 128],
                                           qfin[o][r, q0:q0 + 512],
                                           start=True, stop=True), "pp")
                if DEBUG_L1 and qc == 0 and h == 0:
                    p.wait("vector", "pp", pc)
                    vcd = p.op("vector", lambda E, pb_=pb: E.tensor_copy(
                        t2r[:], pb_[:, 0:512]), "vp")
                    p.wait("sync", "vp", vcd)
                    p.dma("sync", AP(dbg_s.tensor, kt * 512, [[2048, 128], [1, 512]]),
                          t2r[:], "st")
                p.wait("scalar", "pp", pc)
                sc = p.op("scalar", lambda E, k=kt % 2, pb_=pb: E.activation(
                    pbuf[k][:], pb_[:, 0:512], AF.Exp, bias=csh[:]), "sp")
                p.wait("tensor", "sp", sc)
                ptile = pbuf[kt % 2]
                if (not L1_NOMASK) and kt >= 4 * qc:
                    p.wait("gpsimd", "sp", sc)
                    p.wait("gpsimd", "pp", p.cnt["pp"])
                    gc = p.op("gpsimd", lambda E, k=kt % 2, b=qc * 512 - kt * 128:
                              E.affine_select(pbuf2[k][:], pbuf[k][:],
                                              pattern=[[1, 512]], compare_op=ALU.is_ge,
                                              fill=0.0, base=b, channel_multiplier=-1),
                              "gp")
                    p.wait("tensor", "gp", gc)
                    ptile = pbuf2[kt % 2]
                if DEBUG_L1 and qc == 0 and h == 0:
                    p.wait("vector", "sp", p.cnt["sp"])
                    p.wait("vector", "gp", p.cnt["gp"])
                    vcd = p.op("vector", lambda E, pt=ptile: E.tensor_copy(
                        t1r[:], pt[:]), "vp")
                    p.wait("sync", "vp", vcd)
                    p.dma("sync", AP(dbg_p.tensor, kt * 512, [[2048, 128], [1, 512]]),
                          t1r[:], "st")
                    p.wait("tensor", "vp", vcd)
                pc = p.op("tensor", lambda E, kt_=kt, h_=h, pt=ptile, km=ktmax: E.matmul(
                    ps6[0:65, 0:512], vT[:, (kt_ * 4 + h_) * 65:(kt_ * 4 + h_) * 65 + 65],
                    pt[:], start=(kt_ == 0), stop=(kt_ == km - 1)), "pp")
            p.wait("vector", "pp", pc)
            if DEBUG_L1 and qc == 0 and h == 0:
                vcd = p.op("vector", lambda E: E.tensor_copy(t2r[0:65, :], ps6[0:65, 0:512]),
                           "vp")
                p.wait("sync", "vp", vcd)
                p.dma("sync", dbg_y, t2r[0:65, :], "st")
            p.opss("vector", lambda E: E.reciprocal(rz_s[:], ps6[64:65, 0:512]))
            vc = p.op("vector", lambda E: E.tensor_copy(rz2[:], rz_s[:]), "vp")
            p.wait("tensor", "vp", vc)
            pc = p.op("tensor", lambda E: E.matmul(ps7[0:64, 0:512], onesb64[:], rz2[:],
                                                   start=True, stop=True), "pp")
            p.wait("vector", "pp", pc)
            p.opss("vector", lambda E: E.tensor_copy(rzb[:], ps7[0:64, 0:512]))
            vc = p.op("vector", lambda E, o=oc, a=hh * 64: E.tensor_mul(
                ych[o][a:a + 64, :], ps6[0:64, 0:512], rzb[:]), "vp")
            p.wait("tensor", "vp", vc)
        for d in range(NDC):
            for p2 in range(2):
                pc = p.op("tensor", lambda E, dd=d, pp2=p2: E.matmul(
                    ps7[:, 0:512], wo_s[pp2][:, dd * 128:(dd + 1) * 128], ych[pp2][:],
                    start=(pp2 == 0), stop=(pp2 == 1)), "pp")
            p.wait("vector", "pp", pc)
            vc = p.op("vector", lambda E: E.tensor_copy(pout[:], ps7[:, 0:512]), "vp")
            p.wait("sync", "vp", vc)
            p.dma("sync", part_d[d * 128:(d + 1) * 128, qc * 512:(qc + 1) * 512],
                  pout[:], "st")
            p.wait("tensor", "vp", vc)
            p.wait("vector", "st")
    p.wait("sync", "st")
    return p.build()


def trig_consts():
    import struct
    two_pi = 2 * np.pi
    c1 = np.float32(np.ldexp(np.floor(np.ldexp(two_pi, 11)), -11))
    c2 = np.float32(np.ldexp(np.floor(np.ldexp(two_pi - float(c1), 11 + 22)), -(11 + 22)))
    c3 = np.float32(two_pi - float(c1) - float(c2))
    inv2pi = np.float32(1.0 / two_pi)
    half_pi = np.float32(np.pi / 2)
    return np.array([[c1, c2, c3, inv2pi, half_pi, np.float32(two_pi),
                      np.float32(np.pi), 0]], dtype=np.float32)


def perm_cols_qk(base):
    return [base + 2 * i for i in range(32)] + [base + 2 * i + 1 for i in range(32)]


def l1_q_col_order(g):
    cols = []
    for pair in range(2):
        hA, hB = 4 * g + 2 * pair, 4 * g + 2 * pair + 1
        pa = perm_cols_qk(hA * 64)
        pb = perm_cols_qk(hB * 64)
        cols += pa[0:32] + pb[0:32] + pa[32:64] + pb[32:64]
    return cols


def emit_gating_cols(p, xnb, gw_sb, E, gcols, ts0, nts, tmp):
    gl, m8, g1, msk1, msk2, t_a, g2 = (tmp["gl"], tmp["m8"], tmp["g1"], tmp["msk1"],
                                       tmp["msk2"], tmp["t_a"], tmp["g2"])
    ps_g = tmp["ps_g"]
    xhi, xlo = xnb
    ghi, glo = gw_sb
    for ts in range(nts):
        nmm = 0
        for d in range(NDC):
            for a_, b_ in ((xhi, ghi), (xhi, glo), (xlo, ghi)):
                nmm += 1
                pc = p.op("tensor", lambda E_, dd=d, t=ts, aa=a_, bb=b_, n=nmm: E_.matmul(
                    ps_g[0:128, 0:E], aa[:, dd * 512 + t * 128: dd * 512 + t * 128 + 128],
                    bb[:, dd * E:(dd + 1) * E], start=(n == 1), stop=(n == 24)), "pp")
        p.wait("vector", "pp", pc)
        vc0 = p.op("vector", lambda E_: E_.tensor_copy(gl[:, 0:E], ps_g[0:128, 0:E]), "vp")
        p.wait("tensor", "vp", vc0)
        p.wait("vector", "vp", vc0)
        p.opss("vector", lambda E_: E_.max(out=m8[:], in_=gl[:, 0:E]))
        vc = p.op("vector", lambda E_: E_.tensor_sub(t_a[:, 0:1], m8[:, 0:1], m8[:, 1:2]),
                  "vp")
        p.wait("scalar", "vp", vc)
        sc = p.op("scalar", lambda E_: E_.activation(g1[:], t_a[:, 0:1], AF.Sigmoid), "sp")
        p.wait("vector", "sp", sc)
        p.opss("vector", lambda E_: E_.tensor_scalar(g2[:], g1[:], -1.0, 1.0,
                                                     op0=ALU.mult, op1=ALU.add))
        p.opss("vector", lambda E_: E_.tensor_scalar(msk1[:, 0:E], gl[:, 0:E],
                                                     m8[:, 0:1], None, op0=ALU.is_equal))
        p.opss("vector", lambda E_: E_.tensor_scalar(msk2[:, 0:E], gl[:, 0:E],
                                                     m8[:, 1:2], None, op0=ALU.is_equal))
        p.opss("vector", lambda E_: E_.tensor_scalar(msk1[:, 0:E], msk1[:, 0:E],
                                                     g1[:], None, op0=ALU.mult))
        p.opss("vector", lambda E_: E_.tensor_scalar(msk2[:, 0:E], msk2[:, 0:E],
                                                     g2[:], None, op0=ALU.mult))
        p.op("vector", lambda E_, t=ts: E_.tensor_add(
            gcols[:, (ts0 + t) * E:(ts0 + t + 1) * E], msk1[:, 0:E], msk2[:, 0:E]), "vp")


MOA_DBG = False

def build_moa():
    p = P("moa")
    x_in = p.din("x", [D, S])
    if MOA_DBG:
        dbg_gl = p.dout("dbg_gl", [128, 8])
    ln1w_d = p.din("ln1w", [D])
    ln1b_d = p.din("ln1b", [D])
    gw_d = p.din("gw", [D, 8])
    qw_d = [p.din(f"qw{e}", [D, 128]) for e in range(2)]
    qb_d = p.din("qb", [2, 128])
    kw_d = p.din("kw", [D, 128])
    kb_d = p.din("kb", [128])
    vw_d = p.din("vw", [D, 128])
    vb_d = p.din("vb", [128])
    ow_d = [p.din(f"ow{e}", [128, D]) for e in range(2)]
    relT_d = p.din("relT", [128, 129], BF16)
    part_d = p.dout("part", [S, D])
    pext = [p.nc.dram_tensor(f"pext{e}", [S, 1536], F32, kind="Internal").ap()
            for e in range(2)]

    for s in ("ld", "vp", "sp", "pp", "gp", "st", "tn"):
        p.sem(s)
    pool = [p.ps(f"pl{i}", [128, 512]) for i in range(8)]
    consts = mk_ln_consts(p, pool=pool)

    xin = p.sb("xin", [128, 4096])
    srcf = p.sb("srcf", [128, 4096])
    xnb = p.sb("xnb", [128, 4096], BF16)
    xn1f = p.sb("xn1f", [128, 4096])
    xn1lo = p.sb("xn1lo", [128, 4096], BF16)
    gwhi = p.sb("gwhi", [128, 64], BF16)
    gwlo = p.sb("gwlo", [128, 64], BF16)
    gwf = p.sb("gwf", [128, 64])
    qfm = [p.sb(f"qfm{e}", [128, S], BF16) for e in range(2)]
    kfm = p.sb("kfm", [128, S], BF16)
    vT = p.sb("vT", [128, S], BF16)
    gw_sb = p.sb("gw_sb", [128, 64], BF16)
    qw_sb = [p.sb(f"qw_sb{e}", [128, 1024], BF16) for e in range(2)]
    kw_sb = p.sb("kw_sb", [128, 1024], BF16)
    vw_sb = p.sb("vw_sb", [128, 1024], BF16)
    ow_sb = [p.sb(f"ow_sb{e}", [128, 1024], BF16) for e in range(2)]
    relT_sb = p.sb("relT_sb", [128, 129], BF16)
    wcol = p.sb("wcol", [128, NDC])
    bcol = p.sb("bcol", [128, NDC])
    qb_c = p.sb("qb_c", [128, 2])
    kb_c = p.sb("kb_c", [128, 1])
    vb_c = p.sb("vb_c", [128, 1])
    gcols = [p.sb(f"gcols{e}", [128, 64]) for e in range(2)]
    tmpg = dict(gl=p.sb("gl", [128, 16]), m8=p.sb("m8", [128, 8]),
                g1=p.sb("g1", [128, 1]), g2=p.sb("g2", [128, 1]),
                msk1=p.sb("msk1", [128, 16]), msk2=p.sb("msk2", [128, 16]),
                t_a=p.sb("t_a", [128, 2]), ps_g=pool[4])
    gall = p.sb("gall", [128, 8 * 64])
    if MOA_DBG:
        tmpg["dbg"] = dbg_gl
        tmpg["dbg2"] = p.sb("dbg2", [128, 8])
    P0t = [p.sb(f"P0t{e}", [128, 64]) for e in range(2)]
    P128t = [p.sb(f"P128t{e}", [128, 64]) for e in range(2)]
    Psb = p.sb("Psb", [128, 129])
    onesw = p.sb("onesw", [128, 831])
    zext = p.sb("zext", [128, 831])
    fexp = p.sb("fexp", [128, 2])
    negc = p.sb("negc", [128, 1])
    bias_sb = p.sb("bias_sb", [128, 512])
    s2 = p.sb("s2", [128, 512])
    pch = [p.sb(f"pch{i}", [128, 512], BF16) for i in range(2)]
    zacc = p.sb("zacc", [128, 16])
    zrow = p.sb("zrow", [128, 1])
    rz = p.sb("rz", [128, 1])
    gz = p.sb("gz", [128, 1])
    ydg = p.sb("ydg", [128, 128], BF16)
    pTe = p.sb("pTe", [128, 128], BF16)
    ydgT = p.sb("ydgT", [128, 128], BF16)
    pout = p.sb("pout2", [128, 128])
    iden = p.sb("iden", [128, 128], BF16)
    emit_identity(p, iden)

    def ldcast(dst_b, src_ap, ncolw, f32buf):
        for d2 in range(NDC):
            c = p.dma("sync", f32buf[:, 0:ncolw], src_ap[d2 * 128:(d2 + 1) * 128, :], "ld")
            p.wait("vector", "ld", c)
            p.op("vector", lambda E_, dd=d2, n=ncolw: E_.tensor_copy(
                dst_b[:, dd * n:(dd + 1) * n], f32buf[:, 0:n]), "vp")
            p.wait("sync", "vp")
    f32buf = p.sb("f32buf", [128, 1024])
    ldcast(gw_sb, gw_d, 8, f32buf)
    for d2 in range(NDC):
        c = p.dma("sync", gwf[:, d2 * 8:(d2 + 1) * 8], gw_d[d2 * 128:(d2 + 1) * 128, :],
                  "ld")
    p.wait("vector", "ld", c)
    p.op("vector", lambda E_: E_.tensor_copy(gwhi[:], gwf[:]))
    p.op("vector", lambda E_: E_.tensor_sub(gwf[:], gwf[:], gwhi[:]))
    p.op("vector", lambda E_: E_.tensor_copy(gwlo[:], gwf[:]), "vp")
    for e in range(2):
        ldcast(qw_sb[e], qw_d[e], 128, f32buf)
    ldcast(kw_sb, kw_d, 128, f32buf)
    ldcast(vw_sb, vw_d, 128, f32buf)
    for e in range(2):
        c = p.dma("sync", f32buf[:, 0:1024], ow_d[e], "ld")
        p.wait("vector", "ld", c)
        p.op("vector", lambda E_, ee=e: E_.tensor_copy(ow_sb[ee][:], f32buf[:, 0:1024]),
             "vp")
        p.wait("sync", "vp")
    c = p.dma("sync", relT_sb[:], relT_d, "ld")
    for d in range(NDC):
        p.dma("sync", wcol[:, d:d + 1], AP(ln1w_d.tensor, d * 128, [[1, 128]]), "ld")
        p.dma("sync", bcol[:, d:d + 1], AP(ln1b_d.tensor, d * 128, [[1, 128]]), "ld")
    for e in range(2):
        p.dma("sync", qb_c[:, e:e + 1], AP(qb_d.tensor, e * 128, [[1, 128]]), "ld")
    p.dma("sync", kb_c[:], AP(kb_d.tensor, 0, [[1, 128]]), "ld")
    c = p.dma("sync", vb_c[:], AP(vb_d.tensor, 0, [[1, 128]]), "ld")
    p.wait("vector", "ld", c)
    p.op("vector", lambda E_: E_.tensor_scalar(qb_c[:], qb_c[:], 0.08838834764831845,
                                               None, op0=ALU.mult))
    p.op("vector", lambda E_: E_.memset(onesw[:], 1.0))
    p.op("vector", lambda E_: E_.memset(negc[:], -16.0), "vp")

    for xc in range(S // 512):
        sl = slice(xc * 512, xc * 512 + 512)
        if xc >= 1:
            p.wait("sync", "vp", p.cnt["vp"])
            p.wait("sync", "sp", p.cnt["sp"])
        for d in range(NDC):
            c = p.dma("sync", xin[:, d * 512:(d + 1) * 512],
                      x_in[d * 128:(d + 1) * 128, sl], "ld")
        p.wait("vector", "ld", c)
        p.wait("scalar", "ld", c)
        xaps = [xin[:, d * 512:(d + 1) * 512] for d in range(NDC)]
        saps = [srcf[:, d * 512:(d + 1) * 512] for d in range(NDC)]
        naps = [xn1f[:, d * 512:(d + 1) * 512] for d in range(NDC)]
        emit_ln_chunk(p, xaps, saps, consts)
        emit_ln_chunk(p, saps, naps, consts, affine=(wcol, bcol))
        for d in range(NDC):
            p.op("vector", lambda E_, dd=d: E_.tensor_copy(
                xnb[:, dd * 512:(dd + 1) * 512], xn1f[:, dd * 512:(dd + 1) * 512]))
            p.op("vector", lambda E_, dd=d: E_.tensor_sub(
                srcf[:, dd * 512:(dd + 1) * 512], xn1f[:, dd * 512:(dd + 1) * 512],
                xnb[:, dd * 512:(dd + 1) * 512]))
            p.op("vector", lambda E_, dd=d: E_.tensor_copy(
                xn1lo[:, dd * 512:(dd + 1) * 512], srcf[:, dd * 512:(dd + 1) * 512]),
                "vp" if d == 7 else None)
        p.wait("tensor", "vp", p.cnt["vp"])
        for which, wsb, bc, dstt in (("q0", qw_sb[0], qb_c[:, 0:1], qfm[0]),
                                     ("q1", qw_sb[1], qb_c[:, 1:2], qfm[1]),
                                     ("k", kw_sb, kb_c[:], kfm),
                                     ("v", vw_sb, vb_c[:], None)):
            pb = pool[5]
            for d in range(NDC):
                pc = p.op("tensor", lambda E_, dd=d, w=wsb, pb_=pb: E_.matmul(
                    pb_[:, 0:512], w[:, dd * 128:(dd + 1) * 128],
                    xnb[:, dd * 512:(dd + 1) * 512],
                    start=(dd == 0), stop=(dd == 7)), "pp")
            p.wait("scalar", "pp", pc)
            if which in ("q0", "q1"):
                sc = p.op("scalar", lambda E_, w2=dstt, b=bc, s_=sl, pb_=pb: E_.activation(
                    w2[:, s_], pb_[:, 0:512], AF.Identity, bias=b,
                    scale=0.08838834764831845), "sp")
                p.wait("tensor", "sp", sc)
            elif which == "k":
                sc = p.op("scalar", lambda E_, w2=dstt, b=bc, s_=sl, pb_=pb: E_.activation(
                    w2[:, s_], pb_[:, 0:512], AF.Identity, bias=b), "sp")
                p.wait("tensor", "sp", sc)
            else:
                sc = p.op("scalar", lambda E_, b=bc, pb_=pb: E_.activation(
                    s2[:], pb_[:, 0:512], AF.Identity, bias=b), "sp")
                p.wait("vector", "sp", sc)
                vc = p.op("vector", lambda E_: E_.tensor_copy(pch[0][:], s2[:]), "vp")
                p.wait("tensor", "vp", vc)
                for kt4 in range(4):
                    kt = xc * 4 + kt4
                    pc = p.op("tensor", lambda E_, a=kt4 * 128: E_.transpose(
                        pool[6][:, 0:128].bitcast(BF16)[:, 0:128], pch[0][:, a:a + 128],
                        iden[:]), "pp")
                    p.wait("vector", "pp", pc)
                    vc = p.op("vector", lambda E_, k=kt: E_.tensor_copy(
                        vT[:, k * 128:(k + 1) * 128],
                        pool[6][:, 0:128].bitcast(BF16)[:, 0:128]), "vp")
                    p.wait("tensor", "vp", vc)
        emit_gating_cols(p, (xnb, xn1lo), (gwhi, gwlo), 8, gall, xc * 4, 4, tmpg)
        p.op("vector", lambda E_: E_.nop(), "vp")
    for e in range(2):
        ge = MOA_LOCAL_EXPERTS[e]
        for t in range(64):
            p.op("vector", lambda E_, ee=e, tt=t, g=ge: E_.tensor_copy(
                gcols[ee][:, tt:tt + 1], gall[:, tt * 8 + g:tt * 8 + g + 1]),
                "vp" if t == 63 else None)

    p.wait("tensor", "sp", p.cnt["sp"])
    p.wait("tensor", "vp", p.cnt["vp"])
    for e in range(2):
        for t in range(64):
            pc = p.op("tensor", lambda E_, ee=e, tt=t: E_.matmul(
                pool[5][:, 0:129], qfm[ee][:, tt * 128:(tt + 1) * 128], relT_sb[:],
                start=True, stop=True), "pp")
            p.wait("vector", "pp", pc)
            p.opss("vector", lambda E_: E_.tensor_copy(Psb[:], pool[5][:, 0:129]))
            p.op("vector", lambda E_, ee=e, tt=t: E_.tensor_copy(
                P0t[ee][:, tt:tt + 1], Psb[:, 0:1]))
            p.op("vector", lambda E_, ee=e, tt=t: E_.tensor_copy(
                P128t[ee][:, tt:tt + 1], Psb[:, 128:129]))
            p.opss("vector", lambda E_: E_.tensor_scalar(
                zext[:, 0:576], onesw[:, 0:576], Psb[:, 0:1], None, op0=ALU.mult))
            vc = p.op("vector", lambda E_: E_.nop(), "vp")
            p.wait("gpsimd", "vp", vc)
            p.dma("gpsimd", pext[e][t * 128:(t + 1) * 128, 0:576], zext[:, 0:576], "st")
            p.dma("gpsimd", pext[e][t * 128:(t + 1) * 128, 576:705], Psb[:], "st")
            p.opss("vector", lambda E_: E_.tensor_scalar(
                zext[:, 0:831], onesw[:, 0:831], Psb[:, 128:129], None, op0=ALU.mult))
            vc = p.op("vector", lambda E_: E_.nop(), "vp")
            p.wait("gpsimd", "vp", vc)
            p.dma("gpsimd", pext[e][t * 128:(t + 1) * 128, 705:1536], zext[:, 0:831],
                  "st")
            p.wait("vector", "st")
            p.wait("tensor", "vp", vc)

    p.wait("sync", "st", p.cnt["st"])
    p.wait("tensor", "st", p.cnt["st"])
    p.wait("scalar", "vp", p.cnt["vp"])
    for t in range(64):
        q0 = t * 128
        for e in range(2):
            p.wait("vector", "sp", p.cnt["sp"])
            p.op("vector", lambda E_, ee=e, tt=t: E_.tensor_tensor(
                fexp[:, 0:1], P0t[ee][:, tt:tt + 1], negc[:], ALU.add))
            vc = p.op("vector", lambda E_, ee=e, tt=t: E_.tensor_tensor(
                fexp[:, 1:2], P128t[ee][:, tt:tt + 1], negc[:], ALU.add), "vp")
            p.wait("scalar", "vp", vc)
            for c5 in range(16):
                k0 = c5 * 512
                lo_band, hi_band = q0 - 64, q0 + 192
                banded = not (k0 + 512 <= lo_band or k0 >= hi_band)
                pb = pool[c5 % 2]
                pc = p.op("tensor", lambda E_, ee=e, tt=t, kk=k0, pb_=pb: E_.matmul(
                    pb_[:, 0:512], qfm[ee][:, tt * 128:(tt + 1) * 128],
                    kfm[:, kk:kk + 512], start=True, stop=True), "pp")
                if banded:
                    off = t * 128 * 1536 + k0 - t * 128 + 768
                    assert 768 + k0 - (q0 + 127) >= 0 and 768 + k0 + 511 - q0 < 1536
                    p.wait("sync", "vp", p.cnt["vp"])
                    c = p.dma("sync", bias_sb[:],
                              AP(pext[e].tensor, off, [[1535, 128], [1, 512]]), "ld")
                    p.wait("vector", "ld", c)
                    p.wait("vector", "pp", pc)
                    vc = p.op("vector", lambda E_, pb_=pb: E_.tensor_add(
                        s2[:], pb_[:, 0:512], bias_sb[:]), "vp")
                    p.wait("scalar", "vp", vc)
                    sc = p.op("scalar", lambda E_, i=c5 % 2, cc=c5: E_.activation(
                        pch[i][:], s2[:], AF.Exp, bias=negc[:],
                        accum_out=zacc[:, cc:cc + 1]), "sp")
                    p.wait("tensor", "sp", sc)
                    p.wait("vector", "sp", sc)
                else:
                    right = k0 >= hi_band
                    p.wait("scalar", "pp", pc)
                    sc = p.op("scalar", lambda E_, i=c5 % 2, cc=c5, r=right, pb_=pb:
                              E_.activation(pch[i][:], pb_[:, 0:512], AF.Exp,
                                            bias=fexp[:, 1:2] if r else fexp[:, 0:1],
                                            accum_out=zacc[:, cc:cc + 1]), "sp")
                    p.wait("tensor", "sp", sc)
                for kt4 in range(4):
                    kt = c5 * 4 + kt4
                    pc = p.op("tensor", lambda E_, i=c5 % 2, a=kt4 * 128: E_.transpose(
                        pool[2][:, 0:64].bitcast(BF16)[:, 0:128], pch[i][:, a:a + 128],
                        iden[:]), "pp")
                    p.wait("vector", "pp", pc)
                    vc = p.op("vector", lambda E_: E_.tensor_copy(
                        pTe[:], pool[2][:, 0:64].bitcast(BF16)[:, 0:128]), "vp")
                    p.wait("tensor", "vp", vc)
                    pc = p.op("tensor", lambda E_, k=kt: E_.matmul(
                        pool[3][:, 0:128], pTe[:], vT[:, k * 128:(k + 1) * 128],
                        start=(k == 0), stop=(k == 63), skip_group_check=True), "pp")
                    p.op("tensor", lambda E_: E_.nop(), "tn")
                    p.wait("vector", "tn", p.cnt["tn"])
            p.wait("vector", "pp", pc)
            p.opss("vector", lambda E_: E_.reduce_sum(zrow[:], zacc[:], axis=AX.X))
            p.opss("vector", lambda E_: E_.reciprocal(rz[:], zrow[:]))
            p.opss("vector", lambda E_, ee=e, tt=t: E_.tensor_tensor(
                gz[:], rz[:], gcols[ee][:, tt:tt + 1], ALU.mult))
            vc = p.op("vector", lambda E_: E_.tensor_scalar(
                ydg[:], pool[3][:, 0:128], gz[:], None, op0=ALU.mult), "vp")
            p.wait("tensor", "vp", vc)
            pc = p.op("tensor", lambda E_: E_.transpose(
                pool[2][:, 0:64].bitcast(BF16)[:, 0:128], ydg[:], iden[:]), "pp")
            p.wait("vector", "pp", pc)
            vc = p.op("vector", lambda E_: E_.tensor_copy(
                ydgT[:], pool[2][:, 0:64].bitcast(BF16)[:, 0:128]), "vp")
            p.wait("tensor", "vp", vc)
            for dc2 in range(2):
                pc = p.op("tensor", lambda E_, ee=e, d2=dc2: E_.matmul(
                    pool[6 + d2][:, 0:512], ydgT[:],
                    ow_sb[ee][:, d2 * 512:(d2 + 1) * 512],
                    start=(ee == 0), stop=(ee == 1), skip_group_check=True), "pp")
                if e == 1:
                    p.wait("vector", "pp", pc)
                    vc = p.op("vector", lambda E_, d2=dc2: E_.tensor_copy(
                        s2[:], pool[6 + d2][:, 0:512]), "vp")
                    p.wait("gpsimd", "vp", vc)
                    p.dma("gpsimd", part_d[t * 128:(t + 1) * 128,
                                           dc2 * 512:(dc2 + 1) * 512], s2[:], "st")
                    p.wait("vector", "st")
                    p.wait("tensor", "vp", vc)
    if MOA_DBG:
        dbgo = p.dout("dbg_pers", [128, 64 * 4 + 128 + 512])
        p.wait("gpsimd", "vp", p.cnt["vp"])
        p.dma("gpsimd", AP(dbgo.tensor, 0, [[896, 128], [1, 64]]), P0t[0][:], "st")
        p.dma("gpsimd", AP(dbgo.tensor, 64, [[896, 128], [1, 64]]), P128t[0][:], "st")
        p.dma("gpsimd", AP(dbgo.tensor, 128, [[896, 128], [1, 64]]), gcols[0][:], "st")
        p.dma("gpsimd", AP(dbgo.tensor, 192, [[896, 128], [1, 64]]), gcols[1][:], "st")
        p.dma("gpsimd", AP(dbgo.tensor, 256, [[896, 128], [1, 128]]),
              qfm[0][:, 256:384], "st")
        p.dma("gpsimd", AP(dbgo.tensor, 384, [[896, 128], [1, 512]]),
              kfm[:, 1024:1536], "st")
    p.wait("sync", "st")
    p.wait("gpsimd", "st")
    return p.build()


MOA_LOCAL_EXPERTS = (0, 1)


G3_DBG = False

def build_g3():
    p = P("g3")
    x_in = p.din("x", [D, TOK])
    parts = [p.din(f"part{i}", [D, TOK]) for i in range(4)]
    ln1w_d = p.din("ln1w", [D]); ln1b_d = p.din("ln1b", [D])
    ln2w_d = p.din("ln2w", [D]); ln2b_d = p.din("ln2b", [D])
    gw8_d = p.din("gw8", [D, 8])
    ob_d = p.din("ob", [8, D])
    gw16_d = p.din("gw16", [D, 16])
    w1_d = p.din("w1", [16, 1024, 512])
    b1_d = p.din("b1", [16, 512])
    w2_d = p.din("w2", [16, 512, 1024])
    b2_d = p.din("b2", [16, 1024])
    xout_d = p.dout("xout", [D, TOK])

    for s in ("ld", "vp", "sp", "pp", "gp", "st", "tn"):
        p.sem(s)
    pool = [p.ps(f"pl{i}", [128, 512]) for i in range(8)]
    consts = mk_ln_consts(p, pool=pool)

    xin = p.sb("xin", [128, 4096])
    srcf = p.sb("srcf", [128, 4096])
    xn1f = p.sb("xn1f", [128, 4096])
    out1 = p.sb("out1", [128, 4096])
    xn2f = p.sb("xn2f", [128, 4096])
    xn2b = p.sb("xn2b", [128, 4096], BF16)
    outacc = p.sb("outacc", [128, 4096])
    rf = p.sb("rf", [128, 2048])
    rgb = p.sb("rgb", [128, 2048], BF16)
    w1sb = p.sb("w1sb", [128, 4096], BF16)
    w2sb = p.sb("w2sb", [128, 4096], BF16)
    f32buf = p.sb("f32buf", [128, 1024])
    gwf8 = p.sb("gwf8", [128, 64])
    gwf16 = p.sb("gwf16", [128, 128])
    gw8hi = p.sb("gw8hi", [128, 64], BF16); gw8lo = p.sb("gw8lo", [128, 64], BF16)
    gw16hi = p.sb("gw16hi", [128, 128], BF16); gw16lo = p.sb("gw16lo", [128, 128], BF16)
    xg_hi = p.sb("xg_hi", [128, 4096], BF16); xg_lo = p.sb("xg_lo", [128, 4096], BF16)
    obb = p.sb("obb", [8, 1024], BF16)
    b2b = p.sb("b2b", [16, 1024], BF16)
    b1c = p.sb("b1c", [128, 4])
    gA = p.sb("gA", [8, 512], BF16)
    gFblk = p.sb("gFblk", [16, 512], BF16)
    gtb = p.sb("gtb", [128, 16], BF16)
    selb = p.sb("selb", [16, 2048], BF16)
    sel_i1 = p.sb("sel_i1", [16, 1])
    sel_i2 = p.sb("sel_i2", [16, 2048])
    w1col = p.sb("w1col", [128, NDC]); b1col = p.sb("b1col", [128, NDC])
    w2col = p.sb("w2col", [128, NDC]); b2col = p.sb("b2col", [128, NDC])
    tmpg = dict(gl=p.sb("gl", [128, 16]), m8=p.sb("m8", [128, 8]),
                g1=p.sb("g1", [128, 1]), g2=p.sb("g2", [128, 1]),
                msk1=p.sb("msk1", [128, 16]), msk2=p.sb("msk2", [128, 16]),
                t_a=p.sb("t_a", [128, 2]), ps_g=pool[4])
    gall8 = p.sb("gall8", [128, 8 * 4])
    gall16 = p.sb("gall16", [128, 16 * 4])
    gcl = p.sb("gcl", [128, 1])
    tchunk = p.sb("tchunk", [128, 512])
    iden = p.sb("iden", [128, 128], BF16)
    emit_identity(p, iden)
    idenf = p.sb("idenf", [128, 128])
    p.op("vector", lambda E_: E_.tensor_copy(idenf[:], iden[:]), "vp")
    consts["idenf_g3"] = idenf
    if G3_DBG:
        iden1 = p.sb("iden1", [128, 128], BF16)
        iden2 = p.sb("iden2", [128, 128], BF16)
        iden3 = p.sb("iden3", [128, 128], BF16)
        _ic, _ir, _if = p._id_parts
        p.op("vector", lambda E_: E_.tensor_copy(iden1[:], _if[:]))
        p.op("vector", lambda E_: E_.tensor_copy(iden2[:, 0:64], _if[:, 0:64]))
        p.op("vector", lambda E_: E_.tensor_copy(iden2[:, 64:128], _if[:, 64:128]))
        p.op("vector", lambda E_: E_.tensor_scalar(iden3[:], _ic[:], _ir[:], None,
                                                   op0=ALU.is_equal), "vp")
        idsnap = p.sb("idsnap", [128, 512])
        _snap = [0]
        def snap_iden():
            k = _snap[0]; _snap[0] += 1
            if k < 4:
                p.op("vector", lambda E_, kk=k: E_.tensor_copy(
                    idsnap[:, kk * 128:(kk + 1) * 128], iden[:]), "vp")
        snap_iden()
    else:
        def snap_iden():
            pass
    p.op("gpsimd", lambda E_: E_.iota(sel_i1[:], pattern=[[1, 1]], base=0,
                                      channel_multiplier=1,
                                      allow_small_or_imprecise_dtypes=True), "gp")
    p.op("gpsimd", lambda E_: E_.iota(sel_i2[:], pattern=[[1, 16], [0, 128]], base=0,
                                      channel_multiplier=0,
                                      allow_small_or_imprecise_dtypes=True), "gp")
    p.wait("vector", "gp")
    p.op("vector", lambda E_: E_.tensor_scalar(selb[:], sel_i2[:], sel_i1[:], None,
                                               op0=ALU.is_equal), "vp")

    for d in range(NDC):
        p.dma("sync", gwf8[:, d * 8:(d + 1) * 8], gw8_d[d * 128:(d + 1) * 128, :], "ld")
        p.dma("sync", gwf16[:, d * 16:(d + 1) * 16], gw16_d[d * 128:(d + 1) * 128, :], "ld")
        p.dma("sync", w1col[:, d:d + 1], AP(ln1w_d.tensor, d * 128, [[1, 128]]), "ld")
        p.dma("sync", b1col[:, d:d + 1], AP(ln1b_d.tensor, d * 128, [[1, 128]]), "ld")
        p.dma("sync", w2col[:, d:d + 1], AP(ln2w_d.tensor, d * 128, [[1, 128]]), "ld")
        p.dma("sync", b2col[:, d:d + 1], AP(ln2b_d.tensor, d * 128, [[1, 128]]), "ld")
    p.wait("vector", "ld")
    p.op("vector", lambda E_: E_.tensor_copy(gw8hi[:], gwf8[:]))
    p.op("vector", lambda E_: E_.tensor_sub(f32buf[:, 0:64], gwf8[:], gw8hi[:]))
    p.op("vector", lambda E_: E_.tensor_copy(gw8lo[:], f32buf[:, 0:64]))
    p.op("vector", lambda E_: E_.tensor_copy(gw16hi[:], gwf16[:]))
    p.op("vector", lambda E_: E_.tensor_sub(f32buf[:, 0:128], gwf16[:], gw16hi[:]))
    p.op("vector", lambda E_: E_.tensor_copy(gw16lo[:], f32buf[:, 0:128]), "vp")
    p.wait("sync", "vp")
    c = p.dma("sync", f32buf[0:8, 0:1024], ob_d, "ld")
    p.wait("vector", "ld", c)
    p.op("vector", lambda E_: E_.tensor_copy(obb[:], f32buf[0:8, 0:1024]), "vp")
    p.wait("sync", "vp")
    c = p.dma("sync", f32buf[0:16, 0:1024], b2_d, "ld")
    p.wait("vector", "ld", c)
    p.op("vector", lambda E_: E_.tensor_copy(b2b[:], f32buf[0:16, 0:1024]), "vp")
    p.wait("sync", "vp")

    p.wait("vector", "ld", p.cnt["ld"])
    p.op("vector", lambda E_: E_.tensor_copy(iden[:], p._id_parts[2][:]), "vp")
    snap_iden()
    for ci in range(TOK // 512):
        sl = slice(ci * 512, ci * 512 + 512)
        if ci >= 1:
            p.wait("sync", "vp", p.cnt["vp"])
            p.wait("sync", "sp", p.cnt["sp"])
        for d in range(NDC):
            c = p.dma("sync", xin[:, d * 512:(d + 1) * 512],
                      x_in[d * 128:(d + 1) * 128, sl], "ld")
        p.wait("vector", "ld", c)
        p.wait("scalar", "ld", c)
        xaps = [xin[:, d * 512:(d + 1) * 512] for d in range(NDC)]
        saps = [srcf[:, d * 512:(d + 1) * 512] for d in range(NDC)]
        n1aps = [xn1f[:, d * 512:(d + 1) * 512] for d in range(NDC)]
        emit_ln_chunk(p, xaps, saps, consts)
        emit_ln_chunk(p, saps, n1aps, consts, affine=(w1col, b1col))
        for d in range(NDC):
            p.op("vector", lambda E_, dd=d: E_.tensor_copy(
                xg_hi[:, dd * 512:(dd + 1) * 512], xn1f[:, dd * 512:(dd + 1) * 512]))
            p.op("vector", lambda E_, dd=d: E_.tensor_sub(
                tchunk[:], xn1f[:, dd * 512:(dd + 1) * 512],
                xg_hi[:, dd * 512:(dd + 1) * 512]))
            p.op("vector", lambda E_, dd=d: E_.tensor_copy(
                xg_lo[:, dd * 512:(dd + 1) * 512], tchunk[:]),
                "vp" if d == 7 else None)
        p.wait("tensor", "vp", p.cnt["vp"])
        emit_gating_cols(p, (xg_hi, xg_lo), (gw8hi, gw8lo), 8, gall8, 0, 4, tmpg)
        p.wait("tensor", "vp", p.cnt["vp"])
        for ts in range(4):
            vc = p.op("vector", lambda E_, t=ts: E_.tensor_copy(
                gtb[:, 0:8], gall8[:, t * 8:(t + 1) * 8]), "vp")
            p.wait("tensor", "vp", vc)
            pc = p.op("tensor", lambda E_: E_.matmul(
                pool[5][0:8, 0:128], gtb[:, 0:8], iden[:],
                start=True, stop=True), "pp")
            p.wait("vector", "pp", pc)
            vc = p.op("vector", lambda E_, t=ts: E_.tensor_copy(
                gA[0:8, t * 128:(t + 1) * 128], pool[5][0:8, 0:128]), "vp")
            p.wait("tensor", "vp", vc)
        snap_iden()
        if True:
            pass
        for d in range(NDC):
            dsl = slice(d * 128, (d + 1) * 128)
            pc = p.op("tensor", lambda E_, dd=d: E_.matmul(
                pool[6][:, 0:512], obb[:, dd * 128:(dd + 1) * 128], gA[:],
                start=True, stop=True), "pp")
            c = p.dma("sync", tchunk[:], parts[0][dsl, sl], "ld")
            p.wait("vector", "ld", c)
            p.op("vector", lambda E_, dd=d: E_.tensor_add(
                out1[:, dd * 512:(dd + 1) * 512], srcf[:, dd * 512:(dd + 1) * 512],
                tchunk[:]), "vp")
            p.wait("sync", "vp")
            for pi in range(1, 4):
                c = p.dma("sync", tchunk[:], parts[pi][dsl, sl], "ld")
                p.wait("vector", "ld", c)
                p.op("vector", lambda E_, dd=d: E_.tensor_add(
                    out1[:, dd * 512:(dd + 1) * 512], out1[:, dd * 512:(dd + 1) * 512],
                    tchunk[:]), "vp")
                p.wait("sync", "vp")
            p.wait("vector", "pp", p.cnt["pp"])
            p.op("vector", lambda E_, dd=d: E_.tensor_add(
                out1[:, dd * 512:(dd + 1) * 512], out1[:, dd * 512:(dd + 1) * 512],
                pool[6][:, 0:512]), "vp")
            p.wait("tensor", "vp", p.cnt["vp"])
        o1aps = [out1[:, d * 512:(d + 1) * 512] for d in range(NDC)]
        n2aps = [xn2f[:, d * 512:(d + 1) * 512] for d in range(NDC)]
        emit_ln_chunk(p, o1aps, n2aps, consts, affine=(w2col, b2col))
        for d in range(NDC):
            p.op("vector", lambda E_, dd=d: E_.tensor_copy(
                xn2b[:, dd * 512:(dd + 1) * 512], xn2f[:, dd * 512:(dd + 1) * 512]),
                "vp" if d == 7 else None)
        p.wait("tensor", "vp", p.cnt["vp"])
        for d in range(NDC):
            p.op("vector", lambda E_, dd=d: E_.tensor_copy(
                xg_hi[:, dd * 512:(dd + 1) * 512], xn2f[:, dd * 512:(dd + 1) * 512]))
            p.op("vector", lambda E_, dd=d: E_.tensor_sub(
                tchunk[:], xn2f[:, dd * 512:(dd + 1) * 512],
                xg_hi[:, dd * 512:(dd + 1) * 512]))
            p.op("vector", lambda E_, dd=d: E_.tensor_copy(
                xg_lo[:, dd * 512:(dd + 1) * 512], tchunk[:]),
                "vp" if d == 7 else None)
        p.wait("tensor", "vp", p.cnt["vp"])
        emit_gating_cols(p, (xg_hi, xg_lo), (gw16hi, gw16lo), 16, gall16, 0, 4, tmpg)
        p.wait("tensor", "vp", p.cnt["vp"])
        for ts in range(4):
            vc = p.op("vector", lambda E_, t=ts: E_.tensor_copy(
                gtb[:, 0:16], gall16[:, t * 16:(t + 1) * 16]), "vp")
            p.wait("tensor", "vp", vc)
            pc = p.op("tensor", lambda E_: E_.matmul(
                pool[5][0:16, 0:128], gtb[:, 0:16], iden[:],
                start=True, stop=True), "pp")
            p.wait("vector", "pp", pc)
            vc = p.op("vector", lambda E_, t=ts: E_.tensor_copy(
                gFblk[0:16, t * 128:(t + 1) * 128], pool[5][0:16, 0:128]), "vp")
            p.wait("tensor", "vp", vc)
        snap_iden()
        for d in range(NDC):
            p.op("vector", lambda E_, dd=d: E_.memset(
                outacc[:, dd * 512:(dd + 1) * 512], 0.0), "vp" if d == 7 else None)
        for e in range(16):
            for d in range(NDC):
                c = p.dma("sync", f32buf[:, 0:512],
                          AP(w1_d.tensor, e * 1024 * 512 + d * 128 * 512,
                             [[512, 128], [1, 512]]), "ld")
                p.wait("vector", "ld", c)
                p.op("vector", lambda E_, dd=d: E_.tensor_copy(
                    w1sb[:, dd * 512:(dd + 1) * 512], f32buf[:, 0:512]), "vp")
                p.wait("sync", "vp")
            for hc in range(4):
                c = p.dma("sync", f32buf[:, 0:1024],
                          AP(w2_d.tensor, e * 512 * 1024 + hc * 128 * 1024,
                             [[1024, 128], [1, 1024]]), "ld")
                p.wait("vector", "ld", c)
                p.op("vector", lambda E_, h=hc: E_.tensor_copy(
                    w2sb[:, h * 1024:(h + 1) * 1024], f32buf[:, 0:1024]), "vp")
                p.wait("sync", "vp")
            c = p.dma("sync", b1c[:, 0:4], AP(b1_d.tensor, e * 512, [[1, 128], [128, 4]]),
                      "ld", slow=True)
            p.wait("scalar", "ld", c)
            p.wait("tensor", "vp", p.cnt["vp"])
            for hc in range(4):
                for d in range(NDC):
                    pc = p.op("tensor", lambda E_, dd=d, h=hc: E_.matmul(
                        pool[5][:, 0:512],
                        w1sb[:, dd * 512 + h * 128: dd * 512 + h * 128 + 128],
                        xn2b[:, dd * 512:(dd + 1) * 512],
                        start=(dd == 0), stop=(dd == 7)), "pp")
                p.wait("scalar", "pp", pc)
                sc = p.op("scalar", lambda E_, h=hc: E_.activation(
                    rf[:, h * 512:(h + 1) * 512], pool[5][:, 0:512], AF.Relu,
                    bias=b1c[:, h:h + 1]), "sp")
                p.wait("tensor", "sp", sc)
            pc = p.op("tensor", lambda E_, ee=e: E_.matmul(
                pool[6][:, 0:512], selb[:, ee * 128:(ee + 1) * 128], gFblk[:],
                start=True, stop=True), "pp")
            p.wait("vector", "pp", pc)
            p.wait("vector", "sp", p.cnt["sp"])
            for hc in range(4):
                vc = p.op("vector", lambda E_, h=hc: E_.tensor_mul(
                    rgb[:, h * 512:(h + 1) * 512], rf[:, h * 512:(h + 1) * 512],
                    pool[6][:, 0:512]), "vp")
            p.wait("tensor", "vp", vc)
            for d in range(NDC):
                for hc in range(4):
                    pc = p.op("tensor", lambda E_, dd=d, h=hc, ee=e: E_.matmul(
                        pool[7][:, 0:512],
                        w2sb[:, h * 1024 + dd * 128: h * 1024 + dd * 128 + 128],
                        rgb[:, h * 512:(h + 1) * 512],
                        start=(h == 0), stop=(h == 3 and ee != 15)), "pp")
                if e == 15:
                    pc = p.op("tensor", lambda E_, dd=d: E_.matmul(
                        pool[7][:, 0:512], b2b[:, dd * 128:(dd + 1) * 128], gFblk[:],
                        start=False, stop=True), "pp")
                p.wait("vector", "pp", pc)
                vc = p.op("vector", lambda E_, dd=d: E_.tensor_add(
                    outacc[:, dd * 512:(dd + 1) * 512], outacc[:, dd * 512:(dd + 1) * 512],
                    pool[7][:, 0:512]), "vp")
                p.wait("tensor", "vp", vc)
        for d in range(NDC):
            p.op("vector", lambda E_, dd=d: E_.tensor_add(
                tchunk[:], out1[:, dd * 512:(dd + 1) * 512],
                outacc[:, dd * 512:(dd + 1) * 512]))
            vc = p.op("vector", lambda E_, dd=d: E_.scalar_tensor_tensor(
                tchunk[:], tchunk[:], 0.5, xin[:, dd * 512:(dd + 1) * 512],
                op0=ALU.mult, op1=ALU.add), "vp")
            p.wait("gpsimd", "vp", vc)
            p.dma("gpsimd", xout_d[d * 128:(d + 1) * 128, sl], tchunk[:], "st")
            p.wait("vector", "st")
    if G3_DBG:
        W_ = 2816
        dbgo = p.dout("dbg_pers", [128, W_])
        p.wait("gpsimd", "vp", p.cnt["vp"])
        p.dma("gpsimd", AP(dbgo.tensor, 0, [[W_, 8], [1, 512]]), gA[0:8, :], "st")
        p.dma("gpsimd", AP(dbgo.tensor, 512, [[W_, 16], [1, 512]]), gFblk[0:16, :], "st")
        p.dma("gpsimd", AP(dbgo.tensor, 1024, [[W_, 128], [1, 512]]),
              out1[:, 0:512], "st")
        p.dma("gpsimd", AP(dbgo.tensor, 1536, [[W_, 128], [1, 512]]),
              outacc[:, 0:512], "st")
        p.dma("gpsimd", AP(dbgo.tensor, 2048, [[W_, 128], [1, 512]]),
              xn2f[:, 0:512], "st")
        p.dma("gpsimd", AP(dbgo.tensor, 2560, [[W_, 128], [1, 32]]), gall8[:], "st")
        p.dma("gpsimd", AP(dbgo.tensor, 2592, [[W_, 128], [1, 64]]), gall16[:], "st")
        p.dma("gpsimd", AP(dbgo.tensor, 2656, [[W_, 128], [1, 16]]),
              tmpg["gl"][:], "st")
        p.dma("gpsimd", AP(dbgo.tensor, 2672, [[W_, 128], [1, 8]]),
              tmpg["m8"][:], "st")
        p.dma("gpsimd", AP(dbgo.tensor, 2680, [[W_, 128], [1, 16]]),
              tmpg["msk1"][:], "st")
        p.dma("gpsimd", AP(dbgo.tensor, 2696, [[W_, 128], [1, 16]]),
              tmpg["msk2"][:], "st")
        p.dma("gpsimd", AP(dbgo.tensor, 2712, [[W_, 128], [1, 16]]),
              gtb[:], "st")
        dbgo2 = p.dout("dbg_iden", [128, 128])
        p.dma("gpsimd", dbgo2, iden[:], "st")
        dbgo3 = p.dout("dbg_idsnap", [128, 512])
        p.dma("gpsimd", dbgo3, idsnap[:], "st")
        dbgo4 = p.dout("dbg_idparts", [128, 512])
        _ic, _ir, _if = p._id_parts
        p.dma("gpsimd", AP(dbgo4.tensor, 0, [[512, 128], [1, 128]]), _if[:], "st")
        p.dma("gpsimd", AP(dbgo4.tensor, 128, [[512, 128], [1, 128]]), iden1[:], "st")
        p.dma("gpsimd", AP(dbgo4.tensor, 256, [[512, 128], [1, 128]]), iden2[:], "st")
        p.dma("gpsimd", AP(dbgo4.tensor, 384, [[512, 128], [1, 128]]), iden3[:], "st")
    p.wait("sync", "st")
    p.wait("gpsimd", "st")
    return p.build()


L0_NSUB = 17
L0_DBG = False

def build_l0attn():
    p = P("l0attn")
    x_in = p.din("x", [D, S])
    if L0_DBG:
        dbg_stl = p.dout("dbg_stl", [128, 2176])
        dbg_cand = p.dout("dbg_cand", [128, 64])
        dbg_p = p.dout("dbg_pt", [128, 2176])
    wq_d = p.din("wq", [1024, 256])
    bq_d = p.din("bq", [256])
    wk_d = p.din("wk", [1024, 64])
    bk_d = p.din("bk", [64])
    wv_d = p.din("wv", [1024, 64])
    bv_d = p.din("bv", [64])
    qnw_d = p.din("qnw", [64])
    knw_d = p.din("knw", [64])
    slopes_d = p.din("slopes", [1, 4])
    wo_d = p.din("wo", [256, 1024])
    part_d = p.dout("part", [D, S])

    for s in ("ld", "vp", "sp", "pp", "gp", "st", "tn"):
        p.sem(s)
    pool = [p.ps(f"pl{i}", [128, 512]) for i in range(8)]
    consts = mk_ln_consts(p, pool=pool)

    xin = p.sb("xin", [128, 4096])
    xnb = p.sb("xnb", [128, 4096], BF16)
    qfm = [p.sb(f"qfm{i}", [128, S], BF16) for i in range(2)]
    kfm = p.sb("kfm", [128, S], BF16)
    vT = p.sb("vT", [128, 64 * 64], BF16)
    wq_sb = p.sb("wq_sb", [128, 256 * 8], BF16)
    wk_sb = p.sb("wk_sb", [128, 64 * 8], BF16)
    wv_sb = p.sb("wv_sb", [128, 64 * 8], BF16)
    woh = p.sb("woh", [64, 4096], BF16)
    f32buf = p.sb("f32buf", [128, 1024])
    bq_c = p.sb("bq_c", [128, 2])
    bk_c = p.sb("bk_c", [128, 1])
    bv_c = p.sb("bv_c", [128, 1])
    qn_c = p.sb("qn_c", [128, 1])
    kn_c = p.sb("kn_c", [128, 1])
    epsr = p.sb("epsr", [1, 1])
    slop = p.sb("slop", [1, 4])
    onesr64 = p.sb("onesr64", [1, 64])
    ali_row = p.sb("ali_row", [1, 2176])
    iotaw = p.sb("iotaw", [1, 2176])
    alib = p.sb("alib", [128, 2176])
    qraw = p.sb("qraw", [128, 512])
    qsq = p.sb("qsq", [128, 512], BF16)
    rstdr = p.sb("rstdr", [1, 512])
    rstr2 = p.sb("rstr2", [1, 512])
    rb64 = p.sb("rb64", [64, 512])
    stl = p.sb("stl", [128, 2176])
    w1e = p.sb("w1e", [128, 2176])
    w2e = p.sb("w2e", [128, 2176])
    cand = p.sb("cand", [128, 64])
    negm = p.sb("negm", [128, 1])
    thr = p.sb("thr", [128, 1])
    zc = p.sb("zc", [128, 1])
    rz = p.sb("rz", [128, 1])
    etile = p.sb("etile", [128, 2176])
    ptile = p.sb("ptile", [128, 2176], BF16)
    pTs = p.sb("pTs", [128, 128], BF16)
    ynorm = p.sb("ynorm", [128, 64], BF16)
    yT = p.sb("yT", [64, 128 * 4], BF16)
    pout = p.sb("pout3", [128, 128])
    iden = p.sb("iden", [128, 128], BF16)
    emit_identity(p, iden)
    cm_c = p.sb("cm_c", [128, 128])
    cm_r = p.sb("cm_r", [128, 1])
    cmask = p.sb("cmask", [128, 128])
    p.op("gpsimd", lambda E_: E_.iota(cm_c[:], pattern=[[1, 128]], base=0,
                                      channel_multiplier=0,
                                      allow_small_or_imprecise_dtypes=True), "gp")
    p.op("gpsimd", lambda E_: E_.iota(cm_r[:], pattern=[[1, 1]], base=0,
                                      channel_multiplier=1,
                                      allow_small_or_imprecise_dtypes=True), "gp")
    p.wait("vector", "gp")
    p.op("vector", lambda E_: E_.tensor_scalar(cmask[:], cm_c[:], cm_r[:], -1e30,
                                               op0=ALU.is_gt, op1=ALU.mult), "vp")

    def ldcastw(dst_b, src_ap, ncolw):
        for d2 in range(NDC):
            c = p.dma("sync", f32buf[:, 0:ncolw], src_ap[d2 * 128:(d2 + 1) * 128, :], "ld")
            p.wait("vector", "ld", c)
            p.op("vector", lambda E_, dd=d2, n=ncolw: E_.tensor_copy(
                dst_b[:, dd * n:(dd + 1) * n], f32buf[:, 0:n]), "vp")
            p.wait("sync", "vp")
    ldcastw(wq_sb, wq_d, 256)
    ldcastw(wk_sb, wk_d, 64)
    ldcastw(wv_sb, wv_d, 64)
    for i in range(4):
        c = p.dma("sync", f32buf[0:64, 0:1024], wo_d[i * 64:(i + 1) * 64, :], "ld")
        p.wait("vector", "ld", c)
        p.op("vector", lambda E_, ii=i: E_.tensor_copy(
            woh[0:64, ii * 1024:(ii + 1) * 1024], f32buf[0:64, 0:1024]), "vp")
        p.wait("sync", "vp")
    for e in range(2):
        p.dma("sync", bq_c[:, e:e + 1], AP(bq_d.tensor, e * 128, [[1, 128]]), "ld")
    p.dma("sync", bk_c[0:64, :], AP(bk_d.tensor, 0, [[1, 64]]), "ld")
    p.dma("sync", bv_c[0:64, :], AP(bv_d.tensor, 0, [[1, 64]]), "ld")
    p.dma("sync", qn_c[0:64, :], AP(qnw_d.tensor, 0, [[1, 64]]), "ld")
    p.dma("sync", kn_c[0:64, :], AP(knw_d.tensor, 0, [[1, 64]]), "ld")
    c = p.dma("sync", slop[:], slopes_d, "ld")
    p.wait("vector", "ld", c)
    p.op("vector", lambda E_: E_.tensor_copy(qn_c[64:128, :], qn_c[0:64, :]))
    p.op("vector", lambda E_: E_.tensor_copy(kn_c[64:128, :], kn_c[0:64, :]))
    p.op("vector", lambda E_: E_.tensor_copy(bq_c[64:128, 0:1], bq_c[0:64, 0:1]))
    p.op("vector", lambda E_: E_.tensor_scalar(qn_c[:], qn_c[:], 0.125, None,
                                               op0=ALU.mult))
    p.op("vector", lambda E_: E_.memset(epsr[:], 1e-8))
    p.op("vector", lambda E_: E_.memset(onesr64[:], 1.0), "vp")
    for e in range(2):
        c = p.dma("sync", bq_c[:, e:e + 1], AP(bq_d.tensor, e * 128, [[1, 128]]), "ld")
    p.wait("gpsimd", "ld", c)
    gc = p.op("gpsimd", lambda E_: E_.iota(iotaw[:], pattern=[[1, 2176]], base=-2048,
                                           channel_multiplier=0,
                                           allow_small_or_imprecise_dtypes=True), "gp")
    p.wait("vector", "gp", gc)

    def rmsnorm_emit(pb, dst, sl2, base, nwcol, hname):
        sc = p.op("scalar", lambda E_, b=base: E_.activation(
            qsq[b:b + 64, :], pb[b:b + 64, 0:512], AF.Square), "sp")
        p.wait("tensor", "sp", sc)
        pc = p.op("tensor", lambda E_, b=base: E_.matmul(
            pool[6][0:1, 0:512], consts["ones_col"][b:b + 64, :], qsq[b:b + 64, :],
            start=True, stop=True), "pp")
        p.wait("scalar", "pp", pc)
        sc = p.op("scalar", lambda E_: E_.activation(rstdr[:], pool[6][0:1, 0:512],
                                                     AF.Sqrt, bias=epsr[:],
                                                     scale=1.0 / 64.0), "sp")
        p.wait("vector", "sp", sc)
        p.opss("vector", lambda E_: E_.reciprocal(rstr2[:], rstdr[:]))
        vc = p.op("vector", lambda E_: E_.tensor_copy(rstdr[:], rstr2[:]), "vp")
        p.wait("tensor", "vp", vc)
        pc = p.op("tensor", lambda E_: E_.matmul(pool[7][0:64, 0:512], onesr64[:],
                                                 rstdr[:], start=True, stop=True), "pp")
        p.wait("vector", "pp", pc)
        p.opss("vector", lambda E_: E_.tensor_copy(rb64[0:64, :], pool[7][0:64, 0:512]))
        p.op("vector", lambda E_, b=base: E_.tensor_mul(
            qraw[b:b + 64, :], pb[b:b + 64, 0:512], rb64[0:64, :]))
        vc = p.op("vector", lambda E_, b=base, w=nwcol, d2=dst, s_=sl2: E_.tensor_scalar(
            d2[b:b + 64, s_] if d2 is not kfm else d2[0:64, s_],
            qraw[b:b + 64, :], w[b:b + 64, :], None, op0=ALU.mult), "vp")
        p.wait("tensor", "vp", vc)

    for xc in range(S // 512):
        sl = slice(xc * 512, xc * 512 + 512)
        if xc >= 1:
            p.wait("sync", "vp", p.cnt["vp"])
            p.wait("sync", "sp", p.cnt["sp"])
        for d in range(NDC):
            c = p.dma("sync", xin[:, d * 512:(d + 1) * 512],
                      x_in[d * 128:(d + 1) * 128, sl], "ld")
        p.wait("vector", "ld", c)
        p.wait("scalar", "ld", c)
        xaps = [xin[:, d * 512:(d + 1) * 512] for d in range(NDC)]
        naps = [xnb[:, d * 512:(d + 1) * 512] for d in range(NDC)]
        emit_ln_chunk(p, xaps, naps, consts)
        for oc in range(2):
            pb = pool[5]
            for d in range(NDC):
                pc = p.op("tensor", lambda E_, dd=d, o=oc, pb_=pb: E_.matmul(
                    pb_[:, 0:512], wq_sb[:, dd * 256 + o * 128: dd * 256 + o * 128 + 128],
                    xnb[:, dd * 512:(dd + 1) * 512], start=(dd == 0), stop=(dd == 7)),
                    "pp")
            p.wait("scalar", "pp", pc)
            sc = p.op("scalar", lambda E_, o=oc, pb_=pb: E_.activation(
                pb_[:, 0:512], pb_[:, 0:512], AF.Identity, bias=bq_c[:, o:o + 1]), "sp")
            p.wait("vector", "sp", sc)
            p.wait("tensor", "sp", sc)
            for hh in range(2):
                rmsnorm_emit(pb, qfm[oc], sl, hh * 64, qn_c, f"q{oc}{hh}")
        pb = pool[5]
        for d in range(NDC):
            pc = p.op("tensor", lambda E_, dd=d, pb_=pb: E_.matmul(
                pb_[0:64, 0:512], wk_sb[:, dd * 64:(dd + 1) * 64],
                xnb[:, dd * 512:(dd + 1) * 512], start=(dd == 0), stop=(dd == 7)), "pp")
        p.wait("scalar", "pp", pc)
        sc = p.op("scalar", lambda E_, pb_=pb: E_.activation(
            pb_[0:64, 0:512], pb_[0:64, 0:512], AF.Identity, bias=bk_c[0:64, :]), "sp")
        p.wait("vector", "sp", sc)
        p.wait("tensor", "sp", sc)
        rmsnorm_emit(pb, kfm, sl, 0, kn_c, "k")
        p.op("vector", lambda E_, s_=sl: E_.tensor_copy(kfm[64:128, s_], kfm[0:64, s_]),
             "vp")
        pb = pool[5]
        for d in range(NDC):
            pc = p.op("tensor", lambda E_, dd=d, pb_=pb: E_.matmul(
                pb_[0:64, 0:512], wv_sb[:, dd * 64:(dd + 1) * 64],
                xnb[:, dd * 512:(dd + 1) * 512], start=(dd == 0), stop=(dd == 7)), "pp")
        p.wait("scalar", "pp", pc)
        sc = p.op("scalar", lambda E_, pb_=pb: E_.activation(
            qraw[0:64, :], pb_[0:64, 0:512], AF.Identity, bias=bv_c[0:64, :]), "sp")
        p.wait("vector", "sp", sc)
        vc = p.op("vector", lambda E_: E_.tensor_copy(qsq[0:64, :], qraw[0:64, :]), "vp")
        p.wait("tensor", "vp", vc)
        for kt4 in range(4):
            kt = xc * 4 + kt4
            pc = p.op("tensor", lambda E_, a=kt4 * 128: E_.transpose(
                pool[6][:, 0:32].bitcast(BF16)[:, 0:64], qsq[0:64, a:a + 128],
                iden[0:64, 0:64]), "pp")
            p.wait("vector", "pp", pc)
            vc = p.op("vector", lambda E_, k=kt: E_.tensor_copy(
                vT[:, k * 64:(k + 1) * 64], pool[6][:, 0:32].bitcast(BF16)[:, 0:64]),
                "vp")
            p.wait("tensor", "vp", vc)

    p.wait("tensor", "vp", p.cnt["vp"])
    p.wait("tensor", "sp", p.cnt["sp"])
    p.wait("scalar", "vp", p.cnt["vp"])
    p.wait("gpsimd", "vp", p.cnt["vp"])
    for t in range(64):
        lo = max(0, t - 16)
        nsub = t - lo + 1
        L = nsub * 128
        u0 = 2176 - L
        for h in range(4):
            oc, hh = h // 2, h % 2
            if h == 0 or True:
                pass
            vc = p.op("vector", lambda E_, h_=h: E_.tensor_scalar(
                ali_row[:], iotaw[:], slop[:, h_:h_ + 1], None, op0=ALU.mult), "vp")
            p.wait("tensor", "vp", vc)
            nch = _ceil(L, 512)
            for cc in range(nch):
                c0 = cc * 512
                w_ = min(512, L - c0)
                pc = p.op("tensor", lambda E_, a=u0 + c0, w2=w_: E_.matmul(
                    pool[4][:, 0:w2], consts["ones_row1"][:], ali_row[:, a:a + w2],
                    start=True, stop=True), "pp")
                p.wait("vector", "pp", pc)
                vc = p.op("vector", lambda E_, a=c0, w2=w_: E_.tensor_copy(
                    alib[:, a:a + w2], pool[4][:, 0:w2]), "vp")
                p.wait("tensor", "vp", vc)
            for cc in range(nch):
                c0 = cc * 512
                w_ = min(512, L - c0)
                pb = pool[cc % 2]
                for j4 in range(_ceil(w_, 128)):
                    kt = lo + (c0 // 128) + j4
                    pc = p.op("tensor", lambda E_, o=oc, r0=hh * 64, k=kt, j=j4 * 128,
                              t_=t, pb_=pb: E_.matmul(
                        pb_[:, j:j + 128], qfm[o][r0:r0 + 64, t_ * 128:t_ * 128 + 128],
                        kfm[r0:r0 + 64, k * 128:(k + 1) * 128],
                        start=True, stop=True), "pp")
                p.wait("vector", "pp", pc)
                vc = p.op("vector", lambda E_, a=c0, w2=w_, pb_=pb: E_.tensor_add(
                    stl[:, a:a + w2], pb_[:, 0:w2], alib[:, a:a + w2]), "vp")
                p.wait("tensor", "vp", vc)
            vc = p.op("vector", lambda E_, a=L - 128: E_.tensor_add(
                stl[:, a:a + 128], stl[:, a:a + 128], cmask[:]), "vp")
            src_t = stl
            for r in range(8):
                p.opss("vector", lambda E_, rr=r, s_=src_t, w2=L: E_.max(
                    out=cand[:, rr * 8:(rr + 1) * 8], in_=s_[:, 0:w2]))
                dst_t = w1e if src_t is not w1e else w2e
                p.opss("vector", lambda E_, rr=r, s_=src_t, d2=dst_t, w2=L:
                       E_.match_replace(out=d2[:, 0:w2],
                                        in_to_replace=cand[:, rr * 8:(rr + 1) * 8],
                                        in_values=s_[:, 0:w2], imm_value=-1e30))
                src_t = dst_t
            if L0_DBG and t == 2 and h == 1:
                p.wait("sync", "_ss_vector", p.cnt["_ss_vector"])
                p.dma("sync", dbg_stl[:, 0:2176], stl[:], "st")
                p.dma("sync", dbg_cand, cand[:], "st")
            p.opss("vector", lambda E_: E_.tensor_scalar(negm[:], cand[:, 0:1], -1.0,
                                                         None, op0=ALU.mult))
            vc = p.op("vector", lambda E_: E_.tensor_copy(thr[:], cand[:, 63:64]), "vp")
            p.wait("scalar", "vp", vc)
            sc = p.op("scalar", lambda E_: E_.activation(qsq[:, 0:64], cand[:], AF.Exp,
                                                         bias=negm[:],
                                                         accum_out=zc[:]), "sp")
            p.wait("vector", "sp", sc)
            p.opss("vector", lambda E_: E_.reciprocal(rz[:], zc[:]))
            for cc in range(nch):
                c0 = cc * 512
                w_ = min(512, L - c0)
                sc = p.op("scalar", lambda E_, a=c0, w2=w_: E_.activation(
                    etile[:, a:a + w2], stl[:, a:a + w2], AF.Exp, bias=negm[:]), "sp")
                p.wait("vector", "sp", sc)
                vc = p.op("vector", lambda E_, a=c0, w2=w_: E_.scalar_tensor_tensor(
                    ptile[:, a:a + w2], stl[:, a:a + w2], thr[:], etile[:, a:a + w2],
                    op0=ALU.is_ge, op1=ALU.mult), "vp")
            p.wait("tensor", "vp", vc)
            if L0_DBG and t == 2 and h == 1:
                p.wait("sync", "vp", p.cnt["vp"])
                vcd = p.op("gpsimd", lambda E_: E_.dma_start(
                    out=dbg_p[:, 0:2176], in_=ptile[:]), "st", 16)
            for j in range(nsub):
                kt = lo + j
                pc = p.op("tensor", lambda E_, a=j * 128: E_.transpose(
                    pool[2][:, 0:64].bitcast(BF16)[:, 0:128], ptile[:, a:a + 128],
                    iden[:]), "pp")
                p.wait("vector", "pp", pc)
                vc = p.op("vector", lambda E_: E_.tensor_copy(
                    pTs[:], pool[2][:, 0:64].bitcast(BF16)[:, 0:128]), "vp")
                p.wait("tensor", "vp", vc)
                pc = p.op("tensor", lambda E_, k=kt, j_=j, n=nsub: E_.matmul(
                    pool[3][:, 0:64], pTs[:], vT[:, k * 64:(k + 1) * 64],
                    start=(j_ == 0), stop=(j_ == n - 1), skip_group_check=True), "pp")
                p.op("tensor", lambda E_: E_.nop(), "tn")
                p.wait("vector", "tn", p.cnt["tn"])
            p.wait("vector", "pp", pc)
            vc = p.op("vector", lambda E_: E_.tensor_scalar(
                ynorm[:], pool[3][:, 0:64], rz[:], None, op0=ALU.mult), "vp")
            p.wait("tensor", "vp", vc)
            pc = p.op("tensor", lambda E_: E_.transpose(
                pool[2][0:64, 0:64].bitcast(BF16)[:, 0:128], ynorm[:], iden[:]), "pp")
            p.wait("vector", "pp", pc)
            vc = p.op("vector", lambda E_, h_=h: E_.tensor_copy(
                yT[0:64, h_ * 128:(h_ + 1) * 128],
                pool[2][0:64, 0:64].bitcast(BF16)[:, 0:128]), "vp")
            p.wait("tensor", "vp", vc)
        for d in range(NDC):
            for h in range(4):
                pc = p.op("tensor", lambda E_, dd=d, h_=h: E_.matmul(
                    pool[6][:, 0:128],
                    woh[0:64, h_ * 1024 + dd * 128: h_ * 1024 + dd * 128 + 128],
                    yT[0:64, h_ * 128:(h_ + 1) * 128],
                    start=(h_ == 0), stop=(h_ == 3), skip_group_check=True), "pp")
            p.wait("vector", "pp", pc)
            vc = p.op("vector", lambda E_: E_.tensor_copy(pout[:], pool[6][:, 0:128]),
                      "vp")
            p.wait("gpsimd", "vp", vc)
            p.dma("gpsimd", part_d[d * 128:(d + 1) * 128, t * 128:(t + 1) * 128],
                  pout[:], "st")
            p.wait("vector", "st")
            p.wait("tensor", "vp", vc)
    p.wait("sync", "st")
    p.wait("gpsimd", "st")
    return p.build()


_NC_CACHE = {}

def _get_nc(name, builder):
    if name not in _NC_CACHE:
        _NC_CACHE[name] = builder()
    return _NC_CACHE[name]

def _ct(a):
    return np.ascontiguousarray(a)

def kernel(**z):
    ncores = list(range(NCORES))
    f32 = np.float32

    nc = _get_nc("prep", build_prep)
    im = dict(w_fc=_ct(z["L0_ffn_w_fc"]), w_proj=_ct(z["L0_ffn_w_proj"]),
              u=_ct(z["L0_ffn_u"]), v=_ct(z["L0_ffn_v"]), fast_b=_ct(z["L0_ffn_fast_b"]),
              gate=np.asarray(z["L0_ffn_gate"], f32).reshape(1, 1),
              W_KQV=_ct(z["L1_attn_W_KQV"]), W_O=_ct(z["L1_attn_W_O"]),
              rel=_ct(z["L1_ffn_rel_table"]))
    pr = {k: np.asarray(v) for k, v in
          run_bass_kernel_spmd(nc, [im] * NCORES, core_ids=ncores).results[0].items()}

    x = np.asarray(z["L0_attn_x"], f32)
    xf = [_ct(x[b].T) for b in range(2)]

    def slices(xfb):
        return [_ct(xfb[c // 4][:, (c % 4) * TOK:(c % 4 + 1) * TOK]) for c in range(NCORES)]

    def run(name, builder, ims):
        import sys, time as _t
        t0 = _t.time()
        print(f"[kernel] launch {name} ...", file=sys.stderr, flush=True)
        nc2 = _get_nc(name, builder)
        r = run_bass_kernel_spmd(nc2, ims, core_ids=ncores).results
        print(f"[kernel] launch {name} done {_t.time()-t0:.1f}s", file=sys.stderr,
              flush=True)
        return r

    def ffn0(xfb, parts=None, bias=None):
        ims = []
        xs = slices(xfb)
        for c in range(NCORES):
            im2 = dict(x=xs[c], tfcT=pr["tfcT"], scfc=pr["scfc"], tprT=pr["tprT"],
                       scpr=pr["scpr"], uT=pr["uT"], vT=pr["vT"], fbTg=pr["fbTg"])
            if parts is not None:
                b, g = c // 4, c % 4
                tsl = slice(g * TOK, (g + 1) * TOK)
                for j in range(4):
                    im2[f"part{j}"] = _ct(parts[b * 4 + j][:, tsl])
                im2["bias"] = _ct(bias)
            ims.append(im2)
        if parts is not None:
            res = run("ffn0c", lambda: build_ffn0(with_comb=True), ims)
        else:
            res = run("ffn0", build_ffn0, ims)
        return [np.concatenate([np.asarray(res[b * 4 + g2]["xout"]) for g2 in range(4)],
                               axis=1) for b in range(2)]

    def l0attn(xfb):
        slopes = (2.0 ** (-(np.arange(16, dtype=f32) / 15 * 8.0))).astype(f32)
        ims = []
        for c in range(NCORES):
            b, g = c // 4, c % 4
            ims.append(dict(
                x=xfb[b],
                wq=_ct(z["L0_attn_wq"][:, g * 256:(g + 1) * 256]),
                bq=_ct(z["L0_attn_bq"][g * 256:(g + 1) * 256]),
                wk=_ct(z["L0_attn_wkv"][:, g * 64:(g + 1) * 64]),
                bk=_ct(z["L0_attn_bkv"][g * 64:(g + 1) * 64]),
                wv=_ct(z["L0_attn_wkv"][:, 256 + g * 64:256 + (g + 1) * 64]),
                bv=_ct(z["L0_attn_bkv"][256 + g * 64:256 + (g + 1) * 64]),
                qnw=_ct(z["L0_attn_qn_w"]), knw=_ct(z["L0_attn_kn_w"]),
                slopes=_ct(slopes[4 * g:4 * g + 4]).reshape(1, 4),
                wo=_ct(z["L0_attn_wo"][g * 256:(g + 1) * 256, :])))
        res = run("l0attn", build_l0attn, ims)
        return [np.asarray(res[c]["part"]) for c in range(NCORES)]

    def comb(xfb, parts, bias):
        ims = []
        for c in range(NCORES):
            b, g = c // 4, c % 4
            tsl = slice(g * TOK, (g + 1) * TOK)
            ims.append(dict(
                x=_ct(xfb[b][:, tsl]),
                part0=_ct(parts[b * 4 + 0][:, tsl]), part1=_ct(parts[b * 4 + 1][:, tsl]),
                part2=_ct(parts[b * 4 + 2][:, tsl]), part3=_ct(parts[b * 4 + 3][:, tsl]),
                bias=_ct(bias)))
        res = run("comb", build_comb, ims)
        return [np.concatenate([np.asarray(res[b * 4 + g2]["xout"]) for g2 in range(4)],
                               axis=1) for b in range(2)]

    def l1attn(xfb):
        tc = trig_consts()
        wkqvT = pr["wkqvT"]
        ims = []
        for c in range(NCORES):
            b, g = c // 4, c % 4
            qcols = [1024 + i for i in l1_q_col_order(g)]
            kcols = [0 + i for i in l1_q_col_order(g)]
            vcols = [2048 + g * 256 + i for i in range(256)]
            ims.append(dict(
                x=xfb[b],
                wq=_ct(wkqvT[:, qcols]), wk=_ct(wkqvT[:, kcols]), wv=_ct(wkqvT[:, vcols]),
                wo=_ct(pr["woT"][g * 256:(g + 1) * 256, :]), tc=tc))
        res = run("l1attn", build_l1attn, ims)
        return [np.asarray(res[c]["part"]) for c in range(NCORES)]

    def moa(xfb):
        ims = []
        for c in range(NCORES):
            b, g = c // 4, c % 4
            perm = [2 * g, 2 * g + 1] + [e for e in range(8) if e not in (2 * g, 2 * g + 1)]
            ims.append(dict(
                x=xfb[b], ln1w=_ct(z["L1_ffn_ln1_w"]), ln1b=_ct(z["L1_ffn_ln1_b"]),
                gw=_ct(z["L1_ffn_attn_gate_w"][:, perm]),
                qw0=_ct(z["L1_ffn_q_w"][2 * g]), qw1=_ct(z["L1_ffn_q_w"][2 * g + 1]),
                qb=_ct(z["L1_ffn_q_b"][2 * g:2 * g + 2]),
                kw=_ct(z["L1_ffn_k_w"]), kb=_ct(z["L1_ffn_k_b"]),
                vw=_ct(z["L1_ffn_v_w"]), vb=_ct(z["L1_ffn_v_b"]),
                ow0=_ct(z["L1_ffn_o_w"][2 * g]), ow1=_ct(z["L1_ffn_o_w"][2 * g + 1]),
                relT=pr["relT"]))
        res = run("moa", build_moa, ims)
        return [np.asarray(res[c]["part"]) for c in range(NCORES)]

    def g3(xfb, mparts):
        ims = []
        for c in range(NCORES):
            b, g = c // 4, c % 4
            tsl = slice(g * TOK, (g + 1) * TOK)
            pslices = [_ct(mparts[b * 4 + j][tsl, :].T) for j in range(4)]
            ims.append(dict(
                x=_ct(xfb[b][:, tsl]),
                part0=pslices[0], part1=pslices[1], part2=pslices[2], part3=pslices[3],
                ln1w=_ct(z["L1_ffn_ln1_w"]), ln1b=_ct(z["L1_ffn_ln1_b"]),
                ln2w=_ct(z["L1_ffn_ln2_w"]), ln2b=_ct(z["L1_ffn_ln2_b"]),
                gw8=_ct(z["L1_ffn_attn_gate_w"]), ob=_ct(z["L1_ffn_o_b"]),
                gw16=_ct(z["L1_ffn_ff_gate_w"]),
                w1=_ct(z["L1_ffn_ff_w1"]), b1=_ct(z["L1_ffn_ff_b1"]),
                w2=_ct(z["L1_ffn_ff_w2"]), b2=_ct(z["L1_ffn_ff_b2"])))
        res = run("g3", build_g3, ims)
        return [np.concatenate([np.asarray(res[b * 4 + g2]["xout"]) for g2 in range(4)],
                               axis=1) for b in range(2)]

    x1 = ffn0(xf)
    parts = l0attn(x1)
    x3 = ffn0(x1, parts, np.asarray(z["L0_attn_bo"], f32))
    mparts = moa(x3)
    x4 = g3(x3, mparts)
    parts = l1attn(x4)
    x5 = comb(x4, parts, np.zeros(D, f32))
    mparts = moa(x5)
    x6 = g3(x5, mparts)

    out = np.stack([x6[0].T, x6[1].T], axis=0).astype(f32)
    return np.ascontiguousarray(out)
